# version 15
# baseline (speedup 1.0000x reference)
import numpy as np
import ml_dtypes
from contextlib import ExitStack

import concourse.bass as bass
import concourse.mybir as mybir
from concourse.bass_utils import run_bass_kernel_spmd
from concourse.bass import IndirectOffsetOnAxis

F32 = mybir.dt.float32
BF16 = mybir.dt.bfloat16
I32 = mybir.dt.int32
I16 = mybir.dt.int16
U8 = mybir.dt.uint8
AF = mybir.ActivationFunctionType
ALU = mybir.AluOpType
AXX = mybir.AxisListType.X

D = 1024
T = 2048
NSEQ = 4
NC = 8
HID = 3072
HC = HID // 128
R127 = float(np.float32(1.0 / 127.0))
MAGIC = 8388608.0
LNEPS = 1e-5

H1 = 8
POS = 16
NS = 33
NA = 17
WIN = 672
PAD = 272
TPAD = T + 2 * PAD
L1_LIMIT = 99
L1_NOMAC = False
L1_FIXIDX = False
L1_NGATHER = NS
L1_NSLOT = 8
L1_SPLIT = 4
L1_ILV = True


class Sem:
    def __init__(self, h):
        self.h = h
        self.v = 0

    def inc(self, ins, n=1):
        ins.then_inc(self.h, n)
        self.v += n
        return ins

    def wait(self, eng, v=None):
        eng.wait_ge(self.h, self.v if v is None else v)


class Rot:

    def __init__(self, n):
        self.n = n
        self.q = []

    def acquire(self, *engs):
        if len(self.q) >= self.n:
            sem, v = self.q.pop(0)
            for e in engs:
                e.wait_ge(sem.h, v)

    def release(self, sem):
        self.q.append((sem, sem.v))


class Ctx:
    def __init__(self, nc, stack):
        self.nc = nc
        self.stack = stack
        self._sems = {}

    def sem(self, name) -> Sem:
        if name not in self._sems:
            self._sems[name] = Sem(self.stack.enter_context(self.nc.semaphore(name)))
        return self._sems[name]

    def sb(self, name, shape, dtype):
        return self.stack.enter_context(self.nc.sbuf_tensor(name, shape, dtype))

    @property
    def vec(self):
        if not hasattr(self, "_vec"):
            self._vec = Guarded(self, self.nc.vector, "vgd")
        return self._vec

    @property
    def act(self):
        if not hasattr(self, "_act"):
            self._act = Guarded(self, self.nc.scalar, "agd")
        return self._act


_GUARD_OPS = {
    "tensor_copy", "tensor_tensor", "tensor_add", "tensor_sub", "tensor_mul",
    "tensor_max", "tensor_scalar", "tensor_single_scalar", "tensor_scalar_add",
    "tensor_scalar_sub", "tensor_scalar_mul", "tensor_scalar_max",
    "tensor_scalar_min", "tensor_relu", "tensor_reduce", "reduce_max",
    "reduce_sum", "scalar_tensor_tensor", "reciprocal", "select", "memset",
    "activation", "copy", "mul", "add", "square", "sqrt", "sign",
}


class Guarded:

    SHORT = 64

    def __init__(self, ctx, eng, name):
        self._ctx = ctx
        self._eng = eng
        self._name = name
        self._recent = []
        self._pad = ctx.sb("gpad_" + name, [128, 128], F32)
        if hasattr(eng, "memset"):
            eng.memset(self._pad[:, :], 0.0)
        else:
            eng.memzero(self._pad[:, :])

    def _gap(self):
        if self._name.startswith("a"):
            self._eng.activation(self._pad[:, 0:64], self._pad[:, 64:128], AF.Copy)
        else:
            self._eng.tensor_copy(self._pad[:, 0:64], self._pad[:, 64:128])

    def __getattr__(self, attr):
        real = getattr(self._eng, attr)
        if attr not in _GUARD_OPS:
            return real

        def wrap(*args, **kw):
            aps = [a for a in list(args) + list(kw.values()) if isinstance(a, bass.AP)]
            out = aps[0] if aps else None
            ins = aps[1:]
            innames = {a.tensor.name for a in ins}
            if out is not None and attr != "memset":
                innames.add(out.tensor.name)
            if any(n in self._recent for n in innames):
                self._gap()
                self._recent = []
            ins_res = real(*args, **kw)
            if out is not None and out.free_size() < self.SHORT:
                self._recent.append(out.tensor.name)
                self._recent = self._recent[-2:]
            else:
                self._recent = []
            return ins_res

        return wrap


def barrier(ctx):
    nc = ctx.nc
    with nc.Block() as blk:
        @blk.vector
        def _(v):
            v.engine_nop()


TRANSPOSED = {
    "wgT": ("L0_ffn_w_gate", HID, D, True),
    "wuT": ("L0_ffn_w_up", HID, D, True),
    "wdT": ("L0_ffn_w_down", D, HID, True),
    "woT": ("L0_attn_W_o", D, D, False),
    "wrT": ("L1_ffn_W_r", D, D, False),
    "wkT": ("L1_ffn_W_k", D, D, False),
    "wvT": ("L1_ffn_W_v", D, D, False),
}
NATURAL = {
    "wdq": ("L0_attn_W_dq", D, 256),
    "wuq": ("L0_attn_W_uq", 256, D),
    "wdkv": ("L0_attn_W_dkv", D, 256),
    "wukv": ("L0_attn_W_ukv", 256, 2 * D),
    "wwave": ("L1_attn_wave_w", D, 24),
    "wquery": ("L1_attn_query_w", D, 128),
    "wse1": ("L1_attn_se1_w", D, 256),
    "wse2": ("L1_attn_se2_w", 256, D),
    "woutw": ("L1_attn_out_w", D, D),
}
SHAPES = {
    "L0_attn_W_dq": ([D, 256], F32), "L0_attn_W_uq": ([256, D], F32),
    "L0_attn_q_ln_g": ([256], F32), "L0_attn_q_ln_b": ([256], F32),
    "L0_attn_W_dkv": ([D, 256], F32), "L0_attn_W_ukv": ([256, 2 * D], F32),
    "L0_attn_kv_ln_g": ([256], F32), "L0_attn_kv_ln_b": ([256], F32),
    "L0_attn_W_o": ([D, D], F32),
    "L0_ffn_w_gate": ([HID, D], I32), "L0_ffn_w_up": ([HID, D], I32),
    "L0_ffn_w_down": ([D, HID], I32),
    "L1_attn_wave_w": ([D, 24], F32), "L1_attn_wave_b": ([24], F32),
    "L1_attn_query_w": ([D, 128], F32), "L1_attn_query_b": ([128], F32),
    "L1_attn_key_w": ([POS], F32), "L1_attn_out_w": ([D, D], F32),
    "L1_attn_se1_w": ([D, 256], F32), "L1_attn_se1_b": ([256], F32),
    "L1_attn_se2_w": ([256, D], F32), "L1_attn_se2_b": ([D], F32),
    "L1_ffn_W_r": ([D, D], F32), "L1_ffn_W_k": ([D, D], F32),
    "L1_ffn_W_v": ([D, D], F32),
    "L1_ffn_mu_r": ([D], F32), "L1_ffn_mu_k": ([D], F32),
}


def _needed_inputs(phases):
    need = set()
    if "l0ffn" in phases:
        need |= {"L0_ffn_w_gate", "L0_ffn_w_up", "L0_ffn_w_down"}
    if "l0attn" in phases:
        need |= {"L0_attn_W_dq", "L0_attn_W_uq", "L0_attn_q_ln_g", "L0_attn_q_ln_b",
                 "L0_attn_W_dkv", "L0_attn_W_ukv", "L0_attn_kv_ln_g", "L0_attn_kv_ln_b",
                 "L0_attn_W_o"}
    if "l1ffn" in phases:
        need |= {"L1_ffn_W_r", "L1_ffn_W_k", "L1_ffn_W_v", "L1_ffn_mu_r", "L1_ffn_mu_k"}
    if "l1attn" in phases:
        need |= {"L1_attn_wave_w", "L1_attn_wave_b", "L1_attn_query_w", "L1_attn_query_b",
                 "L1_attn_key_w", "L1_attn_out_w", "L1_attn_se1_w", "L1_attn_se1_b",
                 "L1_attn_se2_w", "L1_attn_se2_b"}
    return sorted(need)


def declare_io(nc, phases):
    io = {}
    io["x"] = nc.dram_tensor("x", [NSEQ, D, T], F32, kind="ExternalInput").ap()
    io["y"] = nc.dram_tensor("y", [NSEQ, D, T], F32, kind="ExternalOutput").ap()
    for n in _needed_inputs(phases):
        sh, dt = SHAPES[n]
        io[n] = nc.dram_tensor(n, sh, dt, kind="ExternalInput").ap()
    return io


def declare_scratch(nc, phases):
    sc = {}
    tneed = []
    nneed = []
    if "l0ffn" in phases:
        tneed += ["wgT", "wuT", "wdT"]
    if "l0attn" in phases:
        tneed += ["woT"]
        nneed += ["wdq", "wuq", "wdkv", "wukv"]
    if "l1ffn" in phases:
        tneed += ["wrT", "wkT", "wvT"]
    if "l1attn" in phases:
        nneed += ["wwave", "wquery", "wse1", "wse2"]
        sc["woutw"] = nc.dram_tensor("sc_woutw", [D // 128, 128, D], BF16, kind="Internal").ap()
        sc["xtok"] = nc.dram_tensor("sc_xtok", [TPAD, D], BF16, kind="Internal").ap()
    for n in tneed:
        R, C = TRANSPOSED[n][1], TRANSPOSED[n][2]
        sc[n] = nc.dram_tensor("sc_" + n, [R // 128, 128, C], BF16, kind="Internal").ap()
    for n in nneed:
        K, N = NATURAL[n][1], NATURAL[n][2]
        sc[n] = nc.dram_tensor("sc_" + n, [K // 128, 128, N], BF16, kind="Internal").ap()
    return sc


def emit_consts(ctx, io, phases):
    nc = ctx.nc
    consts = {}
    ones_bf = ctx.sb("ones_bf", [128, 128], BF16)
    ident_bf = ctx.sb("ident_bf", [128, 128], BF16)
    epscol = ctx.sb("epscol", [128, 1], F32)
    consts["ones_bf"] = ones_bf
    consts["ident_bf"] = ident_bf
    consts["epscol"] = epscol
    scw = ctx.sem("c_cw")
    ctx.vec.memset(epscol[:, :], LNEPS)
    ctx.vec.memset(ident_bf[:, :], 0.0)
    scw.inc(ctx.vec.memset(ones_bf[:, :], 1.0))
    scw.wait(nc.gpsimd)
    scw.inc(nc.gpsimd.affine_select(ident_bf[:, :], ones_bf[:, :], [[-1, 128]],
                                    ALU.is_equal, 0.0, base=0, channel_multiplier=1))
    if "l0attn" in phases:
        tril = ctx.sb("tril_bf", [128, 128], BF16)
        consts["tril"] = tril
        scw.inc(nc.gpsimd.affine_select(tril[:, :], ones_bf[:, :], [[-1, 128]],
                                        ALU.is_ge, 0.0, base=0, channel_multiplier=1))
    if "l1attn" in phases:
        obc = ctx.sb("obc", [128, NS], F32)
        abc = ctx.sb("abc", [128, NA], F32)
        center = ctx.sb("center", [128, 1], F32)
        iota16 = ctx.sb("iota16", [128, 128], mybir.dt.float16)
        consts["obc"], consts["abc"], consts["center"] = obc, abc, center
        consts["iota16"] = iota16
        scw.inc(nc.gpsimd.iota(iota16[:, :], [[1, 128]], base=0, channel_multiplier=0,
                               allow_small_or_imprecise_dtypes=True))
        scw.inc(nc.gpsimd.iota(obc[:, :], [[1, NS]], base=-16, channel_multiplier=0,
                               allow_small_or_imprecise_dtypes=True))
        scw.inc(nc.gpsimd.iota(abc[:, :], [[1, NA]], base=0, channel_multiplier=0,
                               allow_small_or_imprecise_dtypes=True))
        scw.inc(nc.gpsimd.iota(center[:, :], [[0, 1]], base=0, channel_multiplier=1,
                               allow_small_or_imprecise_dtypes=True))
    scw.wait(nc.tensor)
    scw.wait(nc.vector)
    scw.wait(nc.scalar)
    return consts


def emit_params(ctx, io, phases):
    nc = ctx.nc
    prm = {}
    sdl = ctx.sem("prm_dl")

    def col(name, src_ap, n):
        t = ctx.sb("p_" + name, [128, max(1, n // 128)], F32)
        sdl.inc(nc.sync.dma_start(t[:, :], src_ap.rearrange("(c p) -> p c", p=128),
                                  allow_slow_non_contiguous=True), 16)
        prm[name] = t

    def row(name, src_ap, n):
        t = ctx.sb("r_" + name, [1, n], F32)
        sdl.inc(nc.sync.dma_start(t[0:1, :], src_ap.rearrange("(o n) -> o n", o=1)), 16)
        prm[name] = t

    if "l0attn" in phases:
        col("q_ln_g", io["L0_attn_q_ln_g"], 256)
        col("q_ln_b", io["L0_attn_q_ln_b"], 256)
        col("kv_ln_g", io["L0_attn_kv_ln_g"], 256)
        col("kv_ln_b", io["L0_attn_kv_ln_b"], 256)
    if "l1ffn" in phases:
        col("mu_r", io["L1_ffn_mu_r"], D)
        col("mu_k", io["L1_ffn_mu_k"], D)
    if "l1attn" in phases:
        row("wave_b", io["L1_attn_wave_b"], 24)
        row("query_b", io["L1_attn_query_b"], 128)
        row("key_w", io["L1_attn_key_w"], POS)
        col("se1_b", io["L1_attn_se1_b"], 256)
        col("se2_b", io["L1_attn_se2_b"], D)
    for eng in (nc.vector, nc.scalar, nc.tensor, nc.gpsimd):
        sdl.wait(eng)
    return prm


def emit_preprocess(ctx, io, sc, phases, consts):
    nc = ctx.nc
    s_ld = [ctx.sem("pp_ld0"), ctx.sem("pp_ld1")]
    s_cv = ctx.sem("pp_cv")
    s_tp = ctx.sem("pp_tp")
    s_cp = ctx.sem("pp_cp")
    s_st = [ctx.sem("pp_st0"), ctx.sem("pp_st1")]
    stack = ExitStack()
    sload = stack.enter_context(nc.sbuf_tensor("pp_load", [128, 2 * HID], F32))
    sloadi = stack.enter_context(nc.sbuf_tensor("pp_loadi", [128, 2 * HID], I32))
    sbf = stack.enter_context(nc.sbuf_tensor("pp_bf", [128, 2 * HID], BF16))
    sstage = stack.enter_context(nc.sbuf_tensor("pp_stage", [128, 2 * HID], BF16))
    ppt = [stack.enter_context(nc.psum_tensor(f"pp_pt{i}", [128, 128], BF16)) for i in range(4)]
    ident = consts["ident_bf"]

    load_rot = Rot(2)
    pt_rot = Rot(2)

    idx = 0
    for n in [t for t in TRANSPOSED if t in sc]:
        src_name, R, C, is_int = TRANSPOSED[n]
        src, dst = io[src_name], sc[n]
        for rc in range(R // 128):
            sl_i = idx % 2
            b = sl_i * HID
            load_rot.acquire(nc.sync)
            if is_int:
                s_ld[sl_i].inc(nc.sync.dma_start(sloadi[:, b:b + C], src[rc * 128:(rc + 1) * 128, :]), 16)
            else:
                s_ld[sl_i].inc(nc.sync.dma_start(sload[:, b:b + C], src[rc * 128:(rc + 1) * 128, :]), 16)
            s_ld[sl_i].wait(nc.vector)
            s_st[sl_i].wait(nc.vector)
            if is_int:
                s_cv.inc(ctx.vec.tensor_scalar_mul(sbf[:, b:b + C], sloadi[:, b:b + C], R127))
            else:
                s_cv.inc(ctx.vec.tensor_copy(sbf[:, b:b + C], sload[:, b:b + C]))
            load_rot.release(s_cv)
            s_cv.wait(nc.tensor)
            ncc = C // 128
            tmarks = {}
            for cc in range(ncc + 2):
                if cc < ncc:
                    pt = ppt[cc % 4]
                    pt_rot.acquire(nc.tensor)
                    s_tp.inc(nc.tensor.transpose(pt[:, :], sbf[:, b + cc * 128:b + (cc + 1) * 128], ident[:, :]))
                    tmarks[cc] = s_tp.v
                if cc >= 2:
                    cp = cc - 2
                    s_tp.wait(nc.vector, tmarks[cp])
                    s_cp.inc(ctx.vec.tensor_copy(sstage[:, b + cp * 128:b + (cp + 1) * 128], ppt[cp % 4][:, :]))
                    pt_rot.release(s_cp)
            s_cp.wait(nc.sync)
            s_st[sl_i].inc(nc.sync.dma_start(dst[rc, :, :], sstage[:, b:b + C]), 16)
            idx += 1

    for n in [t for t in NATURAL if t in sc]:
        src_name, K, N = NATURAL[n]
        src, dst = io[src_name], sc[n]
        grouped = (n == "woutw")
        for kc in range(K // 128):
            sl_i = idx % 2
            b = sl_i * HID
            load_rot.acquire(nc.sync)
            s_ld[sl_i].inc(nc.sync.dma_start(sload[:, b:b + N], src[kc * 128:(kc + 1) * 128, :]), 16)
            s_ld[sl_i].wait(nc.vector)
            s_st[sl_i].wait(nc.vector)
            s_cv.inc(ctx.vec.tensor_copy(sbf[:, b:b + N], sload[:, b:b + N]))
            load_rot.release(s_cv)
            s_cv.wait(nc.sync)
            if grouped:
                s_st[sl_i].inc(nc.sync.dma_start(
                    dst[:, :, kc * 128:(kc + 1) * 128].rearrange("o p j -> p o j"),
                    sbf[:, b:b + N].rearrange("p (o j) -> p o j", j=128)), 16)
            else:
                s_st[sl_i].inc(nc.sync.dma_start(dst[kc, :, :], sbf[:, b:b + N]), 16)
            idx += 1
    if "xtok" in sc:
        zt = stack.enter_context(nc.sbuf_tensor("pp_zt", [128, D], BF16))
        s_cv.inc(ctx.vec.memset(zt[:, :], 0.0))
        s_cv.wait(nc.sync)
        xt = sc["xtok"]
        s_st[0].inc(nc.sync.dma_start(xt[0:128, :], zt[:, :]), 16)
        s_st[0].inc(nc.sync.dma_start(xt[128:256, :], zt[:, :]), 16)
        s_st[0].inc(nc.sync.dma_start(xt[256:PAD, :], zt[0:16, :]), 16)
        s_st[0].inc(nc.sync.dma_start(xt[T + PAD:T + PAD + 128, :], zt[:, :]), 16)
        s_st[0].inc(nc.sync.dma_start(xt[T + PAD + 128:T + PAD + 256, :], zt[:, :]), 16)
        s_st[0].inc(nc.sync.dma_start(xt[T + PAD + 256:TPAD, :], zt[0:16, :]), 16)
    s_st[0].wait(nc.sync)
    s_st[1].wait(nc.sync)
    stack.close()


def emit_ln(ctx, bufs, tag):
    nc = ctx.nc
    xres, xln = bufs["xres"], bufs["xln"]
    ones = bufs["consts"]["ones_bf"]
    eps = bufs["consts"]["epscol"]
    s_pp = ctx.sem("ln_pp")
    s_mm = ctx.sem("ln_mm")
    s_ev = ctx.sem("ln_ev")
    s_vv = ctx.sem("ln_vv")
    stack = ExitStack()
    xbf = stack.enter_context(nc.sbuf_tensor("ln_xbf" + tag, [128, NC * T], BF16))
    xsq = stack.enter_context(nc.sbuf_tensor("ln_xsq" + tag, [128, NC * T], BF16))
    mub = stack.enter_context(nc.sbuf_tensor("ln_mu" + tag, [128, T], BF16))
    rstdb = stack.enter_context(nc.sbuf_tensor("ln_rstd" + tag, [128, T], BF16))
    muf = stack.enter_context(nc.sbuf_tensor("ln_muf" + tag, [128, 512], F32))
    tmp = stack.enter_context(nc.sbuf_tensor("ln_tmp" + tag, [128, 512], F32))
    tmpb = stack.enter_context(nc.sbuf_tensor("ln_tb" + tag, [128, T], BF16))
    ps = stack.enter_context(nc.psum_tensor("ln_ps" + tag, [128, 1024], F32))
    pq = stack.enter_context(nc.psum_tensor("ln_pq" + tag, [128, 1024], F32))
    st_rot = Rot(2)
    for st in range(4):
        n0 = st * 512
        c0 = (st % 2) * 512
        for c in range(NC):
            i1 = ctx.act.activation(xbf[:, c * T + n0:c * T + n0 + 512], xres[:, c * T + n0:c * T + n0 + 512], AF.Copy)
            i2 = ctx.act.activation(xsq[:, c * T + n0:c * T + n0 + 512], xres[:, c * T + n0:c * T + n0 + 512], AF.Square)
            if c == NC - 1:
                s_pp.inc(i2)
        s_pp.wait(nc.tensor)
        st_rot.acquire(nc.tensor)
        for c in range(NC):
            nc.tensor.matmul(ps[:, c0:c0 + 512], ones[:, :], xbf[:, c * T + n0:c * T + n0 + 512],
                             start=(c == 0), stop=(c == NC - 1))
        for c in range(NC):
            mm = nc.tensor.matmul(pq[:, c0:c0 + 512], ones[:, :], xsq[:, c * T + n0:c * T + n0 + 512],
                                  start=(c == 0), stop=(c == NC - 1))
            if c == NC - 1:
                s_mm.inc(mm)
        s_mm.wait(nc.vector)
        if st >= 1:
            s_ev.wait(nc.vector)
        ctx.vec.tensor_scalar_mul(muf[:, :], ps[:, c0:c0 + 512], 1.0 / D)
        ctx.vec.tensor_scalar_mul(tmp[:, :], pq[:, c0:c0 + 512], 1.0 / D)
        ctx.vec.tensor_mul(xln[:, 0:512].bitcast(F32) if False else tmpb[:, 0:1024].bitcast(F32), muf[:, :], muf[:, :])
        i1 = ctx.vec.tensor_sub(tmp[:, :], tmp[:, :], tmpb[:, 0:1024].bitcast(F32))
        s_vv.inc(i1)
        st_rot.release(s_vv)
        s_vv.wait(nc.scalar)
        ctx.act.activation(mub[:, n0:n0 + 512], muf[:, :], AF.Copy)
        ctx.act.activation(tmp[:, :], tmp[:, :], AF.Ln, bias=eps[:, 0:1])
        s_ev.inc(ctx.act.activation(rstdb[:, n0:n0 + 512], tmp[:, :], AF.Exp, scale=-0.5))
        s_ev.wait(nc.vector)
        for c in range(NC):
            sl = slice(c * T + n0, c * T + n0 + 512)
            ctx.vec.tensor_sub(tmpb[:, n0:n0 + 512], xbf[:, sl], mub[:, n0:n0 + 512])
            i1 = ctx.vec.tensor_mul(xln[:, sl], tmpb[:, n0:n0 + 512], rstdb[:, n0:n0 + 512])
            if c == NC - 1:
                s_vv.inc(i1)
    s_vv.wait(nc.tensor)
    s_vv.wait(nc.scalar)
    s_vv.wait(nc.sync)
    s_vv.wait(nc.gpsimd)
    stack.close()


def emit_l0ffn(ctx, bufs, sc, tag):
    nc = ctx.nc
    xres, xln = bufs["xres"], bufs["xln"]
    s_dl = [ctx.sem("f0_dl0"), ctx.sem("f0_dl1")]
    s_dd = [ctx.sem("f0_dd0"), ctx.sem("f0_dd1")]
    s_mm = ctx.sem("f0_mm")
    s_ev = ctx.sem("f0_ev")
    s_ab = ctx.sem("f0_ab")
    stack = ExitStack()
    wbuf = stack.enter_context(nc.sbuf_tensor("f0_w" + tag, [128, 2 * 2048], BF16))
    wdbuf = stack.enter_context(nc.sbuf_tensor("f0_wd" + tag, [128, 2 * HID], BF16))
    hbuf = stack.enter_context(nc.sbuf_tensor("f0_h" + tag, [128, HC * 1024], BF16))
    sg = stack.enter_context(nc.sbuf_tensor("f0_sg" + tag, [128, 2 * 1024], BF16))
    pg = [stack.enter_context(nc.psum_tensor(f"f0_pg{i}" + tag, [128, 1024], F32)) for i in range(2)]
    pu = [stack.enter_context(nc.psum_tensor(f"f0_pu{i}" + tag, [128, 1024], F32)) for i in range(2)]

    wgT, wuT, wdT = sc["wgT"], sc["wuT"], sc["wdT"]
    w_rot = Rot(2)
    p_rot = Rot(2)
    sg_rot = Rot(2)
    wd_rot = Rot(2)

    for strip in range(2):
        t0 = strip * 1024
        for hc in range(HC):
            slot = hc % 2
            wofs = slot * 2048
            w_rot.acquire(nc.sync)
            s_dl[slot].inc(nc.sync.dma_start(wbuf[:, wofs:wofs + 1024], wgT[hc, :, :]), 16)
            s_dl[slot].inc(nc.sync.dma_start(wbuf[:, wofs + 1024:wofs + 2048], wuT[hc, :, :]), 16)
            s_dl[slot].wait(nc.tensor)
            p_rot.acquire(nc.tensor)
            for n0 in (0, 512):
                for kc in range(NC):
                    nc.tensor.matmul(pg[slot][:, n0:n0 + 512],
                                     wbuf[:, wofs + kc * 128:wofs + (kc + 1) * 128],
                                     xln[:, kc * T + t0 + n0:kc * T + t0 + n0 + 512],
                                     start=(kc == 0), stop=(kc == NC - 1))
            for n0 in (0, 512):
                for kc in range(NC):
                    mm = nc.tensor.matmul(pu[slot][:, n0:n0 + 512],
                                          wbuf[:, wofs + 1024 + kc * 128:wofs + 1024 + (kc + 1) * 128],
                                          xln[:, kc * T + t0 + n0:kc * T + t0 + n0 + 512],
                                          start=(kc == 0), stop=(kc == NC - 1))
                    if n0 == 512 and kc == NC - 1:
                        s_mm.inc(mm)
            w_rot.release(s_mm)
            s_mm.wait(nc.scalar)
            sg_rot.acquire(nc.scalar)
            s_ab.inc(ctx.act.activation(sg[:, slot * 1024:(slot + 1) * 1024], pg[slot][:, :], AF.Silu))
            s_ab.wait(nc.vector)
            s_ev.inc(ctx.vec.tensor_mul(hbuf[:, hc * 1024:(hc + 1) * 1024],
                                          sg[:, slot * 1024:(slot + 1) * 1024], pu[slot][:, :]))
            sg_rot.release(s_ev)
            p_rot.release(s_ev)
        for oc in range(NC):
            slot = oc % 2
            wofs = slot * HID
            wd_rot.acquire(nc.sync)
            s_dd[slot].inc(nc.sync.dma_start(wdbuf[:, wofs:wofs + HID], wdT[oc, :, :]), 16)
            s_dd[slot].wait(nc.tensor)
            s_ev.wait(nc.tensor)
            p_rot.acquire(nc.tensor)
            for n0 in (0, 512):
                for kc in range(HC):
                    mm = nc.tensor.matmul(pg[slot][:, n0:n0 + 512],
                                          wdbuf[:, wofs + kc * 128:wofs + (kc + 1) * 128],
                                          hbuf[:, kc * 1024 + n0:kc * 1024 + n0 + 512],
                                          start=(kc == 0), stop=(kc == HC - 1))
                    if n0 == 512 and kc == HC - 1:
                        s_mm.inc(mm)
            wd_rot.release(s_mm)
            s_mm.wait(nc.vector)
            sl = slice(oc * T + t0, oc * T + t0 + 1024)
            s_ev.inc(ctx.vec.scalar_tensor_tensor(xres[:, sl], pg[slot][:, :], 0.5, xres[:, sl],
                                                    ALU.mult, ALU.add))
            p_rot.release(s_ev)
    s_ev.wait(nc.tensor)
    s_ev.wait(nc.scalar)
    s_ev.wait(nc.sync)
    s_ev.wait(nc.gpsimd)
    stack.close()


def emit_l0attn(ctx, bufs, sc, tag):
    nc = ctx.nc
    xres, xln = bufs["xres"], bufs["xln"]
    prm, consts = bufs["prm"], bufs["consts"]
    ones, tril = consts["ones_bf"], consts["tril"]
    s_dl = [ctx.sem("a0_dl0"), ctx.sem("a0_dl1")]
    s_mm = ctx.sem("a0_mm")
    s_ev = ctx.sem("a0_ev")
    s_ab = ctx.sem("a0_ab")
    stack = ExitStack()
    pA = stack.enter_context(nc.psum_tensor("a0_pA" + tag, [128, 1024], F32))
    pB = stack.enter_context(nc.psum_tensor("a0_pB" + tag, [128, 1024], F32))
    pS = stack.enter_context(nc.psum_tensor("a0_pS" + tag, [128, 1024], F32))
    pO = stack.enter_context(nc.psum_tensor("a0_pO" + tag, [128, 1024], F32))
    cqln = stack.enter_context(nc.sbuf_tensor("a0_cq" + tag, [128, 2 * T], BF16))
    mcol = stack.enter_context(nc.sbuf_tensor("a0_mc" + tag, [128, 8], F32))
    sqt = stack.enter_context(nc.sbuf_tensor("a0_sq" + tag, [128, 4 * 512], BF16))
    wuq = stack.enter_context(nc.sbuf_tensor("a0_wuq" + tag, [128, 2048], BF16))
    s_dl[1].inc(nc.sync.dma_start(wuq[:, :].rearrange("p (k n) -> p k n", k=NATURAL["wuq"][1] // 128), sc["wuq"].rearrange("k p n -> p k n")), 16)

    def comp_proj(wdown, g_col, b_col, dst_ln, sub, wofs):
        cbf = sub.enter_context(nc.sbuf_tensor(f"a0_cbf{wofs}" + tag, [128, 2 * T], BF16))
        csq = sub.enter_context(nc.sbuf_tensor(f"a0_csq{wofs}" + tag, [128, 2 * T], BF16))
        stat = sub.enter_context(nc.sbuf_tensor(f"a0_stt{wofs}" + tag, [128, 2 * T], F32))
        tmpf = sub.enter_context(nc.sbuf_tensor(f"a0_tf{wofs}" + tag, [128, T], F32))
        mu_t, m2_t = stat[:, 0:T], stat[:, T:2 * T]
        prj_rot = Rot(2)
        for j, (oc, st4) in enumerate([(o, st) for o in range(2) for st in range(4)]):
            n0 = st4 * 512
            ppp = (pA, pB)[j % 2]
            prj_rot.acquire(nc.tensor)
            for kc in range(NC):
                mm = nc.tensor.matmul(ppp[:, 0:512],
                                      wdown[:, kc * 256 + oc * 128:kc * 256 + oc * 128 + 128],
                                      xln[:, kc * T + n0:kc * T + n0 + 512],
                                      start=(kc == 0), stop=(kc == NC - 1))
                if kc == NC - 1:
                    s_mm.inc(mm)
            s_mm.wait(nc.scalar)
            ctx.act.activation(cbf[:, oc * T + n0:oc * T + n0 + 512], ppp[:, 0:512], AF.Copy)
            s_ab.inc(ctx.act.activation(csq[:, oc * T + n0:oc * T + n0 + 512], ppp[:, 0:512], AF.Square))
            prj_rot.release(s_ab)
        s_ab.wait(nc.tensor)
        st_rot = Rot(1)
        for st4 in range(4):
            n0 = st4 * 512
            st_rot.acquire(nc.tensor)
            for oc in range(2):
                nc.tensor.matmul(pS[:, 0:512], ones[:, :], cbf[:, oc * T + n0:oc * T + n0 + 512],
                                 start=(oc == 0), stop=(oc == 1))
            for oc in range(2):
                mm = nc.tensor.matmul(pS[:, 512:1024], ones[:, :], csq[:, oc * T + n0:oc * T + n0 + 512],
                                      start=(oc == 0), stop=(oc == 1))
                if oc == 1:
                    s_mm.inc(mm)
            s_mm.wait(nc.scalar)
            ctx.act.activation(mu_t[:, n0:n0 + 512], pS[:, 0:512], AF.Copy, scale=1.0 / 256)
            s_ab.inc(ctx.act.activation(m2_t[:, n0:n0 + 512], pS[:, 512:1024], AF.Copy, scale=1.0 / 256))
            st_rot.release(s_ab)
        s_ab.wait(nc.vector)
        ctx.vec.tensor_mul(tmpf[:, :], mu_t, mu_t)
        s_ev.inc(ctx.vec.tensor_sub(tmpf[:, :], m2_t, tmpf[:, :]))
        s_ev.wait(nc.scalar)
        ctx.act.activation(tmpf[:, :], tmpf[:, :], AF.Ln, bias=consts["epscol"][:, 0:1])
        s_ab.inc(ctx.act.activation(tmpf[:, :], tmpf[:, :], AF.Exp, scale=-0.5))
        s_ab.wait(nc.vector)
        for oc in range(2):
            ctx.vec.tensor_sub(m2_t, cbf[:, oc * T:(oc + 1) * T], mu_t)
            ctx.vec.tensor_mul(m2_t, m2_t, tmpf[:, :])
            i1 = ctx.vec.tensor_scalar(dst_ln[:, oc * T:(oc + 1) * T], m2_t,
                                         g_col[:, oc:oc + 1], b_col[:, oc:oc + 1], ALU.mult, ALU.add)
            if oc == 1:
                s_ev.inc(i1)
        s_ev.wait(nc.tensor)
        s_ev.wait(nc.scalar)
        s_ev.wait(nc.sync)

    kv_stack = ExitStack()
    ckvln = kv_stack.enter_context(nc.sbuf_tensor("a0_ckv" + tag, [128, 2 * T], BF16, side="right"))
    with ExitStack() as sub:
        wdq = sub.enter_context(nc.sbuf_tensor("a0_wdq" + tag, [128, 2048], BF16))
        s_dl[0].inc(nc.sync.dma_start(wdq[:, :].rearrange("p (k n) -> p k n", k=NATURAL["wdq"][1] // 128), sc["wdq"].rearrange("k p n -> p k n")), 16)
        s_dl[0].wait(nc.tensor)
        comp_proj(wdq, prm["q_ln_g"], prm["q_ln_b"], cqln, sub, 0)
    with ExitStack() as sub:
        wdkv = sub.enter_context(nc.sbuf_tensor("a0_wdkv" + tag, [128, 2048], BF16))
        s_dl[0].inc(nc.sync.dma_start(wdkv[:, :].rearrange("p (k n) -> p k n", k=NATURAL["wdkv"][1] // 128), sc["wdkv"].rearrange("k p n -> p k n")), 16)
        s_dl[0].wait(nc.tensor)
        comp_proj(wdkv, prm["kv_ln_g"], prm["kv_ln_b"], ckvln, sub, 1)

    kbf = stack.enter_context(nc.sbuf_tensor("a0_k" + tag, [128, NC * T], BF16))
    vaug = stack.enter_context(nc.sbuf_tensor("a0_v" + tag, [128, 16 * 1040], BF16))
    wukv = kv_stack.enter_context(nc.sbuf_tensor("a0_wukv" + tag, [128, 4096], BF16, side="right"))
    s_dl[0].inc(nc.sync.dma_start(wukv[:, :].rearrange("p (k n) -> p k n", k=NATURAL["wukv"][1] // 128), sc["wukv"].rearrange("k p n -> p k n")), 16)
    s_dl[0].wait(nc.tensor)

    k_rot = Rot(2)
    for j, (oc, st4) in enumerate([(o, st) for o in range(NC) for st in range(4)]):
        n0 = st4 * 512
        ppp = (pA, pB)[j % 2]
        k_rot.acquire(nc.tensor)
        for kc in range(2):
            mm = nc.tensor.matmul(ppp[:, 0:512],
                                  wukv[:, kc * 2048 + oc * 128:kc * 2048 + oc * 128 + 128],
                                  ckvln[:, kc * T + n0:kc * T + n0 + 512],
                                  start=(kc == 0), stop=(kc == 1))
            if kc == 1:
                s_mm.inc(mm)
        s_mm.wait(nc.scalar)
        s_ab.inc(ctx.act.activation(kbf[:, oc * T + n0:oc * T + n0 + 512], ppp[:, 0:512], AF.Copy))
        k_rot.release(s_ab)
    s_ab.wait(nc.tensor)

    def sumsq_max(src, dstcol, tmpcol):
        sq_rot = Rot(4)
        for st4 in range(4):
            n0 = st4 * 512
            for oc in range(NC):
                sq_rot.acquire(nc.scalar)
                s_ab.inc(ctx.act.activation(sqt[:, (oc % 4) * 512:(oc % 4) * 512 + 512],
                                              src[:, oc * T + n0:oc * T + n0 + 512], AF.Square))
                s_ab.wait(nc.tensor)
                mm = nc.tensor.matmul(pS[:, 0:512], ones[:, :], sqt[:, (oc % 4) * 512:(oc % 4) * 512 + 512],
                                      start=(oc == 0), stop=(oc == NC - 1))
                if oc == NC - 1:
                    s_mm.inc(mm)
                sq_rot.release(s_mm)
            s_mm.wait(nc.vector)
            if st4 == 0:
                s_ev.inc(ctx.vec.tensor_reduce(dstcol, pS[:, 0:512], AXX, ALU.max))
            else:
                ctx.vec.tensor_reduce(tmpcol, pS[:, 0:512], AXX, ALU.max)
                s_ev.inc(ctx.vec.tensor_max(dstcol, dstcol, tmpcol))
            s_ev.wait(nc.tensor)
    sumsq_max(kbf, mcol[:, 0:1], mcol[:, 4:5])

    vview = vaug[:, :].rearrange("p (k h e) -> p k h e", k=16, h=16)
    s_vo = ctx.sem("a0_vo")
    s_vo.inc(ctx.vec.memset(vview[:, :, :, 64:65], 1.0))
    v_rot = Rot(2)
    for j in range(32):
        kt, half = j // 2, j % 2
        n0 = half * 512
        ppp = (pA, pB)[j % 2]
        v_rot.acquire(nc.tensor)
        for kc in range(2):
            mm = nc.tensor.matmul(ppp[:, 0:512],
                                  ckvln[:, kc * T + kt * 128:kc * T + kt * 128 + 128],
                                  wukv[:, kc * 2048 + 1024 + n0:kc * 2048 + 1024 + n0 + 512],
                                  start=(kc == 0), stop=(kc == 1))
            if kc == 1:
                s_mm.inc(mm)
        s_mm.wait(nc.vector)
        s_vo.wait(nc.vector)
        dst = vview[:, kt, half * 8:half * 8 + 8, 0:64]
        src = ppp[:, 0:512].rearrange("p (h d) -> p h d", d=64)
        s_ev.inc(ctx.vec.tensor_copy(dst, src))
        v_rot.release(s_ev)
    s_ev.wait(nc.tensor)
    s_ev.wait(nc.scalar)
    s_ev.wait(nc.sync)
    kv_stack.close()

    qst = stack.enter_context(nc.sbuf_tensor("a0_qs" + tag, [128, 8 * 512], BF16))
    ptb = stack.enter_context(nc.sbuf_tensor("a0_pt" + tag, [128, 4 * 512], BF16))
    amid = stack.enter_context(nc.sbuf_tensor("a0_am" + tag, [128, 8 * 512], BF16))
    rden = stack.enter_context(nc.sbuf_tensor("a0_rd" + tag, [128, 2 * 512], F32))
    wostream = stack.enter_context(nc.sbuf_tensor("a0_wo" + tag, [128, 2 * 1024], BF16))

    woT = sc["woT"]
    wo_rot = Rot(2)
    qp_rot = Rot(2)
    pt_rot = Rot(2)
    sc_rot = Rot(2)
    po_rot = Rot(2)
    rd_rot = Rot(2)
    for st4 in range(4):
        q0 = st4 * 512
        ktmax = (q0 + 511) // 128
        for oc in range(NC):
            ppp = (pA, pB)[oc % 2]
            qp_rot.acquire(nc.tensor)
            for kc in range(2):
                mm = nc.tensor.matmul(ppp[:, 0:512],
                                      wuq[:, kc * 1024 + oc * 128:kc * 1024 + oc * 128 + 128],
                                      cqln[:, kc * T + q0:kc * T + q0 + 512],
                                      start=(kc == 0), stop=(kc == 1))
                if kc == 1:
                    s_mm.inc(mm)
            s_mm.wait(nc.scalar)
            s_ab.inc(ctx.act.activation(qst[:, oc * 512:(oc + 1) * 512], ppp[:, 0:512], AF.Copy))
            qp_rot.release(s_ab)
        s_ab.wait(nc.tensor)
        sq_rot = Rot(4)
        for oc in range(NC):
            sq_rot.acquire(nc.scalar)
            s_ab.inc(ctx.act.activation(sqt[:, (oc % 4) * 512:(oc % 4) * 512 + 512],
                                          qst[:, oc * 512:(oc + 1) * 512], AF.Square))
            s_ab.wait(nc.tensor)
            mm = nc.tensor.matmul(pS[:, 0:512], ones[:, :], sqt[:, (oc % 4) * 512:(oc % 4) * 512 + 512],
                                  start=(oc == 0), stop=(oc == NC - 1))
            if oc == NC - 1:
                s_mm.inc(mm)
            sq_rot.release(s_mm)
        s_mm.wait(nc.vector)
        s_ev.inc(ctx.vec.tensor_reduce(mcol[:, 1:2], pS[:, 0:512], AXX, ALU.max))
        s_ev.wait(nc.tensor)
        s_ev.wait(nc.vector)
        s_ab.wait(nc.vector)
        s_ev.inc(ctx.vec.tensor_mul(mcol[:, 2:3], mcol[:, 0:1], mcol[:, 1:2]))
        s_ev.wait(nc.scalar)
        s_ab.inc(ctx.act.activation(mcol[:, 2:3], mcol[:, 2:3], AF.Sqrt, scale=1.0 / 64))
        s_ab.wait(nc.vector)
        s_ev.inc(ctx.vec.tensor_scalar_mul(mcol[:, 3:4], mcol[:, 2:3], -1.0))
        s_ev.wait(nc.scalar)
        s_ev.wait(nc.tensor)
        denmark = {}
        po_of = {}

        def normalize(hh):
            pbb = (hh % 2) * 64
            hcc = hh // 2
            uu = st4 * 16 + hh
            poo, pdd = po_of[hh]
            nc.scalar.wait_ge(s_mm.h, denmark[hh])
            s_ev.wait(nc.scalar)
            rds = rden[pbb:pbb + 64, (uu % 2) * 512:(uu % 2) * 512 + 512]
            ctx.act.activation(rds, pdd, AF.Ln)
            s_ab.inc(ctx.act.activation(rds, rds, AF.Exp, scale=-1.0))
            s_ab.wait(nc.vector)
            i1 = ctx.vec.tensor_mul(amid[pbb:pbb + 64, hcc * 512:hcc * 512 + 512], poo, rds)
            s_ev.inc(i1)
            po_rot.release(s_ev)

        for h in range(16):
            pb = (h % 2) * 64
            hc = h // 2
            u = st4 * 16 + h
            po = pO[pb:pb + 64, (u % 2) * 512:(u % 2) * 512 + 512]
            pd = pA[pb:pb + 64, (u % 2) * 512:(u % 2) * 512 + 512]
            po_of[h] = (po, pd)
            po_rot.acquire(nc.tensor)

            def emit_qk(p):
                kt2 = p * 2
                npair = min(2, ktmax + 1 - kt2)
                psc = (pS, pB)[p % 2]
                sc_rot.acquire(nc.tensor)
                for j in range(npair):
                    kt = kt2 + j
                    mm = nc.tensor.matmul(psc[:, j * 512:j * 512 + 512],
                                          kbf[pb:pb + 64, hc * T + kt * 128:hc * T + kt * 128 + 128],
                                          qst[pb:pb + 64, hc * 512:hc * 512 + 512],
                                          start=True, stop=True)
                    if j == npair - 1:
                        s_mm.inc(mm)
                s_mm.wait(nc.scalar)
                pt_rot.acquire(nc.scalar)
                pslot = ptb[:, (p % 2) * 1024:(p % 2) * 1024 + npair * 512]
                s_ab.inc(ctx.act.activation(pslot, psc[:, 0:npair * 512], AF.Exp,
                                            scale=0.125, bias=mcol[:, 3:4]))
                sc_rot.release(s_ab)
                need_mask = any(q0 - (kt2 + j) * 128 < 128 for j in range(npair))
                if need_mask:
                    s_ab.wait(nc.vector)
                    last_i = None
                    for j in range(npair):
                        kt = kt2 + j
                        d = q0 - kt * 128
                        if d >= 128:
                            continue
                        sl = ptb[:, (p % 2) * 1024 + j * 512:(p % 2) * 1024 + (j + 1) * 512]
                        if d < 0:
                            ctx.vec.memset(sl[:, 0:-d], 0.0)
                        mw0 = max(0, -d)
                        last_i = ctx.vec.tensor_mul(sl[:, mw0:mw0 + 128], sl[:, mw0:mw0 + 128], tril[:, :])
                    s_ev.inc(last_i)
                return need_mask

            def emit_pv(p, masked):
                kt2 = p * 2
                npair = min(2, ktmax + 1 - kt2)
                if masked:
                    s_ev.wait(nc.tensor)
                else:
                    s_ab.wait(nc.tensor, abmark[p])
                for j in range(npair):
                    kt = kt2 + j
                    sl = ptb[:, (p % 2) * 1024 + j * 512:(p % 2) * 1024 + (j + 1) * 512]
                    nc.tensor.matmul(po, vaug[:, kt * 1040 + h * 65:kt * 1040 + h * 65 + 64],
                                     sl, start=(kt == 0), stop=(kt == ktmax),
                                     tile_position=(0, pb), skip_group_check=True)
                    mm2 = nc.tensor.matmul(pd, ones[:, 0:64],
                                           sl, start=(kt == 0), stop=(kt == ktmax),
                                           tile_position=(0, pb), skip_group_check=True)
                    if j == npair - 1:
                        s_mm.inc(mm2)
                pt_rot.release(s_mm)

            npairs = (ktmax + 2) // 2
            maskedp = {}
            abmark = {}
            for p in range(npairs):
                maskedp[p] = emit_qk(p)
                abmark[p] = s_ab.v
                if p >= 1:
                    emit_pv(p - 1, maskedp[p - 1])
            emit_pv(npairs - 1, maskedp[npairs - 1])
            denmark[h] = s_mm.v
            if h >= 1:
                normalize(h - 1)
        normalize(15)
        s_ev.wait(nc.tensor)
        for oc in range(NC):
            slot = oc % 2
            wo_rot.acquire(nc.sync)
            s_dl[slot].inc(nc.sync.dma_start(wostream[:, slot * 1024:slot * 1024 + 1024], woT[oc, :, :]), 16)
            s_dl[slot].wait(nc.tensor)
            ppp = (pA, pB)[slot]
            qp_rot.acquire(nc.tensor)
            for kc in range(NC):
                mm = nc.tensor.matmul(ppp[:, 512:1024],
                                      wostream[:, slot * 1024 + kc * 128:slot * 1024 + kc * 128 + 128],
                                      amid[:, kc * 512:(kc + 1) * 512],
                                      start=(kc == 0), stop=(kc == NC - 1))
                if kc == NC - 1:
                    s_mm.inc(mm)
            wo_rot.release(s_mm)
            s_mm.wait(nc.vector)
            sl = slice(oc * T + q0, oc * T + q0 + 512)
            s_ev.inc(ctx.vec.tensor_add(xres[:, sl], xres[:, sl], ppp[:, 512:1024]))
            qp_rot.release(s_ev)
        s_ev.wait(nc.tensor)
        s_ev.wait(nc.scalar)
    s_ev.wait(nc.tensor)
    s_ev.wait(nc.scalar)
    s_ev.wait(nc.sync)
    s_ev.wait(nc.gpsimd)
    stack.close()


def emit_l1ffn(ctx, bufs, sc, tag):
    nc = ctx.nc
    xres, xln = bufs["xres"], bufs["xln"]
    mu_r, mu_k = bufs["prm"]["mu_r"], bufs["prm"]["mu_k"]
    s_dl = [ctx.sem("f1_dl0"), ctx.sem("f1_dl1")]
    s_mm = ctx.sem("f1_mm")
    s_ev = ctx.sem("f1_ev")
    s_ab = ctx.sem("f1_ab")
    s_mx = ctx.sem("f1_mx")
    stack = ExitStack()
    TS = 1024
    wbuf = stack.enter_context(nc.sbuf_tensor("f1_w" + tag, [128, 2 * 1024], BF16))
    xr = stack.enter_context(nc.sbuf_tensor("f1_xr" + tag, [128, NC * TS], BF16))
    xk = stack.enter_context(nc.sbuf_tensor("f1_xk" + tag, [128, NC * TS], BF16))
    rr = stack.enter_context(nc.sbuf_tensor("f1_r" + tag, [128, NC * TS], BF16))
    ka = stack.enter_context(nc.sbuf_tensor("f1_ka" + tag, [128, NC * TS], BF16))
    tv = stack.enter_context(nc.sbuf_tensor("f1_tv" + tag, [128, 1024], F32))
    pp = [stack.enter_context(nc.psum_tensor(f"f1_pp{i}" + tag, [128, 1024], F32)) for i in range(2)]
    wrT, wkT, wvT = sc["wrT"], sc["wkT"], sc["wvT"]
    w_rot = Rot(2)
    p_rot = Rot(2)

    for strip in range(2):
        t0 = strip * TS
        for c in range(NC):
            slc = slice(c * T + t0, c * T + t0 + TS)
            for dst, mu in ((xr, mu_r), (xk, mu_k)):
                dsl = dst[:, c * TS:(c + 1) * TS]
                if t0 == 0:
                    ctx.vec.tensor_scalar_mul(dsl[:, 0:1], xln[:, c * T:c * T + 1], mu[:, c:c + 1])
                    prev = xln[:, c * T:c * T + TS - 1]
                    cur = xln[:, c * T + 1:c * T + TS]
                    out = dsl[:, 1:TS]
                else:
                    prev = xln[:, c * T + t0 - 1:c * T + t0 + TS - 1]
                    cur = xln[:, slc]
                    out = dsl[:, :]
                ctx.vec.scalar_tensor_tensor(tv[:, :out.shape[-1]], prev, mu[:, c:c + 1], prev,
                                               ALU.mult, ALU.subtract_rev if False else ALU.subtract)
                i1 = ctx.vec.scalar_tensor_tensor(out, cur, mu[:, c:c + 1], tv[:, :out.shape[-1]],
                                                    ALU.mult, ALU.subtract)
                if c == NC - 1 and dst is xk:
                    s_mx.inc(i1)
        s_mx.wait(nc.tensor)
        for (wT, src, dst, kind) in ((wrT, xr, rr, "r"), (wkT, xk, ka, "k")):
            for oc in range(NC):
                slot = oc % 2
                wofs = slot * 1024
                w_rot.acquire(nc.sync)
                s_dl[slot].inc(nc.sync.dma_start(wbuf[:, wofs:wofs + 1024], wT[oc, :, :]), 16)
                s_dl[slot].wait(nc.tensor)
                p_rot.acquire(nc.tensor)
                for n0 in (0, 512):
                    for kc in range(NC):
                        mm = nc.tensor.matmul(pp[slot][:, n0:n0 + 512],
                                              wbuf[:, wofs + kc * 128:wofs + (kc + 1) * 128],
                                              src[:, kc * TS + n0:kc * TS + n0 + 512],
                                              start=(kc == 0), stop=(kc == NC - 1))
                        if n0 == 512 and kc == NC - 1:
                            s_mm.inc(mm)
                w_rot.release(s_mm)
                osl = dst[:, oc * TS:(oc + 1) * TS]
                if kind == "r":
                    s_mm.wait(nc.scalar)
                    s_ab.inc(ctx.act.activation(osl, pp[slot][:, :], AF.Sigmoid))
                    p_rot.release(s_ab)
                else:
                    s_mm.wait(nc.vector)
                    s_ab.wait(nc.vector)
                    s_ev.inc(ctx.vec.tensor_scalar_max(tv[:, :], pp[slot][:, :], 0.0))
                    p_rot.release(s_ev)
                    s_ev.wait(nc.scalar)
                    s_ab.inc(ctx.act.activation(osl, tv[:, :], AF.Square))
        s_ab.wait(nc.tensor)
        for oc in range(NC):
            slot = oc % 2
            wofs = slot * 1024
            w_rot.acquire(nc.sync)
            s_dl[slot].inc(nc.sync.dma_start(wbuf[:, wofs:wofs + 1024], wvT[oc, :, :]), 16)
            s_dl[slot].wait(nc.tensor)
            p_rot.acquire(nc.tensor)
            for n0 in (0, 512):
                for kc in range(NC):
                    mm = nc.tensor.matmul(pp[slot][:, n0:n0 + 512],
                                          wbuf[:, wofs + kc * 128:wofs + (kc + 1) * 128],
                                          ka[:, kc * TS + n0:kc * TS + n0 + 512],
                                          start=(kc == 0), stop=(kc == NC - 1))
                    if n0 == 512 and kc == NC - 1:
                        s_mm.inc(mm)
            w_rot.release(s_mm)
            s_mm.wait(nc.vector)
            if oc == 0:
                s_ab.wait(nc.vector)
            sl = slice(oc * T + t0, oc * T + t0 + TS)
            ctx.vec.tensor_mul(tv[:, :], rr[:, oc * TS:(oc + 1) * TS], pp[slot][:, :])
            s_ev.inc(ctx.vec.scalar_tensor_tensor(xres[:, sl], tv[:, :], 0.5, xres[:, sl],
                                                    ALU.mult, ALU.add))
            p_rot.release(s_ev)
    s_ev.wait(nc.tensor)
    s_ev.wait(nc.scalar)
    s_ev.wait(nc.sync)
    s_ev.wait(nc.gpsimd)
    stack.close()


def emit_l1attn(ctx, bufs, sc, tag):
    nc = ctx.nc
    xres, xln = bufs["xres"], bufs["xln"]
    prm, consts = bufs["prm"], bufs["consts"]
    ones = consts["ones_bf"]
    ident = consts["ident_bf"]
    obc, abc, center = consts["obc"], consts["abc"], consts["center"]
    xtok = sc["xtok"]
    s_dl = [ctx.sem("a1_dl0"), ctx.sem("a1_dl1")]
    s_mm = ctx.sem("a1_mm")
    s_ev = ctx.sem("a1_ev")
    s_ab = ctx.sem("a1_ab")
    NSLOT = L1_NSLOT
    s_gd = [ctx.sem(f"a1_gd{i}") for i in range(NSLOT)]
    s_xt = [ctx.sem(f"a1_xt{i}") for i in range(2)]
    s_gp = ctx.sem("a1_gp")
    stack = ExitStack()
    pA = stack.enter_context(nc.psum_tensor("a1_pA" + tag, [128, 1024], F32))
    pB = stack.enter_context(nc.psum_tensor("a1_pB" + tag, [128, 1024], F32))
    pT = [stack.enter_context(nc.psum_tensor(f"a1_pT{i}" + tag, [128, 128], BF16)) for i in range(2)]
    amid = stack.enter_context(nc.sbuf_tensor("a1_am" + tag, [128, NC * T], BF16))
    wqt = stack.enter_context(nc.sbuf_tensor("a1_wq" + tag, [128, 16 * 152], BF16))
    fdd = stack.enter_context(nc.sbuf_tensor("a1_fd" + tag, [128, 16 * 24], F32))
    fav = stack.enter_context(nc.sbuf_tensor("a1_fa" + tag, [128, 16 * 2], F32))
    karg = stack.enter_context(nc.sbuf_tensor("a1_ka" + tag, [128, 2176], BF16))
    scr = stack.enter_context(nc.sbuf_tensor("a1_sc" + tag, [128, 264 * 6], F32))
    sml = stack.enter_context(nc.sbuf_tensor("a1_sm" + tag, [128, 64], F32))
    la = stack.enter_context(nc.sbuf_tensor("a1_la" + tag, [128, 2 * 136], F32))
    idx32 = stack.enter_context(nc.sbuf_tensor("a1_ix" + tag, [128, 4 * NS], I32))
    vg = stack.enter_context(nc.sbuf_tensor("a1_vg" + tag, [128, L1_NSLOT * 1024], BF16))
    acc = stack.enter_context(nc.sbuf_tensor("a1_ac" + tag, [128, 1024], F32))
    xts = stack.enter_context(nc.sbuf_tensor("a1_xs" + tag, [128, 2 * 1024], BF16))
    wsml = stack.enter_context(nc.sbuf_tensor("a1_ws" + tag, [128, 192 + 1024 + 2048 + 2048], BF16))
    O_WAVE, O_QUERY, O_SE1, O_SE2 = 0, 192, 192 + 1024, 192 + 1024 + 2048
    wostream = stack.enter_context(nc.sbuf_tensor("a1_wo" + tag, [128, 2 * 1024], BF16))
    brow = stack.enter_context(nc.sbuf_tensor("a1_br" + tag, [1, 152], BF16))
    kcb = stack.enter_context(nc.sbuf_tensor("a1_kc" + tag, [128, 272], F32))
    kcr = stack.enter_context(nc.sbuf_tensor("a1_kr" + tag, [1, 272], F32))

    s_dl[0].inc(nc.sync.dma_start(wsml[:, O_WAVE:O_WAVE + 192].rearrange("p (k n) -> p k n", k=8),
                                  sc["wwave"].rearrange("k p n -> p k n")), 16)
    s_dl[0].inc(nc.sync.dma_start(wsml[:, O_QUERY:O_QUERY + 1024].rearrange("p (k n) -> p k n", k=8),
                                  sc["wquery"].rearrange("k p n -> p k n")), 16)
    s_dl[0].inc(nc.sync.dma_start(wsml[:, O_SE1:O_SE1 + 2048].rearrange("p (k n) -> p k n", k=8),
                                  sc["wse1"].rearrange("k p n -> p k n")), 16)
    s_dl[0].inc(nc.sync.dma_start(wsml[:, O_SE2:O_SE2 + 2048].rearrange("p (k n) -> p k n", k=2),
                                  sc["wse2"].rearrange("k p n -> p k n")), 16)
    s_dl[0].wait(nc.vector)
    s_dl[0].wait(nc.tensor)
    ctx.vec.tensor_copy(brow[0:1, 0:24], prm["wave_b"][0:1, :])
    s_ev.inc(ctx.vec.tensor_copy(brow[0:1, 24:152], prm["query_b"][0:1, :]))
    ctx.vec.tensor_tensor(kcr[0:1, :],
                            abc[0:1, :].rearrange("p (a o) -> p a o", o=1).broadcast_to([1, NA, POS]),
                            prm["key_w"][0:1, :].rearrange("p (o d) -> p o d", o=1).broadcast_to([1, NA, POS]),
                            ALU.mult)
    s_ev.inc(ctx.vec.tensor_copy(xts[0:1, 0:272], kcr[0:1, :]))
    s_ev.wait(nc.tensor)
    mm = nc.tensor.matmul(pA[:, 0:272], ones[0:1, :], xts[0:1, 0:272],
                          start=True, stop=True)
    s_mm.inc(mm)
    s_mm.wait(nc.vector)
    s_ev.inc(ctx.vec.tensor_copy(kcb[:, :], pA[:, 0:272]))
    s_ev.wait(nc.tensor)

    xt_rot = Rot(2)
    ptrot = Rot(2)
    for tc in range(16):
        sl = tc % 2
        xt_rot.acquire(nc.vector)
        s_xt[sl].wait(nc.vector)
        for c in range(NC):
            pt = pT[c % 2]
            ptrot.acquire(nc.tensor)
            s_mm.inc(nc.tensor.transpose(pt[:, :], xln[:, c * T + tc * 128:c * T + tc * 128 + 128], ident[:, :]))
            s_mm.wait(nc.vector)
            i1 = ctx.vec.tensor_copy(xts[:, sl * 1024 + c * 128:sl * 1024 + c * 128 + 128], pt[:, :])
            s_ev.inc(i1)
            ptrot.release(s_ev)
        s_ev.wait(nc.sync)
        s_xt[sl].inc(nc.sync.dma_start(xtok[PAD + tc * 128:PAD + tc * 128 + 128, :], xts[:, sl * 1024:sl * 1024 + 1024]), 16)
        xt_rot.release(s_xt[sl])
    if L1_LIMIT < 2:
        s_xt[0].wait(nc.sync)
        s_xt[1].wait(nc.sync)
        s_ev.wait(nc.tensor); s_ev.wait(nc.scalar); s_ev.wait(nc.sync); s_ev.wait(nc.gpsimd)
        stack.close()
        return
    s_ev.wait(nc.tensor)
    wq_rot = Rot(2)
    for tc in range(16):
        ppp = (pA, pB)[tc % 2]
        wq_rot.acquire(nc.tensor)
        for kc in range(NC):
            nc.tensor.matmul(ppp[:, 0:24], xln[:, kc * T + tc * 128:kc * T + tc * 128 + 128],
                             wsml[:, O_WAVE + kc * 24:O_WAVE + kc * 24 + 24],
                             start=(kc == 0), stop=False, skip_group_check=True)
            nc.tensor.matmul(ppp[:, 24:152], xln[:, kc * T + tc * 128:kc * T + tc * 128 + 128],
                             wsml[:, O_QUERY + kc * 128:O_QUERY + kc * 128 + 128],
                             start=False, stop=False, skip_group_check=True)
        mm = nc.tensor.matmul(ppp[:, 0:152], ones[0:1, :], brow[0:1, :],
                              start=False, stop=True, skip_group_check=True)
        s_mm.inc(mm)
        s_mm.wait(nc.scalar)
        s_ab.inc(ctx.act.activation(wqt[:, tc * 152:(tc + 1) * 152], ppp[:, 0:152], AF.Silu))
        wq_rot.release(s_ab)
    if L1_LIMIT < 3:
        s_ab.wait(nc.tensor); s_ab.wait(nc.vector); s_ab.wait(nc.sync); s_ab.wait(nc.gpsimd)
        stack.close()
        return
    for tc in range(16):
        w0 = tc * 152
        f0 = tc * 24
        ctx.act.activation(fdd[:, f0:f0 + 24].rearrange("p (g e) -> p g e", g=3)[:, 0:3:2, :],
                             wqt[:, w0:w0 + 24].rearrange("p (g e) -> p g e", g=3)[:, 0:3:2, :],
                             AF.Sigmoid)
        i1 = ctx.act.activation(fdd[:, f0 + 8:f0 + 16], wqt[:, w0 + 8:w0 + 16], AF.Tanh)
        if tc == 15:
            s_ab.inc(i1)
    s_ab.wait(nc.vector)
    fview = fdd[:, :].rearrange("p (t g e) -> p t g e", t=16, g=3)
    ctx.vec.tensor_scalar(fview[:, :, 0, :], fview[:, :, 0, :], 15.0, 1.0, ALU.mult, ALU.add)
    ctx.vec.tensor_scalar_mul(fview[:, :, 1, :], fview[:, :, 1, :], 16.0)
    ctx.vec.tensor_scalar(fview[:, :, 2, :], fview[:, :, 2, :], 9.5, 0.5, ALU.mult, ALU.add)
    ctx.vec.tensor_reduce(fav[:, 0:16], fview[:, :, 0, :], AXX, ALU.add)
    ctx.vec.tensor_reduce(fav[:, 16:32], fview[:, :, 1, :], AXX, ALU.add)
    s_ev.inc(ctx.vec.tensor_scalar_mul(fav[:, 0:32], fav[:, 0:32], 0.125))
    s_ev.wait(nc.gpsimd)
    s_xt[0].wait(nc.gpsimd)
    s_xt[1].wait(nc.gpsimd)

    if L1_LIMIT < 4:
        s_ev.wait(nc.tensor); s_ev.wait(nc.scalar); s_ev.wait(nc.sync); s_ev.wait(nc.gpsimd)
        stack.close()
        return
    LOOK = 4
    NMACG = 1
    wwall = stack.enter_context(nc.sbuf_tensor("a1_ww" + tag, [128, LOOK * 264], F32))
    s_cblk = ctx.sem("a1_cb")
    s_mg = ctx.sem("a1_mg")
    s_tp2 = ctx.sem("a1_t2")
    cmark = []
    gmark = {}
    LG, ENV, WE, T33 = 0, 264, 528, 792

    def stage_c_gen(tc):
        f0 = tc * 24
        ws = (tc % LOOK) * 264
        ww = wwall[:, ws:ws + 264]
        ctx.vec.tensor_scalar_add(sml[:, 48:49], center[:, 0:1], float(tc * 128))
        ctx.vec.tensor_add(sml[:, 48:49], sml[:, 48:49], fav[:, 16 + tc:17 + tc])
        ctx.vec.scalar_tensor_tensor(scr[:, T33:T33 + NS], obc[:, :], fav[:, tc:tc + 1],
                                     sml[:, 48:49].broadcast_to([128, NS]), ALU.mult, ALU.add)
        sp = scr[:, T33:T33 + NS]
        yield
        ctx.vec.tensor_single_scalar(scr[:, T33 + NS:T33 + 2 * NS], sp, 0.0, ALU.is_ge)
        ctx.vec.tensor_single_scalar(scr[:, T33 + 2 * NS:T33 + 3 * NS], sp, float(T), ALU.is_lt)
        vm = scr[:, T33 + NS:T33 + 2 * NS]
        ctx.vec.tensor_mul(vm, vm, scr[:, T33 + 2 * NS:T33 + 3 * NS])
        yield
        fl = scr[:, T33 + 3 * NS:T33 + 4 * NS]
        ctx.vec.tensor_scalar_add(fl, sp, MAGIC)
        ctx.vec.tensor_scalar_add(fl, fl, -MAGIC)
        gt = scr[:, T33 + 4 * NS:T33 + 5 * NS]
        ctx.vec.tensor_tensor(gt, fl, sp, ALU.is_gt)
        ctx.vec.tensor_sub(fl, fl, gt)
        ctx.vec.tensor_scalar_add(fl, fl, float(PAD))
        yield
        ctx.vec.tensor_tensor(
            karg[:, :].rearrange("p (h ad) -> p h ad", h=H1),
            kcb[:, :].rearrange("p (o ad) -> p o ad", o=1).broadcast_to([128, H1, 272]),
            fdd[:, f0:f0 + 8].rearrange("p (h o) -> p h o", o=1).broadcast_to([128, H1, 272]),
            ALU.mult)
        s_ev.inc(ctx.vec.engine_nop())
        yield
        s_ev.wait(nc.scalar)
        s_ab.inc(ctx.act.activation(karg[:, :], karg[:, :], AF.Silu))
        yield
        s_ab.wait(nc.vector)
        for h in range(H1):
            q = wqt[:, tc * 152 + 24 + h * 16:tc * 152 + 24 + (h + 1) * 16]
            ctx.vec.tensor_tensor(karg[:, h * 272:(h + 1) * 272],
                                  karg[:, h * 272:(h + 1) * 272].rearrange("p (a d) -> p a d", d=POS),
                                  q.rearrange("p (o d) -> p o d", o=1).broadcast_to([128, NA, POS]),
                                  ALU.mult)
            if h == 3:
                yield
        yield
        ctx.vec.tensor_reduce(la[:, 0:136], karg[:, :].rearrange("p (ha d) -> p ha d", d=POS),
                              AXX, ALU.add)
        yield
        ctx.vec.tensor_scalar_max(sml[:, 32:40], fdd[:, f0 + 16:f0 + 24], 0.1)
        ctx.vec.reciprocal(sml[:, 32:40], sml[:, 32:40])
        ctx.vec.tensor_tensor(
            la[:, 136:272].rearrange("p (h a) -> p h a", h=H1),
            abc[:, :].rearrange("p (o a) -> p o a", o=1).broadcast_to([128, H1, NA]),
            fdd[:, f0:f0 + 8].rearrange("p (h o) -> p h o", o=1).broadcast_to([128, H1, NA]),
            ALU.mult)
        ctx.vec.tensor_tensor(
            la[:, 136:272].rearrange("p (h a) -> p h a", h=H1),
            la[:, 136:272].rearrange("p (h a) -> p h a", h=H1),
            sml[:, 32:40].rearrange("p (h o) -> p h o", o=1).broadcast_to([128, H1, NA]),
            ALU.mult)
        yield
        lg = scr[:, LG:LG + 264]
        env = scr[:, ENV:ENV + 264]
        for (srcoff, dst) in ((0, lg), (136, env)):
            ctx.vec.tensor_copy(dst[:, :].rearrange("p (h s) -> p h s", h=H1)[:, :, 0:16],
                                la[:, srcoff:srcoff + 136].rearrange("p (h a) -> p h a", h=H1)[:, :, 16:0:-1])
            ctx.vec.tensor_copy(dst[:, :].rearrange("p (h s) -> p h s", h=H1)[:, :, 16:33],
                                la[:, srcoff:srcoff + 136].rearrange("p (h a) -> p h a", h=H1)[:, :, 0:17])
            yield
        BIG = 1e30
        ctx.vec.tensor_scalar(scr[:, T33 + 2 * NS:T33 + 3 * NS], vm, BIG, -BIG, ALU.mult, ALU.add)
        mneg = scr[:, T33 + 2 * NS:T33 + 3 * NS]
        ctx.vec.tensor_tensor(lg, lg, vm.rearrange("p (o s) -> p o s", o=1).broadcast_to([128, H1, NS]), ALU.mult)
        ctx.vec.tensor_tensor(lg, lg, mneg.rearrange("p (o s) -> p o s", o=1).broadcast_to([128, H1, NS]), ALU.add)
        yield
        ctx.vec.tensor_reduce(sml[:, 0:8], lg.rearrange("p (h s) -> p h s", h=H1), AXX, ALU.max)
        ctx.vec.tensor_scalar_mul(sml[:, 0:8], sml[:, 0:8], -0.25)
        i1 = ctx.vec.scalar_tensor_tensor(
            lg, lg, 0.25,
            sml[:, 0:8].rearrange("p (h o) -> p h o", o=1).broadcast_to([128, H1, NS]),
            ALU.mult, ALU.add)
        s_ev.inc(i1)
        yield
        s_ev.wait(nc.scalar)
        ctx.act.activation(lg, lg, AF.Exp)
        s_ab.inc(ctx.act.activation(env, env, AF.Exp, scale=-1.0))
        yield
        s_ab.wait(nc.vector)
        we = scr[:, WE:WE + 264]
        ctx.vec.tensor_reduce(sml[:, 8:16], lg.rearrange("p (h s) -> p h s", h=H1), AXX, ALU.add)
        ctx.vec.tensor_mul(we, lg, env)
        yield
        ctx.vec.tensor_reduce(sml[:, 16:24], we.rearrange("p (h s) -> p h s", h=H1), AXX, ALU.add)
        ctx.vec.scalar_tensor_tensor(sml[:, 24:32], sml[:, 8:16], 1e-8, sml[:, 16:24], ALU.mult, ALU.add)
        ctx.vec.reciprocal(sml[:, 24:32], sml[:, 24:32])
        yield "hold"
        s_ab.wait(nc.vector)
        i1 = ctx.vec.tensor_tensor(
            ww[:, :].rearrange("p (h s) -> p h s", h=H1),
            we[:, :].rearrange("p (h s) -> p h s", h=H1),
            sml[:, 24:32].rearrange("p (h o) -> p h o", o=1).broadcast_to([128, H1, NS]),
            ALU.mult)
        if tc >= LOOK:
            s_gp.wait(nc.vector, gmark[tc - LOOK])
        i1 = ctx.vec.tensor_copy(idx32[:, (tc % LOOK) * NS:(tc % LOOK) * NS + NS],
                                 scr[:, T33 + 3 * NS:T33 + 4 * NS])
        s_cblk.inc(i1)
        cmark.append(s_cblk.v)

    s_mac = [ctx.sem(f"a1_mc{i}") for i in range(NSLOT)]
    slot_free_at = [0] * NSLOT
    stream = [(t, si) for t in range(16) for si in range(NS)]
    gnext = [0]

    def issue_gather_upto(k):
        while gnext[0] <= min(k, len(stream) - 1):
            tg, sg = stream[gnext[0]]
            slot = gnext[0] % NSLOT
            if sg == 0:
                s_cblk.wait(nc.gpsimd, cmark[tg])
            if slot_free_at[slot] > 0:
                nc.gpsimd.wait_ge(s_mac[slot].h, slot_free_at[slot])
            ins = nc.gpsimd.indirect_dma_start(
                vg[:, slot * 1024:(slot + 1) * 1024], None,
                xtok, IndirectOffsetOnAxis(idx32[:, (tg % LOOK) * NS + sg:(tg % LOOK) * NS + sg + 1], 0))
            s_gd[slot].inc(ins, 16)
            if sg == NS - 1:
                s_gp.inc(nc.gpsimd.engine_nop())
                gmark[tg] = s_gp.v
            gnext[0] += 1

    scl_ready = [None] * NSLOT

    def macs(tc, gen):
        held = [gen is None]

        def pump():
            if held[0]:
                return
            try:
                if next(gen) == "hold":
                    held[0] = True
            except StopIteration:
                held[0] = True

        ws = (tc % LOOK) * 264
        ww = wwall[:, ws:ws + 264]
        pacc = (pA, pB)[tc % 2]
        if tc >= 2:
            s_ab.wait(nc.tensor, tpmark[tc - 2])
        for si in range(NS):
            k = tc * NS + si
            issue_gather_upto(k + NSLOT - 1)
            slot = k % NSLOT
            on_act = (L1_SPLIT > 0 and si % L1_SPLIT == L1_SPLIT - 1)
            weng = nc.scalar if on_act else nc.vector
            s_gd[slot].wait(weng)
            vgv = vg[:, slot * 1024:(slot + 1) * 1024].rearrange("p (h d) -> p h d", h=H1)
            wbc = ww[:, :].rearrange("p (h s) -> p h s", h=H1)[:, :, si:si + 1].broadcast_to([128, H1, 128])
            if on_act:
                for h in range(H1):
                    wcol = ww[:, h * NS + si:h * NS + si + 1]
                    vs = vg[:, slot * 1024 + h * 128:slot * 1024 + (h + 1) * 128]
                    i1 = ctx.act.activation(vs, vs, AF.Copy, scale=wcol)
                s_ab.inc(i1)
                scl_ready[slot] = (s_ab, s_ab.v)
            else:
                i1 = ctx.vec.tensor_tensor(vgv, vgv, wbc, ALU.mult)
                s_ev.inc(i1)
                scl_ready[slot] = (s_ev, s_ev.v)
            sem, v = scl_ready[slot]
            nc.tensor.wait_ge(sem.h, v)
            for half in range(2):
                mm = nc.tensor.matmul(pacc[:, half * 512:(half + 1) * 512],
                                      consts["ident_bf"][:, :],
                                      vg[:, slot * 1024 + half * 512:slot * 1024 + (half + 1) * 512],
                                      start=(si == 0), stop=(si == NS - 1),
                                      skip_group_check=True)
                if half == 1:
                    s_mac[slot].inc(mm)
                    slot_free_at[slot] = s_mac[slot].v
            pump()

    tpmark = {}

    def evac(tc):
        pacc = (pA, pB)[tc % 2]
        xsl = xts[:, (tc % 2) * 1024:(tc % 2) * 1024 + 1024]
        s_mm.wait(nc.scalar)
        s_mac[(tc * NS + NS - 1) % NSLOT].wait(nc.scalar)
        i1 = ctx.act.activation(xsl, pacc[:, 0:1024], AF.Copy)
        s_ab.inc(i1)
        tpmark[tc] = s_ab.v
        s_ab.wait(nc.tensor)
        for c in range(NC):
            pt = pT[c % 2]
            ptrot.acquire(nc.tensor)
            s_mm.inc(nc.tensor.transpose(pt[:, :], xsl[:, c * 128:c * 128 + 128], ident[:, :]))
            s_mm.wait(nc.scalar)
            i1 = ctx.act.activation(amid[:, c * T + tc * 128:c * T + tc * 128 + 128], pt[:, :], AF.Copy)
            s_ab.inc(i1)
            ptrot.release(s_ab)

    for t in range(LOOK):
        for _ in stage_c_gen(t):
            pass
    for t in range(16):
        gen = stage_c_gen(t + LOOK) if t + LOOK < 16 else None
        if not L1_ILV and gen is not None:
            while next(gen, "hold") != "hold":
                pass
            macs(t, None)
        else:
            macs(t, gen)
        if gen is not None:
            for _ in gen:
                pass
        evac(t)
    s_ev.wait(nc.tensor)
    s_ab.wait(nc.tensor)
    s_ev.wait(nc.gpsimd)
    s_ab.wait(nc.gpsimd)

    if L1_LIMIT < 7:
        s_ev.wait(nc.tensor); s_ev.wait(nc.scalar); s_ev.wait(nc.sync); s_ev.wait(nc.gpsimd)
        s_gd[0].wait(nc.vector); s_gd[1].wait(nc.vector); s_gd[2].wait(nc.vector); s_gd[3].wait(nc.vector)
        stack.close()
        return
    se_rot = Rot(2)
    seh = stack.enter_context(nc.sbuf_tensor("a1_sh" + tag, [128, 2 * T], BF16))
    sec = stack.enter_context(nc.sbuf_tensor("a1_se" + tag, [128, 512], BF16))
    for st4 in range(4):
        n0 = st4 * 512
        for oc in range(2):
            ppp = (pA, pB)[oc % 2]
            se_rot.acquire(nc.tensor)
            for kc in range(NC):
                mm = nc.tensor.matmul(ppp[:, 0:512],
                                      wsml[:, O_SE1 + kc * 256 + oc * 128:O_SE1 + kc * 256 + oc * 128 + 128],
                                      amid[:, kc * T + n0:kc * T + n0 + 512],
                                      start=(kc == 0), stop=(kc == NC - 1))
                if kc == NC - 1:
                    s_mm.inc(mm)
            s_mm.wait(nc.scalar)
            s_ab.inc(ctx.act.activation(seh[:, oc * T + n0:oc * T + n0 + 512], ppp[:, 0:512],
                                          AF.Silu, bias=prm["se1_b"][:, oc:oc + 1]))
            se_rot.release(s_ab)
    s_ab.wait(nc.tensor)
    for st4 in range(4):
        n0 = st4 * 512
        for oc in range(NC):
            ppp = (pA, pB)[oc % 2]
            se_rot.acquire(nc.tensor)
            for kc in range(2):
                mm = nc.tensor.matmul(ppp[:, 0:512],
                                      wsml[:, O_SE2 + kc * 1024 + oc * 128:O_SE2 + kc * 1024 + oc * 128 + 128],
                                      seh[:, kc * T + n0:kc * T + n0 + 512],
                                      start=(kc == 0), stop=(kc == 1))
                if kc == 1:
                    s_mm.inc(mm)
            s_mm.wait(nc.scalar)
            s_ab.inc(ctx.act.activation(sec[:, 0:512], ppp[:, 0:512],
                                          AF.Sigmoid, bias=prm["se2_b"][:, oc:oc + 1]))
            s_ab.wait(nc.vector)
            sl = slice(oc * T + n0, oc * T + n0 + 512)
            s_ev.inc(ctx.vec.tensor_mul(amid[:, sl], amid[:, sl], sec[:, 0:512]))
            se_rot.release(s_ev)
            s_ev.wait(nc.scalar)
    s_ev.wait(nc.tensor)
    woutw = sc["woutw"]
    wo_rot = Rot(2)
    for st4 in range(4):
        n0 = st4 * 512
        for oc in range(NC):
            slot = oc % 2
            wo_rot.acquire(nc.sync)
            s_dl[slot].inc(nc.sync.dma_start(wostream[:, slot * 1024:(slot + 1) * 1024], woutw[oc, :, :]), 16)
            s_dl[slot].wait(nc.tensor)
            ppp = (pA, pB)[slot]
            se_rot.acquire(nc.tensor)
            for kc in range(NC):
                mm = nc.tensor.matmul(ppp[:, 0:512],
                                      wostream[:, slot * 1024 + kc * 128:slot * 1024 + kc * 128 + 128],
                                      amid[:, kc * T + n0:kc * T + n0 + 512],
                                      start=(kc == 0), stop=(kc == NC - 1))
                if kc == NC - 1:
                    s_mm.inc(mm)
            wo_rot.release(s_mm)
            s_mm.wait(nc.scalar)
            s_ab.inc(ctx.act.activation(sec[:, 0:512], ppp[:, 0:512], AF.Silu))
            s_ab.wait(nc.vector)
            sl = slice(oc * T + n0, oc * T + n0 + 512)
            s_ev.inc(ctx.vec.tensor_add(xres[:, sl], xres[:, sl], sec[:, 0:512]))
            se_rot.release(s_ev)
            s_ev.wait(nc.scalar)
    s_ev.wait(nc.tensor)
    s_ev.wait(nc.scalar)
    s_ev.wait(nc.sync)
    s_ev.wait(nc.gpsimd)
    stack.close()


def build_program(phases, n_seq=NSEQ):
    nc = bass.Bass("TRN2", target_bir_lowering=False, detect_race_conditions=False)
    stack = ExitStack()
    ctx = Ctx(nc, stack)
    io = declare_io(nc, phases)
    sc = declare_scratch(nc, phases)

    ctx.vec
    ctx.act
    consts = emit_consts(ctx, io, phases)
    prm = emit_params(ctx, io, phases)
    barrier(ctx)
    emit_preprocess(ctx, io, sc, phases, consts)
    barrier(ctx)

    xres = ctx.sb("xres", [128, NC * T], F32)
    xln = ctx.sb("xln", [128, NC * T], BF16)
    bufs = {"xres": xres, "xln": xln, "consts": consts, "prm": prm, "io": io}

    sx = ctx.sem("x_dl")
    sy = ctx.sem("y_st")
    for s in range(n_seq):
        for c in range(NC):
            sx.inc(nc.sync.dma_start(xres[:, c * T:(c + 1) * T], io["x"][s, c * 128:(c + 1) * 128, :]), 16)
        for eng in (nc.vector, nc.scalar, nc.tensor, nc.gpsimd, nc.sync):
            sx.wait(eng)
        for pi, ph in enumerate(phases):
            tag = f"_s{s}p{pi}"
            emit_ln(ctx, bufs, tag)
            if ph == "l0ffn":
                emit_l0ffn(ctx, bufs, sc, tag)
            elif ph == "l0attn":
                emit_l0attn(ctx, bufs, sc, tag)
            elif ph == "l1ffn":
                emit_l1ffn(ctx, bufs, sc, tag)
            elif ph == "l1attn":
                emit_l1attn(ctx, bufs, sc, tag)
            else:
                raise ValueError(ph)
        for c in range(NC):
            sy.inc(nc.sync.dma_start(io["y"][s, c * 128:(c + 1) * 128, :], xres[:, c * T:(c + 1) * T]), 16)
        if s < n_seq - 1:
            sy.wait(nc.sync)
    sy.wait(nc.sync)
    stack.close()
    return nc


ALL_PHASES = ("l0ffn", "l0attn", "l0ffn", "l1ffn", "l1attn", "l1ffn")


def _run(phases, inputs, x_override=None, cores=8):
    x = inputs["L0_attn_x"] if x_override is None else x_override
    xt = np.ascontiguousarray(x.transpose(0, 2, 1))
    nc = build_program(phases)
    in_maps = [{"x": np.ascontiguousarray(xt[ci * NSEQ:(ci + 1) * NSEQ])} for ci in range(cores)]
    for m in in_maps:
        for n in _needed_inputs(phases):
            m[n] = np.ascontiguousarray(inputs[n])
    res = run_bass_kernel_spmd(nc, in_maps, core_ids=list(range(cores)))
    out = np.empty((cores * NSEQ, T, D), np.float32)
    for ci in range(cores):
        y = res.results[ci]["y"]
        out[ci * NSEQ:(ci + 1) * NSEQ] = y.transpose(0, 2, 1)
    return out


def kernel(**inputs):
    return _run(ALL_PHASES, inputs)


# version 16
# speedup vs baseline: 1.0443x; 1.0443x over previous
import numpy as np
import ml_dtypes
from contextlib import ExitStack

import concourse.bass as bass
import concourse.mybir as mybir
from concourse.bass_utils import run_bass_kernel_spmd
from concourse.bass import IndirectOffsetOnAxis

F32 = mybir.dt.float32
BF16 = mybir.dt.bfloat16
I32 = mybir.dt.int32
I16 = mybir.dt.int16
U8 = mybir.dt.uint8
AF = mybir.ActivationFunctionType
ALU = mybir.AluOpType
AXX = mybir.AxisListType.X

D = 1024
T = 2048
NSEQ = 4
NC = 8
HID = 3072
HC = HID // 128
R127 = float(np.float32(1.0 / 127.0))
MAGIC = 8388608.0
LNEPS = 1e-5

H1 = 8
POS = 16
NS = 33
NA = 17
WIN = 672
PAD = 272
TPAD = T + 2 * PAD
L1_LIMIT = 99
L1_NOMAC = False
L1_FIXIDX = False
L1_NGATHER = NS
L1_NSLOT = 8
L1_SPLIT = 4
L1_ILV = True


class Sem:
    def __init__(self, h):
        self.h = h
        self.v = 0

    def inc(self, ins, n=1):
        ins.then_inc(self.h, n)
        self.v += n
        return ins

    def wait(self, eng, v=None):
        eng.wait_ge(self.h, self.v if v is None else v)


class Rot:

    def __init__(self, n):
        self.n = n
        self.q = []

    def acquire(self, *engs):
        if len(self.q) >= self.n:
            sem, v = self.q.pop(0)
            for e in engs:
                e.wait_ge(sem.h, v)

    def release(self, sem):
        self.q.append((sem, sem.v))


class Ctx:
    def __init__(self, nc, stack):
        self.nc = nc
        self.stack = stack
        self._sems = {}

    def sem(self, name) -> Sem:
        if name not in self._sems:
            self._sems[name] = Sem(self.stack.enter_context(self.nc.semaphore(name)))
        return self._sems[name]

    def sb(self, name, shape, dtype):
        return self.stack.enter_context(self.nc.sbuf_tensor(name, shape, dtype))

    @property
    def vec(self):
        if not hasattr(self, "_vec"):
            self._vec = Guarded(self, self.nc.vector, "vgd")
        return self._vec

    @property
    def act(self):
        if not hasattr(self, "_act"):
            self._act = Guarded(self, self.nc.scalar, "agd")
        return self._act


_GUARD_OPS = {
    "tensor_copy", "tensor_tensor", "tensor_add", "tensor_sub", "tensor_mul",
    "tensor_max", "tensor_scalar", "tensor_single_scalar", "tensor_scalar_add",
    "tensor_scalar_sub", "tensor_scalar_mul", "tensor_scalar_max",
    "tensor_scalar_min", "tensor_relu", "tensor_reduce", "reduce_max",
    "reduce_sum", "scalar_tensor_tensor", "reciprocal", "select", "memset",
    "activation", "copy", "mul", "add", "square", "sqrt", "sign",
}


class Guarded:

    SHORT = 64

    def __init__(self, ctx, eng, name):
        self._ctx = ctx
        self._eng = eng
        self._name = name
        self._recent = []
        self._pad = ctx.sb("gpad_" + name, [128, 128], F32)
        if hasattr(eng, "memset"):
            eng.memset(self._pad[:, :], 0.0)
        else:
            eng.memzero(self._pad[:, :])

    def _gap(self):
        if self._name.startswith("a"):
            self._eng.activation(self._pad[:, 0:64], self._pad[:, 64:128], AF.Copy)
        else:
            self._eng.tensor_copy(self._pad[:, 0:64], self._pad[:, 64:128])

    def __getattr__(self, attr):
        real = getattr(self._eng, attr)
        if attr not in _GUARD_OPS:
            return real

        def wrap(*args, **kw):
            aps = [a for a in list(args) + list(kw.values()) if isinstance(a, bass.AP)]
            out = aps[0] if aps else None
            ins = aps[1:]
            innames = {a.tensor.name for a in ins}
            if out is not None and attr != "memset":
                innames.add(out.tensor.name)
            if any(n in self._recent for n in innames):
                self._gap()
                self._recent = []
            ins_res = real(*args, **kw)
            if out is not None and out.free_size() < self.SHORT:
                self._recent.append(out.tensor.name)
                self._recent = self._recent[-2:]
            else:
                self._recent = []
            return ins_res

        return wrap


def barrier(ctx):
    nc = ctx.nc
    with nc.Block() as blk:
        @blk.vector
        def _(v):
            v.engine_nop()


TRANSPOSED = {
    "wgT": ("L0_ffn_w_gate", HID, D, True),
    "wuT": ("L0_ffn_w_up", HID, D, True),
    "wdT": ("L0_ffn_w_down", D, HID, True),
    "woT": ("L0_attn_W_o", D, D, False),
    "wrT": ("L1_ffn_W_r", D, D, False),
    "wkT": ("L1_ffn_W_k", D, D, False),
    "wvT": ("L1_ffn_W_v", D, D, False),
}
NATURAL = {
    "wdq": ("L0_attn_W_dq", D, 256),
    "wuq": ("L0_attn_W_uq", 256, D),
    "wdkv": ("L0_attn_W_dkv", D, 256),
    "wukv": ("L0_attn_W_ukv", 256, 2 * D),
    "wwave": ("L1_attn_wave_w", D, 24),
    "wquery": ("L1_attn_query_w", D, 128),
    "wse1": ("L1_attn_se1_w", D, 256),
    "wse2": ("L1_attn_se2_w", 256, D),
    "woutw": ("L1_attn_out_w", D, D),
}
SHAPES = {
    "L0_attn_W_dq": ([D, 256], F32), "L0_attn_W_uq": ([256, D], F32),
    "L0_attn_q_ln_g": ([256], F32), "L0_attn_q_ln_b": ([256], F32),
    "L0_attn_W_dkv": ([D, 256], F32), "L0_attn_W_ukv": ([256, 2 * D], F32),
    "L0_attn_kv_ln_g": ([256], F32), "L0_attn_kv_ln_b": ([256], F32),
    "L0_attn_W_o": ([D, D], F32),
    "L0_ffn_w_gate": ([HID, D], I32), "L0_ffn_w_up": ([HID, D], I32),
    "L0_ffn_w_down": ([D, HID], I32),
    "L1_attn_wave_w": ([D, 24], F32), "L1_attn_wave_b": ([24], F32),
    "L1_attn_query_w": ([D, 128], F32), "L1_attn_query_b": ([128], F32),
    "L1_attn_key_w": ([POS], F32), "L1_attn_out_w": ([D, D], F32),
    "L1_attn_se1_w": ([D, 256], F32), "L1_attn_se1_b": ([256], F32),
    "L1_attn_se2_w": ([256, D], F32), "L1_attn_se2_b": ([D], F32),
    "L1_ffn_W_r": ([D, D], F32), "L1_ffn_W_k": ([D, D], F32),
    "L1_ffn_W_v": ([D, D], F32),
    "L1_ffn_mu_r": ([D], F32), "L1_ffn_mu_k": ([D], F32),
}


def _needed_inputs(phases):
    need = set()
    if "l0ffn" in phases:
        need |= {"L0_ffn_w_gate", "L0_ffn_w_up", "L0_ffn_w_down"}
    if "l0attn" in phases:
        need |= {"L0_attn_W_dq", "L0_attn_W_uq", "L0_attn_q_ln_g", "L0_attn_q_ln_b",
                 "L0_attn_W_dkv", "L0_attn_W_ukv", "L0_attn_kv_ln_g", "L0_attn_kv_ln_b",
                 "L0_attn_W_o"}
    if "l1ffn" in phases:
        need |= {"L1_ffn_W_r", "L1_ffn_W_k", "L1_ffn_W_v", "L1_ffn_mu_r", "L1_ffn_mu_k"}
    if "l1attn" in phases:
        need |= {"L1_attn_wave_w", "L1_attn_wave_b", "L1_attn_query_w", "L1_attn_query_b",
                 "L1_attn_key_w", "L1_attn_out_w", "L1_attn_se1_w", "L1_attn_se1_b",
                 "L1_attn_se2_w", "L1_attn_se2_b"}
    return sorted(need)


def declare_io(nc, phases):
    io = {}
    io["x"] = nc.dram_tensor("x", [NSEQ, D, T], F32, kind="ExternalInput").ap()
    io["y"] = nc.dram_tensor("y", [NSEQ, D, T], F32, kind="ExternalOutput").ap()
    for n in _needed_inputs(phases):
        sh, dt = SHAPES[n]
        io[n] = nc.dram_tensor(n, sh, dt, kind="ExternalInput").ap()
    return io


def declare_scratch(nc, phases):
    sc = {}
    tneed = []
    nneed = []
    if "l0ffn" in phases:
        tneed += ["wgT", "wuT", "wdT"]
    if "l0attn" in phases:
        tneed += ["woT"]
        nneed += ["wdq", "wuq", "wdkv", "wukv"]
    if "l1ffn" in phases:
        tneed += ["wrT", "wkT", "wvT"]
    if "l1attn" in phases:
        nneed += ["wwave", "wquery", "wse1", "wse2"]
        sc["woutw"] = nc.dram_tensor("sc_woutw", [D // 128, 128, D], BF16, kind="Internal").ap()
        sc["xtok"] = nc.dram_tensor("sc_xtok", [TPAD, D], BF16, kind="Internal").ap()
    for n in tneed:
        R, C = TRANSPOSED[n][1], TRANSPOSED[n][2]
        sc[n] = nc.dram_tensor("sc_" + n, [R // 128, 128, C], BF16, kind="Internal").ap()
    for n in nneed:
        K, N = NATURAL[n][1], NATURAL[n][2]
        sc[n] = nc.dram_tensor("sc_" + n, [K // 128, 128, N], BF16, kind="Internal").ap()
    return sc


def emit_consts(ctx, io, phases):
    nc = ctx.nc
    consts = {}
    ones_bf = ctx.sb("ones_bf", [128, 128], BF16)
    ident_bf = ctx.sb("ident_bf", [128, 128], BF16)
    epscol = ctx.sb("epscol", [128, 1], F32)
    consts["ones_bf"] = ones_bf
    consts["ident_bf"] = ident_bf
    consts["epscol"] = epscol
    scw = ctx.sem("c_cw")
    ctx.vec.memset(epscol[:, :], LNEPS)
    ctx.vec.memset(ident_bf[:, :], 0.0)
    scw.inc(ctx.vec.memset(ones_bf[:, :], 1.0))
    scw.wait(nc.gpsimd)
    scw.inc(nc.gpsimd.affine_select(ident_bf[:, :], ones_bf[:, :], [[-1, 128]],
                                    ALU.is_equal, 0.0, base=0, channel_multiplier=1))
    if "l0attn" in phases:
        tril = ctx.sb("tril_bf", [128, 128], BF16)
        consts["tril"] = tril
        scw.inc(nc.gpsimd.affine_select(tril[:, :], ones_bf[:, :], [[-1, 128]],
                                        ALU.is_ge, 0.0, base=0, channel_multiplier=1))
    if "l1attn" in phases:
        obc = ctx.sb("obc", [128, NS], F32)
        abc = ctx.sb("abc", [128, NA], F32)
        center = ctx.sb("center", [128, 1], F32)
        iota16 = ctx.sb("iota16", [128, 128], mybir.dt.float16)
        consts["obc"], consts["abc"], consts["center"] = obc, abc, center
        consts["iota16"] = iota16
        scw.inc(nc.gpsimd.iota(iota16[:, :], [[1, 128]], base=0, channel_multiplier=0,
                               allow_small_or_imprecise_dtypes=True))
        scw.inc(nc.gpsimd.iota(obc[:, :], [[1, NS]], base=-16, channel_multiplier=0,
                               allow_small_or_imprecise_dtypes=True))
        scw.inc(nc.gpsimd.iota(abc[:, :], [[1, NA]], base=0, channel_multiplier=0,
                               allow_small_or_imprecise_dtypes=True))
        scw.inc(nc.gpsimd.iota(center[:, :], [[0, 1]], base=0, channel_multiplier=1,
                               allow_small_or_imprecise_dtypes=True))
    scw.wait(nc.tensor)
    scw.wait(nc.vector)
    scw.wait(nc.scalar)
    return consts


def emit_params(ctx, io, phases):
    nc = ctx.nc
    prm = {}
    sdl = ctx.sem("prm_dl")

    def col(name, src_ap, n):
        t = ctx.sb("p_" + name, [128, max(1, n // 128)], F32)
        sdl.inc(nc.sync.dma_start(t[:, :], src_ap.rearrange("(c p) -> p c", p=128),
                                  allow_slow_non_contiguous=True), 16)
        prm[name] = t

    def row(name, src_ap, n):
        t = ctx.sb("r_" + name, [1, n], F32)
        sdl.inc(nc.sync.dma_start(t[0:1, :], src_ap.rearrange("(o n) -> o n", o=1)), 16)
        prm[name] = t

    if "l0attn" in phases:
        col("q_ln_g", io["L0_attn_q_ln_g"], 256)
        col("q_ln_b", io["L0_attn_q_ln_b"], 256)
        col("kv_ln_g", io["L0_attn_kv_ln_g"], 256)
        col("kv_ln_b", io["L0_attn_kv_ln_b"], 256)
    if "l1ffn" in phases:
        col("mu_r", io["L1_ffn_mu_r"], D)
        col("mu_k", io["L1_ffn_mu_k"], D)
    if "l1attn" in phases:
        row("wave_b", io["L1_attn_wave_b"], 24)
        row("query_b", io["L1_attn_query_b"], 128)
        row("key_w", io["L1_attn_key_w"], POS)
        col("se1_b", io["L1_attn_se1_b"], 256)
        col("se2_b", io["L1_attn_se2_b"], D)
    for eng in (nc.vector, nc.scalar, nc.tensor, nc.gpsimd):
        sdl.wait(eng)
    return prm


def emit_preprocess(ctx, io, sc, phases, consts):
    nc = ctx.nc
    s_ld = [ctx.sem("pp_ld0"), ctx.sem("pp_ld1")]
    s_cv = ctx.sem("pp_cv")
    s_tp = ctx.sem("pp_tp")
    s_cp = ctx.sem("pp_cp")
    s_st = [ctx.sem("pp_st0"), ctx.sem("pp_st1")]
    stack = ExitStack()
    sload = stack.enter_context(nc.sbuf_tensor("pp_load", [128, 2 * HID], F32))
    sloadi = stack.enter_context(nc.sbuf_tensor("pp_loadi", [128, 2 * HID], I32))
    sbf = stack.enter_context(nc.sbuf_tensor("pp_bf", [128, 2 * HID], BF16))
    sstage = stack.enter_context(nc.sbuf_tensor("pp_stage", [128, 2 * HID], BF16))
    ppt = [stack.enter_context(nc.psum_tensor(f"pp_pt{i}", [128, 128], BF16)) for i in range(4)]
    ident = consts["ident_bf"]

    load_rot = Rot(2)
    pt_rot = Rot(2)

    idx = 0
    for n in [t for t in TRANSPOSED if t in sc]:
        src_name, R, C, is_int = TRANSPOSED[n]
        src, dst = io[src_name], sc[n]
        for rc in range(R // 128):
            sl_i = idx % 2
            b = sl_i * HID
            load_rot.acquire(nc.sync)
            if is_int:
                s_ld[sl_i].inc(nc.sync.dma_start(sloadi[:, b:b + C], src[rc * 128:(rc + 1) * 128, :]), 16)
            else:
                s_ld[sl_i].inc(nc.sync.dma_start(sload[:, b:b + C], src[rc * 128:(rc + 1) * 128, :]), 16)
            s_ld[sl_i].wait(nc.vector)
            s_st[sl_i].wait(nc.vector)
            if is_int:
                s_cv.inc(ctx.vec.tensor_scalar_mul(sbf[:, b:b + C], sloadi[:, b:b + C], R127))
            else:
                s_cv.inc(ctx.vec.tensor_copy(sbf[:, b:b + C], sload[:, b:b + C]))
            load_rot.release(s_cv)
            s_cv.wait(nc.tensor)
            ncc = C // 128
            tmarks = {}
            for cc in range(ncc + 2):
                if cc < ncc:
                    pt = ppt[cc % 4]
                    pt_rot.acquire(nc.tensor)
                    s_tp.inc(nc.tensor.transpose(pt[:, :], sbf[:, b + cc * 128:b + (cc + 1) * 128], ident[:, :]))
                    tmarks[cc] = s_tp.v
                if cc >= 2:
                    cp = cc - 2
                    s_tp.wait(nc.vector, tmarks[cp])
                    s_cp.inc(ctx.vec.tensor_copy(sstage[:, b + cp * 128:b + (cp + 1) * 128], ppt[cp % 4][:, :]))
                    pt_rot.release(s_cp)
            s_cp.wait(nc.sync)
            s_st[sl_i].inc(nc.sync.dma_start(dst[rc, :, :], sstage[:, b:b + C]), 16)
            idx += 1

    for n in [t for t in NATURAL if t in sc]:
        src_name, K, N = NATURAL[n]
        src, dst = io[src_name], sc[n]
        grouped = (n == "woutw")
        for kc in range(K // 128):
            sl_i = idx % 2
            b = sl_i * HID
            load_rot.acquire(nc.sync)
            s_ld[sl_i].inc(nc.sync.dma_start(sload[:, b:b + N], src[kc * 128:(kc + 1) * 128, :]), 16)
            s_ld[sl_i].wait(nc.vector)
            s_st[sl_i].wait(nc.vector)
            s_cv.inc(ctx.vec.tensor_copy(sbf[:, b:b + N], sload[:, b:b + N]))
            load_rot.release(s_cv)
            s_cv.wait(nc.sync)
            if grouped:
                s_st[sl_i].inc(nc.sync.dma_start(
                    dst[:, :, kc * 128:(kc + 1) * 128].rearrange("o p j -> p o j"),
                    sbf[:, b:b + N].rearrange("p (o j) -> p o j", j=128)), 16)
            else:
                s_st[sl_i].inc(nc.sync.dma_start(dst[kc, :, :], sbf[:, b:b + N]), 16)
            idx += 1
    if "xtok" in sc:
        zt = stack.enter_context(nc.sbuf_tensor("pp_zt", [128, D], BF16))
        s_cv.inc(ctx.vec.memset(zt[:, :], 0.0))
        s_cv.wait(nc.sync)
        xt = sc["xtok"]
        s_st[0].inc(nc.sync.dma_start(xt[0:128, :], zt[:, :]), 16)
        s_st[0].inc(nc.sync.dma_start(xt[128:256, :], zt[:, :]), 16)
        s_st[0].inc(nc.sync.dma_start(xt[256:PAD, :], zt[0:16, :]), 16)
        s_st[0].inc(nc.sync.dma_start(xt[T + PAD:T + PAD + 128, :], zt[:, :]), 16)
        s_st[0].inc(nc.sync.dma_start(xt[T + PAD + 128:T + PAD + 256, :], zt[:, :]), 16)
        s_st[0].inc(nc.sync.dma_start(xt[T + PAD + 256:TPAD, :], zt[0:16, :]), 16)
    s_st[0].wait(nc.sync)
    s_st[1].wait(nc.sync)
    stack.close()


def emit_ln(ctx, bufs, tag):
    nc = ctx.nc
    xres, xln = bufs["xres"], bufs["xln"]
    ones = bufs["consts"]["ones_bf"]
    eps = bufs["consts"]["epscol"]
    s_pp = ctx.sem("ln_pp")
    s_mm = ctx.sem("ln_mm")
    s_ev = ctx.sem("ln_ev")
    s_vv = ctx.sem("ln_vv")
    stack = ExitStack()
    xbf = stack.enter_context(nc.sbuf_tensor("ln_xbf" + tag, [128, NC * T], BF16))
    xsq = stack.enter_context(nc.sbuf_tensor("ln_xsq" + tag, [128, NC * T], BF16))
    mub = stack.enter_context(nc.sbuf_tensor("ln_mu" + tag, [128, T], BF16))
    rstdb = stack.enter_context(nc.sbuf_tensor("ln_rstd" + tag, [128, T], BF16))
    muf = stack.enter_context(nc.sbuf_tensor("ln_muf" + tag, [128, 2 * 512], F32))
    tmp = stack.enter_context(nc.sbuf_tensor("ln_tmp" + tag, [128, 2 * 512], F32))
    msq = stack.enter_context(nc.sbuf_tensor("ln_ms" + tag, [128, 2 * 512], F32))
    tmpb = stack.enter_context(nc.sbuf_tensor("ln_tb" + tag, [128, T], BF16))
    ps = stack.enter_context(nc.psum_tensor("ln_ps" + tag, [128, 1024], F32))
    pq = stack.enter_context(nc.psum_tensor("ln_pq" + tag, [128, 1024], F32))
    ppmark = {}
    sqmark = {}
    for st in range(4):
        n0 = st * 512
        for c in range(NC):
            i1 = ctx.act.activation(xbf[:, c * T + n0:c * T + n0 + 512],
                                    xres[:, c * T + n0:c * T + n0 + 512], AF.Copy)
        s_pp.inc(i1)
        ppmark[st] = s_pp.v
    s_pp.wait(nc.vector, ppmark[0])
    for st in range(4):
        n0 = st * 512
        if st > 0:
            s_pp.wait(nc.vector, ppmark[st])
        for c in range(NC):
            sl = slice(c * T + n0, c * T + n0 + 512)
            i1 = ctx.vec.tensor_mul(xsq[:, sl], xbf[:, sl], xbf[:, sl])
        s_vv.inc(i1)
        sqmark[st] = s_vv.v
    emark = {}
    amark = {}
    for st in range(4):
        n0 = st * 512
        c0 = (st % 2) * 512
        p0 = (st % 2) * 512
        if st >= 2:
            s_vv.wait(nc.tensor, amark[st - 2])
        s_pp.wait(nc.tensor, ppmark[st])
        for c in range(NC):
            nc.tensor.matmul(ps[:, c0:c0 + 512], ones[:, :], xbf[:, c * T + n0:c * T + n0 + 512],
                             start=(c == 0), stop=(c == NC - 1))
        s_vv.wait(nc.tensor, sqmark[st])
        for c in range(NC):
            mm = nc.tensor.matmul(pq[:, c0:c0 + 512], ones[:, :], xsq[:, c * T + n0:c * T + n0 + 512],
                                  start=(c == 0), stop=(c == NC - 1))
            if c == NC - 1:
                s_mm.inc(mm)
        s_mm.wait(nc.vector)
        if st >= 2:
            s_ev.wait(nc.vector, emark[st - 2])
        ctx.vec.tensor_scalar_mul(muf[:, p0:p0 + 512], ps[:, c0:c0 + 512], 1.0 / D)
        ctx.vec.tensor_scalar_mul(tmp[:, p0:p0 + 512], pq[:, c0:c0 + 512], 1.0 / D)
        ctx.vec.tensor_mul(msq[:, p0:p0 + 512], muf[:, p0:p0 + 512], muf[:, p0:p0 + 512])
        i1 = ctx.vec.tensor_sub(tmp[:, p0:p0 + 512], tmp[:, p0:p0 + 512], msq[:, p0:p0 + 512])
        s_vv.inc(i1)
        s_vv.wait(nc.scalar)
        ctx.act.activation(mub[:, n0:n0 + 512], muf[:, p0:p0 + 512], AF.Copy)
        ctx.act.activation(tmp[:, p0:p0 + 512], tmp[:, p0:p0 + 512], AF.Ln, bias=eps[:, 0:1])
        i1 = ctx.act.activation(rstdb[:, n0:n0 + 512], tmp[:, p0:p0 + 512], AF.Exp, scale=-0.5)
        s_ev.inc(i1)
        emark[st] = s_ev.v
        s_ev.wait(nc.vector, emark[st])
        for c in range(NC):
            sl = slice(c * T + n0, c * T + n0 + 512)
            ctx.vec.tensor_sub(tmpb[:, n0:n0 + 512], xbf[:, sl], mub[:, n0:n0 + 512])
            i1 = ctx.vec.tensor_mul(xln[:, sl], tmpb[:, n0:n0 + 512], rstdb[:, n0:n0 + 512])
            if c == NC - 1:
                s_vv.inc(i1)
        amark[st] = s_vv.v
    s_vv.wait(nc.tensor)
    s_vv.wait(nc.scalar)
    s_vv.wait(nc.sync)
    s_vv.wait(nc.gpsimd)
    s_ev.wait(nc.tensor)
    s_pp.wait(nc.tensor)
    stack.close()


def emit_l0ffn(ctx, bufs, sc, tag):
    nc = ctx.nc
    xres, xln = bufs["xres"], bufs["xln"]
    s_dl = [ctx.sem("f0_dl0"), ctx.sem("f0_dl1")]
    s_dd = [ctx.sem("f0_dd0"), ctx.sem("f0_dd1")]
    s_mm = ctx.sem("f0_mm")
    s_ev = ctx.sem("f0_ev")
    s_ab = ctx.sem("f0_ab")
    stack = ExitStack()
    wbuf = stack.enter_context(nc.sbuf_tensor("f0_w" + tag, [128, 2 * 2048], BF16))
    wdbuf = stack.enter_context(nc.sbuf_tensor("f0_wd" + tag, [128, 2 * HID], BF16))
    hbuf = stack.enter_context(nc.sbuf_tensor("f0_h" + tag, [128, HC * 1024], BF16))
    sg = stack.enter_context(nc.sbuf_tensor("f0_sg" + tag, [128, 2 * 1024], BF16))
    pg = [stack.enter_context(nc.psum_tensor(f"f0_pg{i}" + tag, [128, 1024], F32)) for i in range(2)]
    pu = [stack.enter_context(nc.psum_tensor(f"f0_pu{i}" + tag, [128, 1024], F32)) for i in range(2)]

    wgT, wuT, wdT = sc["wgT"], sc["wuT"], sc["wdT"]
    w_rot = Rot(2)
    p_rot = Rot(2)
    sg_rot = Rot(2)
    wd_rot = Rot(2)

    for strip in range(2):
        t0 = strip * 1024
        for hc in range(HC):
            slot = hc % 2
            wofs = slot * 2048
            w_rot.acquire(nc.sync)
            s_dl[slot].inc(nc.sync.dma_start(wbuf[:, wofs:wofs + 1024], wgT[hc, :, :]), 16)
            s_dl[slot].inc(nc.sync.dma_start(wbuf[:, wofs + 1024:wofs + 2048], wuT[hc, :, :]), 16)
            s_dl[slot].wait(nc.tensor)
            p_rot.acquire(nc.tensor)
            for n0 in (0, 512):
                for kc in range(NC):
                    nc.tensor.matmul(pg[slot][:, n0:n0 + 512],
                                     wbuf[:, wofs + kc * 128:wofs + (kc + 1) * 128],
                                     xln[:, kc * T + t0 + n0:kc * T + t0 + n0 + 512],
                                     start=(kc == 0), stop=(kc == NC - 1))
            for n0 in (0, 512):
                for kc in range(NC):
                    mm = nc.tensor.matmul(pu[slot][:, n0:n0 + 512],
                                          wbuf[:, wofs + 1024 + kc * 128:wofs + 1024 + (kc + 1) * 128],
                                          xln[:, kc * T + t0 + n0:kc * T + t0 + n0 + 512],
                                          start=(kc == 0), stop=(kc == NC - 1))
                    if n0 == 512 and kc == NC - 1:
                        s_mm.inc(mm)
            w_rot.release(s_mm)
            s_mm.wait(nc.scalar)
            sg_rot.acquire(nc.scalar)
            s_ab.inc(ctx.act.activation(sg[:, slot * 1024:(slot + 1) * 1024], pg[slot][:, :], AF.Silu))
            s_ab.wait(nc.vector)
            s_ev.inc(ctx.vec.tensor_mul(hbuf[:, hc * 1024:(hc + 1) * 1024],
                                          sg[:, slot * 1024:(slot + 1) * 1024], pu[slot][:, :]))
            sg_rot.release(s_ev)
            p_rot.release(s_ev)
        for oc in range(NC):
            slot = oc % 2
            wofs = slot * HID
            wd_rot.acquire(nc.sync)
            s_dd[slot].inc(nc.sync.dma_start(wdbuf[:, wofs:wofs + HID], wdT[oc, :, :]), 16)
            s_dd[slot].wait(nc.tensor)
            s_ev.wait(nc.tensor)
            p_rot.acquire(nc.tensor)
            for n0 in (0, 512):
                for kc in range(HC):
                    mm = nc.tensor.matmul(pg[slot][:, n0:n0 + 512],
                                          wdbuf[:, wofs + kc * 128:wofs + (kc + 1) * 128],
                                          hbuf[:, kc * 1024 + n0:kc * 1024 + n0 + 512],
                                          start=(kc == 0), stop=(kc == HC - 1))
                    if n0 == 512 and kc == HC - 1:
                        s_mm.inc(mm)
            wd_rot.release(s_mm)
            s_mm.wait(nc.vector)
            sl = slice(oc * T + t0, oc * T + t0 + 1024)
            s_ev.inc(ctx.vec.scalar_tensor_tensor(xres[:, sl], pg[slot][:, :], 0.5, xres[:, sl],
                                                    ALU.mult, ALU.add))
            p_rot.release(s_ev)
    s_ev.wait(nc.tensor)
    s_ev.wait(nc.scalar)
    s_ev.wait(nc.sync)
    s_ev.wait(nc.gpsimd)
    stack.close()


def emit_l0attn(ctx, bufs, sc, tag):
    nc = ctx.nc
    xres, xln = bufs["xres"], bufs["xln"]
    prm, consts = bufs["prm"], bufs["consts"]
    ones, tril = consts["ones_bf"], consts["tril"]
    s_dl = [ctx.sem("a0_dl0"), ctx.sem("a0_dl1")]
    s_mm = ctx.sem("a0_mm")
    s_ev = ctx.sem("a0_ev")
    s_ab = ctx.sem("a0_ab")
    stack = ExitStack()
    pA = stack.enter_context(nc.psum_tensor("a0_pA" + tag, [128, 1024], F32))
    pB = stack.enter_context(nc.psum_tensor("a0_pB" + tag, [128, 1024], F32))
    pS = stack.enter_context(nc.psum_tensor("a0_pS" + tag, [128, 1024], F32))
    pO = stack.enter_context(nc.psum_tensor("a0_pO" + tag, [128, 1024], F32))
    cqln = stack.enter_context(nc.sbuf_tensor("a0_cq" + tag, [128, 2 * T], BF16))
    mcol = stack.enter_context(nc.sbuf_tensor("a0_mc" + tag, [128, 8], F32))
    sqt = stack.enter_context(nc.sbuf_tensor("a0_sq" + tag, [128, 4 * 512], BF16))
    wuq = stack.enter_context(nc.sbuf_tensor("a0_wuq" + tag, [128, 2048], BF16))
    s_dl[1].inc(nc.sync.dma_start(wuq[:, :].rearrange("p (k n) -> p k n", k=NATURAL["wuq"][1] // 128), sc["wuq"].rearrange("k p n -> p k n")), 16)

    def comp_proj(wdown, g_col, b_col, dst_ln, sub, wofs):
        cbf = sub.enter_context(nc.sbuf_tensor(f"a0_cbf{wofs}" + tag, [128, 2 * T], BF16))
        csq = sub.enter_context(nc.sbuf_tensor(f"a0_csq{wofs}" + tag, [128, 2 * T], BF16))
        stat = sub.enter_context(nc.sbuf_tensor(f"a0_stt{wofs}" + tag, [128, 2 * T], F32))
        tmpf = sub.enter_context(nc.sbuf_tensor(f"a0_tf{wofs}" + tag, [128, T], F32))
        mu_t, m2_t = stat[:, 0:T], stat[:, T:2 * T]
        prj_rot = Rot(2)
        for j, (oc, st4) in enumerate([(o, st) for o in range(2) for st in range(4)]):
            n0 = st4 * 512
            ppp = (pA, pB)[j % 2]
            prj_rot.acquire(nc.tensor)
            for kc in range(NC):
                mm = nc.tensor.matmul(ppp[:, 0:512],
                                      wdown[:, kc * 256 + oc * 128:kc * 256 + oc * 128 + 128],
                                      xln[:, kc * T + n0:kc * T + n0 + 512],
                                      start=(kc == 0), stop=(kc == NC - 1))
                if kc == NC - 1:
                    s_mm.inc(mm)
            s_mm.wait(nc.scalar)
            ctx.act.activation(cbf[:, oc * T + n0:oc * T + n0 + 512], ppp[:, 0:512], AF.Copy)
            s_ab.inc(ctx.act.activation(csq[:, oc * T + n0:oc * T + n0 + 512], ppp[:, 0:512], AF.Square))
            prj_rot.release(s_ab)
        s_ab.wait(nc.tensor)
        st_rot = Rot(1)
        for st4 in range(4):
            n0 = st4 * 512
            st_rot.acquire(nc.tensor)
            for oc in range(2):
                nc.tensor.matmul(pS[:, 0:512], ones[:, :], cbf[:, oc * T + n0:oc * T + n0 + 512],
                                 start=(oc == 0), stop=(oc == 1))
            for oc in range(2):
                mm = nc.tensor.matmul(pS[:, 512:1024], ones[:, :], csq[:, oc * T + n0:oc * T + n0 + 512],
                                      start=(oc == 0), stop=(oc == 1))
                if oc == 1:
                    s_mm.inc(mm)
            s_mm.wait(nc.scalar)
            ctx.act.activation(mu_t[:, n0:n0 + 512], pS[:, 0:512], AF.Copy, scale=1.0 / 256)
            s_ab.inc(ctx.act.activation(m2_t[:, n0:n0 + 512], pS[:, 512:1024], AF.Copy, scale=1.0 / 256))
            st_rot.release(s_ab)
        s_ab.wait(nc.vector)
        ctx.vec.tensor_mul(tmpf[:, :], mu_t, mu_t)
        s_ev.inc(ctx.vec.tensor_sub(tmpf[:, :], m2_t, tmpf[:, :]))
        s_ev.wait(nc.scalar)
        ctx.act.activation(tmpf[:, :], tmpf[:, :], AF.Ln, bias=consts["epscol"][:, 0:1])
        s_ab.inc(ctx.act.activation(tmpf[:, :], tmpf[:, :], AF.Exp, scale=-0.5))
        s_ab.wait(nc.vector)
        for oc in range(2):
            ctx.vec.tensor_sub(m2_t, cbf[:, oc * T:(oc + 1) * T], mu_t)
            ctx.vec.tensor_mul(m2_t, m2_t, tmpf[:, :])
            i1 = ctx.vec.tensor_scalar(dst_ln[:, oc * T:(oc + 1) * T], m2_t,
                                         g_col[:, oc:oc + 1], b_col[:, oc:oc + 1], ALU.mult, ALU.add)
            if oc == 1:
                s_ev.inc(i1)
        s_ev.wait(nc.tensor)
        s_ev.wait(nc.scalar)
        s_ev.wait(nc.sync)

    kv_stack = ExitStack()
    ckvln = kv_stack.enter_context(nc.sbuf_tensor("a0_ckv" + tag, [128, 2 * T], BF16, side="right"))
    with ExitStack() as sub:
        wdq = sub.enter_context(nc.sbuf_tensor("a0_wdq" + tag, [128, 2048], BF16))
        s_dl[0].inc(nc.sync.dma_start(wdq[:, :].rearrange("p (k n) -> p k n", k=NATURAL["wdq"][1] // 128), sc["wdq"].rearrange("k p n -> p k n")), 16)
        s_dl[0].wait(nc.tensor)
        comp_proj(wdq, prm["q_ln_g"], prm["q_ln_b"], cqln, sub, 0)
    with ExitStack() as sub:
        wdkv = sub.enter_context(nc.sbuf_tensor("a0_wdkv" + tag, [128, 2048], BF16))
        s_dl[0].inc(nc.sync.dma_start(wdkv[:, :].rearrange("p (k n) -> p k n", k=NATURAL["wdkv"][1] // 128), sc["wdkv"].rearrange("k p n -> p k n")), 16)
        s_dl[0].wait(nc.tensor)
        comp_proj(wdkv, prm["kv_ln_g"], prm["kv_ln_b"], ckvln, sub, 1)

    kbf = stack.enter_context(nc.sbuf_tensor("a0_k" + tag, [128, NC * T], BF16))
    vaug = stack.enter_context(nc.sbuf_tensor("a0_v" + tag, [128, 16 * 1040], BF16))
    wukv = kv_stack.enter_context(nc.sbuf_tensor("a0_wukv" + tag, [128, 4096], BF16, side="right"))
    s_dl[0].inc(nc.sync.dma_start(wukv[:, :].rearrange("p (k n) -> p k n", k=NATURAL["wukv"][1] // 128), sc["wukv"].rearrange("k p n -> p k n")), 16)
    s_dl[0].wait(nc.tensor)

    k_rot = Rot(2)
    for j, (oc, st4) in enumerate([(o, st) for o in range(NC) for st in range(4)]):
        n0 = st4 * 512
        ppp = (pA, pB)[j % 2]
        k_rot.acquire(nc.tensor)
        for kc in range(2):
            mm = nc.tensor.matmul(ppp[:, 0:512],
                                  wukv[:, kc * 2048 + oc * 128:kc * 2048 + oc * 128 + 128],
                                  ckvln[:, kc * T + n0:kc * T + n0 + 512],
                                  start=(kc == 0), stop=(kc == 1))
            if kc == 1:
                s_mm.inc(mm)
        s_mm.wait(nc.scalar)
        s_ab.inc(ctx.act.activation(kbf[:, oc * T + n0:oc * T + n0 + 512], ppp[:, 0:512], AF.Copy))
        k_rot.release(s_ab)
    s_ab.wait(nc.tensor)

    def sumsq_max(src, dstcol, tmpcol):
        sq_rot = Rot(4)
        for st4 in range(4):
            n0 = st4 * 512
            for oc in range(NC):
                sq_rot.acquire(nc.scalar)
                s_ab.inc(ctx.act.activation(sqt[:, (oc % 4) * 512:(oc % 4) * 512 + 512],
                                              src[:, oc * T + n0:oc * T + n0 + 512], AF.Square))
                s_ab.wait(nc.tensor)
                mm = nc.tensor.matmul(pS[:, 0:512], ones[:, :], sqt[:, (oc % 4) * 512:(oc % 4) * 512 + 512],
                                      start=(oc == 0), stop=(oc == NC - 1))
                if oc == NC - 1:
                    s_mm.inc(mm)
                sq_rot.release(s_mm)
            s_mm.wait(nc.vector)
            if st4 == 0:
                s_ev.inc(ctx.vec.tensor_reduce(dstcol, pS[:, 0:512], AXX, ALU.max))
            else:
                ctx.vec.tensor_reduce(tmpcol, pS[:, 0:512], AXX, ALU.max)
                s_ev.inc(ctx.vec.tensor_max(dstcol, dstcol, tmpcol))
            s_ev.wait(nc.tensor)
    sumsq_max(kbf, mcol[:, 0:1], mcol[:, 4:5])

    vview = vaug[:, :].rearrange("p (k h e) -> p k h e", k=16, h=16)
    s_vo = ctx.sem("a0_vo")
    s_vo.inc(ctx.vec.memset(vview[:, :, :, 64:65], 1.0))
    v_rot = Rot(2)
    for j in range(32):
        kt, half = j // 2, j % 2
        n0 = half * 512
        ppp = (pA, pB)[j % 2]
        v_rot.acquire(nc.tensor)
        for kc in range(2):
            mm = nc.tensor.matmul(ppp[:, 0:512],
                                  ckvln[:, kc * T + kt * 128:kc * T + kt * 128 + 128],
                                  wukv[:, kc * 2048 + 1024 + n0:kc * 2048 + 1024 + n0 + 512],
                                  start=(kc == 0), stop=(kc == 1))
            if kc == 1:
                s_mm.inc(mm)
        s_mm.wait(nc.vector)
        s_vo.wait(nc.vector)
        dst = vview[:, kt, half * 8:half * 8 + 8, 0:64]
        src = ppp[:, 0:512].rearrange("p (h d) -> p h d", d=64)
        s_ev.inc(ctx.vec.tensor_copy(dst, src))
        v_rot.release(s_ev)
    s_ev.wait(nc.tensor)
    s_ev.wait(nc.scalar)
    s_ev.wait(nc.sync)
    kv_stack.close()

    qst = stack.enter_context(nc.sbuf_tensor("a0_qs" + tag, [128, 8 * 512], BF16))
    ptb = stack.enter_context(nc.sbuf_tensor("a0_pt" + tag, [128, 4 * 512], BF16))
    amid = stack.enter_context(nc.sbuf_tensor("a0_am" + tag, [128, 8 * 512], BF16))
    rden = stack.enter_context(nc.sbuf_tensor("a0_rd" + tag, [128, 2 * 512], F32))
    wostream = stack.enter_context(nc.sbuf_tensor("a0_wo" + tag, [128, 2 * 1024], BF16))

    woT = sc["woT"]
    wo_rot = Rot(2)
    qp_rot = Rot(2)
    pt_rot = Rot(2)
    sc_rot = Rot(2)
    po_rot = Rot(2)
    rd_rot = Rot(2)
    for st4 in range(4):
        q0 = st4 * 512
        ktmax = (q0 + 511) // 128
        for oc in range(NC):
            ppp = (pA, pB)[oc % 2]
            qp_rot.acquire(nc.tensor)
            for kc in range(2):
                mm = nc.tensor.matmul(ppp[:, 0:512],
                                      wuq[:, kc * 1024 + oc * 128:kc * 1024 + oc * 128 + 128],
                                      cqln[:, kc * T + q0:kc * T + q0 + 512],
                                      start=(kc == 0), stop=(kc == 1))
                if kc == 1:
                    s_mm.inc(mm)
            s_mm.wait(nc.scalar)
            s_ab.inc(ctx.act.activation(qst[:, oc * 512:(oc + 1) * 512], ppp[:, 0:512], AF.Copy))
            qp_rot.release(s_ab)
        s_ab.wait(nc.tensor)
        sq_rot = Rot(4)
        for oc in range(NC):
            sq_rot.acquire(nc.scalar)
            s_ab.inc(ctx.act.activation(sqt[:, (oc % 4) * 512:(oc % 4) * 512 + 512],
                                          qst[:, oc * 512:(oc + 1) * 512], AF.Square))
            s_ab.wait(nc.tensor)
            mm = nc.tensor.matmul(pS[:, 0:512], ones[:, :], sqt[:, (oc % 4) * 512:(oc % 4) * 512 + 512],
                                  start=(oc == 0), stop=(oc == NC - 1))
            if oc == NC - 1:
                s_mm.inc(mm)
            sq_rot.release(s_mm)
        s_mm.wait(nc.vector)
        s_ev.inc(ctx.vec.tensor_reduce(mcol[:, 1:2], pS[:, 0:512], AXX, ALU.max))
        s_ev.wait(nc.tensor)
        s_ev.wait(nc.vector)
        s_ab.wait(nc.vector)
        s_ev.inc(ctx.vec.tensor_mul(mcol[:, 2:3], mcol[:, 0:1], mcol[:, 1:2]))
        s_ev.wait(nc.scalar)
        s_ab.inc(ctx.act.activation(mcol[:, 2:3], mcol[:, 2:3], AF.Sqrt, scale=1.0 / 64))
        s_ab.wait(nc.vector)
        s_ev.inc(ctx.vec.tensor_scalar_mul(mcol[:, 3:4], mcol[:, 2:3], -1.0))
        s_ev.wait(nc.scalar)
        s_ev.wait(nc.tensor)
        denmark = {}
        po_of = {}

        def normalize(hh):
            pbb = (hh % 2) * 64
            hcc = hh // 2
            uu = st4 * 16 + hh
            poo, pdd = po_of[hh]
            nc.scalar.wait_ge(s_mm.h, denmark[hh])
            s_ev.wait(nc.scalar)
            rds = rden[pbb:pbb + 64, (uu % 2) * 512:(uu % 2) * 512 + 512]
            ctx.act.activation(rds, pdd, AF.Ln)
            s_ab.inc(ctx.act.activation(rds, rds, AF.Exp, scale=-1.0))
            s_ab.wait(nc.vector)
            i1 = ctx.vec.tensor_mul(amid[pbb:pbb + 64, hcc * 512:hcc * 512 + 512], poo, rds)
            s_ev.inc(i1)
            po_rot.release(s_ev)

        for h in range(16):
            pb = (h % 2) * 64
            hc = h // 2
            u = st4 * 16 + h
            po = pO[pb:pb + 64, (u % 2) * 512:(u % 2) * 512 + 512]
            pd = pA[pb:pb + 64, (u % 2) * 512:(u % 2) * 512 + 512]
            po_of[h] = (po, pd)
            po_rot.acquire(nc.tensor)

            def emit_qk(p):
                kt2 = p * 2
                npair = min(2, ktmax + 1 - kt2)
                psc = (pS, pB)[p % 2]
                sc_rot.acquire(nc.tensor)
                for j in range(npair):
                    kt = kt2 + j
                    mm = nc.tensor.matmul(psc[:, j * 512:j * 512 + 512],
                                          kbf[pb:pb + 64, hc * T + kt * 128:hc * T + kt * 128 + 128],
                                          qst[pb:pb + 64, hc * 512:hc * 512 + 512],
                                          start=True, stop=True)
                    if j == npair - 1:
                        s_mm.inc(mm)
                s_mm.wait(nc.scalar)
                pt_rot.acquire(nc.scalar)
                pslot = ptb[:, (p % 2) * 1024:(p % 2) * 1024 + npair * 512]
                s_ab.inc(ctx.act.activation(pslot, psc[:, 0:npair * 512], AF.Exp,
                                            scale=0.125, bias=mcol[:, 3:4]))
                sc_rot.release(s_ab)
                need_mask = any(q0 - (kt2 + j) * 128 < 128 for j in range(npair))
                if need_mask:
                    s_ab.wait(nc.vector)
                    last_i = None
                    for j in range(npair):
                        kt = kt2 + j
                        d = q0 - kt * 128
                        if d >= 128:
                            continue
                        sl = ptb[:, (p % 2) * 1024 + j * 512:(p % 2) * 1024 + (j + 1) * 512]
                        if d < 0:
                            ctx.vec.memset(sl[:, 0:-d], 0.0)
                        mw0 = max(0, -d)
                        last_i = ctx.vec.tensor_mul(sl[:, mw0:mw0 + 128], sl[:, mw0:mw0 + 128], tril[:, :])
                    s_ev.inc(last_i)
                return need_mask

            def emit_pv(p, masked):
                kt2 = p * 2
                npair = min(2, ktmax + 1 - kt2)
                if masked:
                    s_ev.wait(nc.tensor)
                else:
                    s_ab.wait(nc.tensor, abmark[p])
                for j in range(npair):
                    kt = kt2 + j
                    sl = ptb[:, (p % 2) * 1024 + j * 512:(p % 2) * 1024 + (j + 1) * 512]
                    nc.tensor.matmul(po, vaug[:, kt * 1040 + h * 65:kt * 1040 + h * 65 + 64],
                                     sl, start=(kt == 0), stop=(kt == ktmax),
                                     tile_position=(0, pb), skip_group_check=True)
                    mm2 = nc.tensor.matmul(pd, ones[:, 0:64],
                                           sl, start=(kt == 0), stop=(kt == ktmax),
                                           tile_position=(0, pb), skip_group_check=True)
                    if j == npair - 1:
                        s_mm.inc(mm2)
                pt_rot.release(s_mm)

            npairs = (ktmax + 2) // 2
            maskedp = {}
            abmark = {}
            for p in range(npairs):
                maskedp[p] = emit_qk(p)
                abmark[p] = s_ab.v
                if p >= 1:
                    emit_pv(p - 1, maskedp[p - 1])
            emit_pv(npairs - 1, maskedp[npairs - 1])
            denmark[h] = s_mm.v
            if h >= 1:
                normalize(h - 1)
        normalize(15)
        s_ev.wait(nc.tensor)
        for oc in range(NC):
            slot = oc % 2
            wo_rot.acquire(nc.sync)
            s_dl[slot].inc(nc.sync.dma_start(wostream[:, slot * 1024:slot * 1024 + 1024], woT[oc, :, :]), 16)
            s_dl[slot].wait(nc.tensor)
            ppp = (pA, pB)[slot]
            qp_rot.acquire(nc.tensor)
            for kc in range(NC):
                mm = nc.tensor.matmul(ppp[:, 512:1024],
                                      wostream[:, slot * 1024 + kc * 128:slot * 1024 + kc * 128 + 128],
                                      amid[:, kc * 512:(kc + 1) * 512],
                                      start=(kc == 0), stop=(kc == NC - 1))
                if kc == NC - 1:
                    s_mm.inc(mm)
            wo_rot.release(s_mm)
            s_mm.wait(nc.vector)
            sl = slice(oc * T + q0, oc * T + q0 + 512)
            s_ev.inc(ctx.vec.tensor_add(xres[:, sl], xres[:, sl], ppp[:, 512:1024]))
            qp_rot.release(s_ev)
        s_ev.wait(nc.tensor)
        s_ev.wait(nc.scalar)
    s_ev.wait(nc.tensor)
    s_ev.wait(nc.scalar)
    s_ev.wait(nc.sync)
    s_ev.wait(nc.gpsimd)
    stack.close()


def emit_l1ffn(ctx, bufs, sc, tag):
    nc = ctx.nc
    xres, xln = bufs["xres"], bufs["xln"]
    mu_r, mu_k = bufs["prm"]["mu_r"], bufs["prm"]["mu_k"]
    s_dl = [ctx.sem("f1_dl0"), ctx.sem("f1_dl1")]
    s_mm = ctx.sem("f1_mm")
    s_ev = ctx.sem("f1_ev")
    s_ab = ctx.sem("f1_ab")
    s_mx = ctx.sem("f1_mx")
    stack = ExitStack()
    TS = 1024
    wbuf = stack.enter_context(nc.sbuf_tensor("f1_w" + tag, [128, 2 * 1024], BF16))
    xr = stack.enter_context(nc.sbuf_tensor("f1_xr" + tag, [128, NC * TS], BF16))
    xk = stack.enter_context(nc.sbuf_tensor("f1_xk" + tag, [128, NC * TS], BF16))
    rr = stack.enter_context(nc.sbuf_tensor("f1_r" + tag, [128, NC * TS], BF16))
    ka = stack.enter_context(nc.sbuf_tensor("f1_ka" + tag, [128, NC * TS], BF16))
    tv = stack.enter_context(nc.sbuf_tensor("f1_tv" + tag, [128, 1024], F32))
    pp = [stack.enter_context(nc.psum_tensor(f"f1_pp{i}" + tag, [128, 1024], F32)) for i in range(2)]
    wrT, wkT, wvT = sc["wrT"], sc["wkT"], sc["wvT"]
    w_rot = Rot(2)
    p_rot = Rot(2)

    for strip in range(2):
        t0 = strip * TS
        for c in range(NC):
            slc = slice(c * T + t0, c * T + t0 + TS)
            for dst, mu in ((xr, mu_r), (xk, mu_k)):
                dsl = dst[:, c * TS:(c + 1) * TS]
                if t0 == 0:
                    ctx.vec.tensor_scalar_mul(dsl[:, 0:1], xln[:, c * T:c * T + 1], mu[:, c:c + 1])
                    prev = xln[:, c * T:c * T + TS - 1]
                    cur = xln[:, c * T + 1:c * T + TS]
                    out = dsl[:, 1:TS]
                else:
                    prev = xln[:, c * T + t0 - 1:c * T + t0 + TS - 1]
                    cur = xln[:, slc]
                    out = dsl[:, :]
                ctx.vec.scalar_tensor_tensor(tv[:, :out.shape[-1]], prev, mu[:, c:c + 1], prev,
                                               ALU.mult, ALU.subtract_rev if False else ALU.subtract)
                i1 = ctx.vec.scalar_tensor_tensor(out, cur, mu[:, c:c + 1], tv[:, :out.shape[-1]],
                                                    ALU.mult, ALU.subtract)
                if c == NC - 1 and dst is xk:
                    s_mx.inc(i1)
        s_mx.wait(nc.tensor)
        for (wT, src, dst, kind) in ((wrT, xr, rr, "r"), (wkT, xk, ka, "k")):
            for oc in range(NC):
                slot = oc % 2
                wofs = slot * 1024
                w_rot.acquire(nc.sync)
                s_dl[slot].inc(nc.sync.dma_start(wbuf[:, wofs:wofs + 1024], wT[oc, :, :]), 16)
                s_dl[slot].wait(nc.tensor)
                p_rot.acquire(nc.tensor)
                for n0 in (0, 512):
                    for kc in range(NC):
                        mm = nc.tensor.matmul(pp[slot][:, n0:n0 + 512],
                                              wbuf[:, wofs + kc * 128:wofs + (kc + 1) * 128],
                                              src[:, kc * TS + n0:kc * TS + n0 + 512],
                                              start=(kc == 0), stop=(kc == NC - 1))
                        if n0 == 512 and kc == NC - 1:
                            s_mm.inc(mm)
                w_rot.release(s_mm)
                osl = dst[:, oc * TS:(oc + 1) * TS]
                if kind == "r":
                    s_mm.wait(nc.scalar)
                    s_ab.inc(ctx.act.activation(osl, pp[slot][:, :], AF.Sigmoid))
                    p_rot.release(s_ab)
                else:
                    s_mm.wait(nc.vector)
                    s_ab.wait(nc.vector)
                    s_ev.inc(ctx.vec.tensor_scalar_max(tv[:, :], pp[slot][:, :], 0.0))
                    p_rot.release(s_ev)
                    s_ev.wait(nc.scalar)
                    s_ab.inc(ctx.act.activation(osl, tv[:, :], AF.Square))
        s_ab.wait(nc.tensor)
        for oc in range(NC):
            slot = oc % 2
            wofs = slot * 1024
            w_rot.acquire(nc.sync)
            s_dl[slot].inc(nc.sync.dma_start(wbuf[:, wofs:wofs + 1024], wvT[oc, :, :]), 16)
            s_dl[slot].wait(nc.tensor)
            p_rot.acquire(nc.tensor)
            for n0 in (0, 512):
                for kc in range(NC):
                    mm = nc.tensor.matmul(pp[slot][:, n0:n0 + 512],
                                          wbuf[:, wofs + kc * 128:wofs + (kc + 1) * 128],
                                          ka[:, kc * TS + n0:kc * TS + n0 + 512],
                                          start=(kc == 0), stop=(kc == NC - 1))
                    if n0 == 512 and kc == NC - 1:
                        s_mm.inc(mm)
            w_rot.release(s_mm)
            s_mm.wait(nc.vector)
            if oc == 0:
                s_ab.wait(nc.vector)
            sl = slice(oc * T + t0, oc * T + t0 + TS)
            ctx.vec.tensor_mul(tv[:, :], rr[:, oc * TS:(oc + 1) * TS], pp[slot][:, :])
            s_ev.inc(ctx.vec.scalar_tensor_tensor(xres[:, sl], tv[:, :], 0.5, xres[:, sl],
                                                    ALU.mult, ALU.add))
            p_rot.release(s_ev)
    s_ev.wait(nc.tensor)
    s_ev.wait(nc.scalar)
    s_ev.wait(nc.sync)
    s_ev.wait(nc.gpsimd)
    stack.close()


def emit_l1attn(ctx, bufs, sc, tag):
    nc = ctx.nc
    xres, xln = bufs["xres"], bufs["xln"]
    prm, consts = bufs["prm"], bufs["consts"]
    ones = consts["ones_bf"]
    ident = consts["ident_bf"]
    obc, abc, center = consts["obc"], consts["abc"], consts["center"]
    xtok = sc["xtok"]
    s_dl = [ctx.sem("a1_dl0"), ctx.sem("a1_dl1")]
    s_mm = ctx.sem("a1_mm")
    s_ev = ctx.sem("a1_ev")
    s_ab = ctx.sem("a1_ab")
    NSLOT = L1_NSLOT
    s_gd = [ctx.sem(f"a1_gd{i}") for i in range(NSLOT)]
    s_xt = [ctx.sem(f"a1_xt{i}") for i in range(2)]
    s_gp = ctx.sem("a1_gp")
    stack = ExitStack()
    pA = stack.enter_context(nc.psum_tensor("a1_pA" + tag, [128, 1024], F32))
    pB = stack.enter_context(nc.psum_tensor("a1_pB" + tag, [128, 1024], F32))
    pT = [stack.enter_context(nc.psum_tensor(f"a1_pT{i}" + tag, [128, 128], BF16)) for i in range(2)]
    amid = stack.enter_context(nc.sbuf_tensor("a1_am" + tag, [128, NC * T], BF16))
    wqt = stack.enter_context(nc.sbuf_tensor("a1_wq" + tag, [128, 16 * 152], BF16))
    fdd = stack.enter_context(nc.sbuf_tensor("a1_fd" + tag, [128, 16 * 24], F32))
    fav = stack.enter_context(nc.sbuf_tensor("a1_fa" + tag, [128, 16 * 2], F32))
    karg = stack.enter_context(nc.sbuf_tensor("a1_ka" + tag, [128, 2176], BF16))
    scr = stack.enter_context(nc.sbuf_tensor("a1_sc" + tag, [128, 264 * 6], F32))
    sml = stack.enter_context(nc.sbuf_tensor("a1_sm" + tag, [128, 64], F32))
    la = stack.enter_context(nc.sbuf_tensor("a1_la" + tag, [128, 2 * 136], F32))
    idx32 = stack.enter_context(nc.sbuf_tensor("a1_ix" + tag, [128, 4 * NS], I32))
    vg = stack.enter_context(nc.sbuf_tensor("a1_vg" + tag, [128, L1_NSLOT * 1024], BF16))
    acc = stack.enter_context(nc.sbuf_tensor("a1_ac" + tag, [128, 1024], F32))
    xts = stack.enter_context(nc.sbuf_tensor("a1_xs" + tag, [128, 2 * 1024], BF16))
    wsml = stack.enter_context(nc.sbuf_tensor("a1_ws" + tag, [128, 192 + 1024 + 2048 + 2048], BF16))
    O_WAVE, O_QUERY, O_SE1, O_SE2 = 0, 192, 192 + 1024, 192 + 1024 + 2048
    wostream = stack.enter_context(nc.sbuf_tensor("a1_wo" + tag, [128, 2 * 1024], BF16))
    brow = stack.enter_context(nc.sbuf_tensor("a1_br" + tag, [1, 152], BF16))
    kcb = stack.enter_context(nc.sbuf_tensor("a1_kc" + tag, [128, 272], F32))
    kcr = stack.enter_context(nc.sbuf_tensor("a1_kr" + tag, [1, 272], F32))

    s_dl[0].inc(nc.sync.dma_start(wsml[:, O_WAVE:O_WAVE + 192].rearrange("p (k n) -> p k n", k=8),
                                  sc["wwave"].rearrange("k p n -> p k n")), 16)
    s_dl[0].inc(nc.sync.dma_start(wsml[:, O_QUERY:O_QUERY + 1024].rearrange("p (k n) -> p k n", k=8),
                                  sc["wquery"].rearrange("k p n -> p k n")), 16)
    s_dl[0].inc(nc.sync.dma_start(wsml[:, O_SE1:O_SE1 + 2048].rearrange("p (k n) -> p k n", k=8),
                                  sc["wse1"].rearrange("k p n -> p k n")), 16)
    s_dl[0].inc(nc.sync.dma_start(wsml[:, O_SE2:O_SE2 + 2048].rearrange("p (k n) -> p k n", k=2),
                                  sc["wse2"].rearrange("k p n -> p k n")), 16)
    s_dl[0].wait(nc.vector)
    s_dl[0].wait(nc.tensor)
    ctx.vec.tensor_copy(brow[0:1, 0:24], prm["wave_b"][0:1, :])
    s_ev.inc(ctx.vec.tensor_copy(brow[0:1, 24:152], prm["query_b"][0:1, :]))
    ctx.vec.tensor_tensor(kcr[0:1, :],
                            abc[0:1, :].rearrange("p (a o) -> p a o", o=1).broadcast_to([1, NA, POS]),
                            prm["key_w"][0:1, :].rearrange("p (o d) -> p o d", o=1).broadcast_to([1, NA, POS]),
                            ALU.mult)
    s_ev.inc(ctx.vec.tensor_copy(xts[0:1, 0:272], kcr[0:1, :]))
    s_ev.wait(nc.tensor)
    mm = nc.tensor.matmul(pA[:, 0:272], ones[0:1, :], xts[0:1, 0:272],
                          start=True, stop=True)
    s_mm.inc(mm)
    s_mm.wait(nc.vector)
    s_ev.inc(ctx.vec.tensor_copy(kcb[:, :], pA[:, 0:272]))
    s_ev.wait(nc.tensor)

    xt_rot = Rot(2)
    ptrot = Rot(2)
    for tc in range(16):
        sl = tc % 2
        xt_rot.acquire(nc.vector)
        s_xt[sl].wait(nc.vector)
        for c in range(NC):
            pt = pT[c % 2]
            ptrot.acquire(nc.tensor)
            s_mm.inc(nc.tensor.transpose(pt[:, :], xln[:, c * T + tc * 128:c * T + tc * 128 + 128], ident[:, :]))
            s_mm.wait(nc.vector)
            i1 = ctx.vec.tensor_copy(xts[:, sl * 1024 + c * 128:sl * 1024 + c * 128 + 128], pt[:, :])
            s_ev.inc(i1)
            ptrot.release(s_ev)
        s_ev.wait(nc.sync)
        s_xt[sl].inc(nc.sync.dma_start(xtok[PAD + tc * 128:PAD + tc * 128 + 128, :], xts[:, sl * 1024:sl * 1024 + 1024]), 16)
        xt_rot.release(s_xt[sl])
    if L1_LIMIT < 2:
        s_xt[0].wait(nc.sync)
        s_xt[1].wait(nc.sync)
        s_ev.wait(nc.tensor); s_ev.wait(nc.scalar); s_ev.wait(nc.sync); s_ev.wait(nc.gpsimd)
        stack.close()
        return
    s_ev.wait(nc.tensor)
    wq_rot = Rot(2)
    for tc in range(16):
        ppp = (pA, pB)[tc % 2]
        wq_rot.acquire(nc.tensor)
        for kc in range(NC):
            nc.tensor.matmul(ppp[:, 0:24], xln[:, kc * T + tc * 128:kc * T + tc * 128 + 128],
                             wsml[:, O_WAVE + kc * 24:O_WAVE + kc * 24 + 24],
                             start=(kc == 0), stop=False, skip_group_check=True)
            nc.tensor.matmul(ppp[:, 24:152], xln[:, kc * T + tc * 128:kc * T + tc * 128 + 128],
                             wsml[:, O_QUERY + kc * 128:O_QUERY + kc * 128 + 128],
                             start=False, stop=False, skip_group_check=True)
        mm = nc.tensor.matmul(ppp[:, 0:152], ones[0:1, :], brow[0:1, :],
                              start=False, stop=True, skip_group_check=True)
        s_mm.inc(mm)
        s_mm.wait(nc.scalar)
        s_ab.inc(ctx.act.activation(wqt[:, tc * 152:(tc + 1) * 152], ppp[:, 0:152], AF.Silu))
        wq_rot.release(s_ab)
    if L1_LIMIT < 3:
        s_ab.wait(nc.tensor); s_ab.wait(nc.vector); s_ab.wait(nc.sync); s_ab.wait(nc.gpsimd)
        stack.close()
        return
    for tc in range(16):
        w0 = tc * 152
        f0 = tc * 24
        ctx.act.activation(fdd[:, f0:f0 + 24].rearrange("p (g e) -> p g e", g=3)[:, 0:3:2, :],
                             wqt[:, w0:w0 + 24].rearrange("p (g e) -> p g e", g=3)[:, 0:3:2, :],
                             AF.Sigmoid)
        i1 = ctx.act.activation(fdd[:, f0 + 8:f0 + 16], wqt[:, w0 + 8:w0 + 16], AF.Tanh)
        if tc == 15:
            s_ab.inc(i1)
    s_ab.wait(nc.vector)
    fview = fdd[:, :].rearrange("p (t g e) -> p t g e", t=16, g=3)
    ctx.vec.tensor_scalar(fview[:, :, 0, :], fview[:, :, 0, :], 15.0, 1.0, ALU.mult, ALU.add)
    ctx.vec.tensor_scalar_mul(fview[:, :, 1, :], fview[:, :, 1, :], 16.0)
    ctx.vec.tensor_scalar(fview[:, :, 2, :], fview[:, :, 2, :], 9.5, 0.5, ALU.mult, ALU.add)
    ctx.vec.tensor_reduce(fav[:, 0:16], fview[:, :, 0, :], AXX, ALU.add)
    ctx.vec.tensor_reduce(fav[:, 16:32], fview[:, :, 1, :], AXX, ALU.add)
    s_ev.inc(ctx.vec.tensor_scalar_mul(fav[:, 0:32], fav[:, 0:32], 0.125))
    s_ev.wait(nc.gpsimd)
    s_xt[0].wait(nc.gpsimd)
    s_xt[1].wait(nc.gpsimd)

    if L1_LIMIT < 4:
        s_ev.wait(nc.tensor); s_ev.wait(nc.scalar); s_ev.wait(nc.sync); s_ev.wait(nc.gpsimd)
        stack.close()
        return
    LOOK = 4
    NMACG = 1
    wwall = stack.enter_context(nc.sbuf_tensor("a1_ww" + tag, [128, LOOK * 264], F32))
    s_cblk = ctx.sem("a1_cb")
    s_mg = ctx.sem("a1_mg")
    s_tp2 = ctx.sem("a1_t2")
    cmark = []
    gmark = {}
    LG, ENV, WE, T33 = 0, 264, 528, 792

    def stage_c_gen(tc):
        f0 = tc * 24
        ws = (tc % LOOK) * 264
        ww = wwall[:, ws:ws + 264]
        ctx.vec.tensor_scalar_add(sml[:, 48:49], center[:, 0:1], float(tc * 128))
        ctx.vec.tensor_add(sml[:, 48:49], sml[:, 48:49], fav[:, 16 + tc:17 + tc])
        ctx.vec.scalar_tensor_tensor(scr[:, T33:T33 + NS], obc[:, :], fav[:, tc:tc + 1],
                                     sml[:, 48:49].broadcast_to([128, NS]), ALU.mult, ALU.add)
        sp = scr[:, T33:T33 + NS]
        yield
        ctx.vec.tensor_single_scalar(scr[:, T33 + NS:T33 + 2 * NS], sp, 0.0, ALU.is_ge)
        ctx.vec.tensor_single_scalar(scr[:, T33 + 2 * NS:T33 + 3 * NS], sp, float(T), ALU.is_lt)
        vm = scr[:, T33 + NS:T33 + 2 * NS]
        ctx.vec.tensor_mul(vm, vm, scr[:, T33 + 2 * NS:T33 + 3 * NS])
        yield
        fl = scr[:, T33 + 3 * NS:T33 + 4 * NS]
        ctx.vec.tensor_scalar_add(fl, sp, MAGIC)
        ctx.vec.tensor_scalar_add(fl, fl, -MAGIC)
        gt = scr[:, T33 + 4 * NS:T33 + 5 * NS]
        ctx.vec.tensor_tensor(gt, fl, sp, ALU.is_gt)
        ctx.vec.tensor_sub(fl, fl, gt)
        ctx.vec.tensor_scalar_add(fl, fl, float(PAD))
        yield
        ctx.vec.tensor_tensor(
            karg[:, :].rearrange("p (h ad) -> p h ad", h=H1),
            kcb[:, :].rearrange("p (o ad) -> p o ad", o=1).broadcast_to([128, H1, 272]),
            fdd[:, f0:f0 + 8].rearrange("p (h o) -> p h o", o=1).broadcast_to([128, H1, 272]),
            ALU.mult)
        s_ev.inc(ctx.vec.engine_nop())
        yield
        s_ev.wait(nc.scalar)
        s_ab.inc(ctx.act.activation(karg[:, :], karg[:, :], AF.Silu))
        yield
        s_ab.wait(nc.vector)
        for h in range(H1):
            q = wqt[:, tc * 152 + 24 + h * 16:tc * 152 + 24 + (h + 1) * 16]
            ctx.vec.tensor_tensor(karg[:, h * 272:(h + 1) * 272],
                                  karg[:, h * 272:(h + 1) * 272].rearrange("p (a d) -> p a d", d=POS),
                                  q.rearrange("p (o d) -> p o d", o=1).broadcast_to([128, NA, POS]),
                                  ALU.mult)
            if h == 3:
                yield
        yield
        ctx.vec.tensor_reduce(la[:, 0:136], karg[:, :].rearrange("p (ha d) -> p ha d", d=POS),
                              AXX, ALU.add)
        yield
        ctx.vec.tensor_scalar_max(sml[:, 32:40], fdd[:, f0 + 16:f0 + 24], 0.1)
        ctx.vec.reciprocal(sml[:, 32:40], sml[:, 32:40])
        ctx.vec.tensor_tensor(
            la[:, 136:272].rearrange("p (h a) -> p h a", h=H1),
            abc[:, :].rearrange("p (o a) -> p o a", o=1).broadcast_to([128, H1, NA]),
            fdd[:, f0:f0 + 8].rearrange("p (h o) -> p h o", o=1).broadcast_to([128, H1, NA]),
            ALU.mult)
        ctx.vec.tensor_tensor(
            la[:, 136:272].rearrange("p (h a) -> p h a", h=H1),
            la[:, 136:272].rearrange("p (h a) -> p h a", h=H1),
            sml[:, 32:40].rearrange("p (h o) -> p h o", o=1).broadcast_to([128, H1, NA]),
            ALU.mult)
        yield
        lg = scr[:, LG:LG + 264]
        env = scr[:, ENV:ENV + 264]
        for (srcoff, dst) in ((0, lg), (136, env)):
            ctx.vec.tensor_copy(dst[:, :].rearrange("p (h s) -> p h s", h=H1)[:, :, 0:16],
                                la[:, srcoff:srcoff + 136].rearrange("p (h a) -> p h a", h=H1)[:, :, 16:0:-1])
            ctx.vec.tensor_copy(dst[:, :].rearrange("p (h s) -> p h s", h=H1)[:, :, 16:33],
                                la[:, srcoff:srcoff + 136].rearrange("p (h a) -> p h a", h=H1)[:, :, 0:17])
            yield
        BIG = 1e30
        ctx.vec.tensor_scalar(scr[:, T33 + 2 * NS:T33 + 3 * NS], vm, BIG, -BIG, ALU.mult, ALU.add)
        mneg = scr[:, T33 + 2 * NS:T33 + 3 * NS]
        ctx.vec.tensor_tensor(lg, lg, vm.rearrange("p (o s) -> p o s", o=1).broadcast_to([128, H1, NS]), ALU.mult)
        ctx.vec.tensor_tensor(lg, lg, mneg.rearrange("p (o s) -> p o s", o=1).broadcast_to([128, H1, NS]), ALU.add)
        yield
        ctx.vec.tensor_reduce(sml[:, 0:8], lg.rearrange("p (h s) -> p h s", h=H1), AXX, ALU.max)
        ctx.vec.tensor_scalar_mul(sml[:, 0:8], sml[:, 0:8], -0.25)
        i1 = ctx.vec.scalar_tensor_tensor(
            lg, lg, 0.25,
            sml[:, 0:8].rearrange("p (h o) -> p h o", o=1).broadcast_to([128, H1, NS]),
            ALU.mult, ALU.add)
        s_ev.inc(i1)
        yield
        s_ev.wait(nc.scalar)
        ctx.act.activation(lg, lg, AF.Exp)
        s_ab.inc(ctx.act.activation(env, env, AF.Exp, scale=-1.0))
        yield
        s_ab.wait(nc.vector)
        we = scr[:, WE:WE + 264]
        ctx.vec.tensor_reduce(sml[:, 8:16], lg.rearrange("p (h s) -> p h s", h=H1), AXX, ALU.add)
        ctx.vec.tensor_mul(we, lg, env)
        yield
        ctx.vec.tensor_reduce(sml[:, 16:24], we.rearrange("p (h s) -> p h s", h=H1), AXX, ALU.add)
        ctx.vec.scalar_tensor_tensor(sml[:, 24:32], sml[:, 8:16], 1e-8, sml[:, 16:24], ALU.mult, ALU.add)
        ctx.vec.reciprocal(sml[:, 24:32], sml[:, 24:32])
        yield "hold"
        s_ab.wait(nc.vector)
        i1 = ctx.vec.tensor_tensor(
            ww[:, :].rearrange("p (h s) -> p h s", h=H1),
            we[:, :].rearrange("p (h s) -> p h s", h=H1),
            sml[:, 24:32].rearrange("p (h o) -> p h o", o=1).broadcast_to([128, H1, NS]),
            ALU.mult)
        if tc >= LOOK:
            s_gp.wait(nc.vector, gmark[tc - LOOK])
        i1 = ctx.vec.tensor_copy(idx32[:, (tc % LOOK) * NS:(tc % LOOK) * NS + NS],
                                 scr[:, T33 + 3 * NS:T33 + 4 * NS])
        s_cblk.inc(i1)
        cmark.append(s_cblk.v)

    s_mac = [ctx.sem(f"a1_mc{i}") for i in range(NSLOT)]
    slot_free_at = [0] * NSLOT
    stream = [(t, si) for t in range(16) for si in range(NS)]
    gnext = [0]

    def issue_gather_upto(k):
        while gnext[0] <= min(k, len(stream) - 1):
            tg, sg = stream[gnext[0]]
            slot = gnext[0] % NSLOT
            if sg == 0:
                s_cblk.wait(nc.gpsimd, cmark[tg])
            if slot_free_at[slot] > 0:
                nc.gpsimd.wait_ge(s_mac[slot].h, slot_free_at[slot])
            ins = nc.gpsimd.indirect_dma_start(
                vg[:, slot * 1024:(slot + 1) * 1024], None,
                xtok, IndirectOffsetOnAxis(idx32[:, (tg % LOOK) * NS + sg:(tg % LOOK) * NS + sg + 1], 0))
            s_gd[slot].inc(ins, 16)
            if sg == NS - 1:
                s_gp.inc(nc.gpsimd.engine_nop())
                gmark[tg] = s_gp.v
            gnext[0] += 1

    scl_ready = [None] * NSLOT

    def macs(tc, gen):
        held = [gen is None]

        def pump():
            if held[0]:
                return
            try:
                if next(gen) == "hold":
                    held[0] = True
            except StopIteration:
                held[0] = True

        ws = (tc % LOOK) * 264
        ww = wwall[:, ws:ws + 264]
        pacc = (pA, pB)[tc % 2]
        if tc >= 2:
            s_ab.wait(nc.tensor, tpmark[tc - 2])
        for si in range(NS):
            k = tc * NS + si
            issue_gather_upto(k + NSLOT - 1)
            slot = k % NSLOT
            on_act = (L1_SPLIT > 0 and si % L1_SPLIT == L1_SPLIT - 1)
            weng = nc.scalar if on_act else nc.vector
            s_gd[slot].wait(weng)
            vgv = vg[:, slot * 1024:(slot + 1) * 1024].rearrange("p (h d) -> p h d", h=H1)
            wbc = ww[:, :].rearrange("p (h s) -> p h s", h=H1)[:, :, si:si + 1].broadcast_to([128, H1, 128])
            if on_act:
                for h in range(H1):
                    wcol = ww[:, h * NS + si:h * NS + si + 1]
                    vs = vg[:, slot * 1024 + h * 128:slot * 1024 + (h + 1) * 128]
                    i1 = ctx.act.activation(vs, vs, AF.Copy, scale=wcol)
                s_ab.inc(i1)
                scl_ready[slot] = (s_ab, s_ab.v)
            else:
                i1 = ctx.vec.tensor_tensor(vgv, vgv, wbc, ALU.mult)
                s_ev.inc(i1)
                scl_ready[slot] = (s_ev, s_ev.v)
            sem, v = scl_ready[slot]
            nc.tensor.wait_ge(sem.h, v)
            for half in range(2):
                mm = nc.tensor.matmul(pacc[:, half * 512:(half + 1) * 512],
                                      consts["ident_bf"][:, :],
                                      vg[:, slot * 1024 + half * 512:slot * 1024 + (half + 1) * 512],
                                      start=(si == 0), stop=(si == NS - 1),
                                      skip_group_check=True)
                if half == 1:
                    s_mac[slot].inc(mm)
                    slot_free_at[slot] = s_mac[slot].v
            pump()

    tpmark = {}

    def evac(tc):
        pacc = (pA, pB)[tc % 2]
        xsl = xts[:, (tc % 2) * 1024:(tc % 2) * 1024 + 1024]
        s_mm.wait(nc.scalar)
        s_mac[(tc * NS + NS - 1) % NSLOT].wait(nc.scalar)
        i1 = ctx.act.activation(xsl, pacc[:, 0:1024], AF.Copy)
        s_ab.inc(i1)
        tpmark[tc] = s_ab.v
        s_ab.wait(nc.tensor)
        for c in range(NC):
            pt = pT[c % 2]
            ptrot.acquire(nc.tensor)
            s_mm.inc(nc.tensor.transpose(pt[:, :], xsl[:, c * 128:c * 128 + 128], ident[:, :]))
            s_mm.wait(nc.scalar)
            i1 = ctx.act.activation(amid[:, c * T + tc * 128:c * T + tc * 128 + 128], pt[:, :], AF.Copy)
            s_ab.inc(i1)
            ptrot.release(s_ab)

    for t in range(LOOK):
        for _ in stage_c_gen(t):
            pass
    for t in range(16):
        gen = stage_c_gen(t + LOOK) if t + LOOK < 16 else None
        if not L1_ILV and gen is not None:
            while next(gen, "hold") != "hold":
                pass
            macs(t, None)
        else:
            macs(t, gen)
        if gen is not None:
            for _ in gen:
                pass
        evac(t)
    s_ev.wait(nc.tensor)
    s_ab.wait(nc.tensor)
    s_ev.wait(nc.gpsimd)
    s_ab.wait(nc.gpsimd)

    if L1_LIMIT < 7:
        s_ev.wait(nc.tensor); s_ev.wait(nc.scalar); s_ev.wait(nc.sync); s_ev.wait(nc.gpsimd)
        s_gd[0].wait(nc.vector); s_gd[1].wait(nc.vector); s_gd[2].wait(nc.vector); s_gd[3].wait(nc.vector)
        stack.close()
        return
    se_rot = Rot(2)
    seh = stack.enter_context(nc.sbuf_tensor("a1_sh" + tag, [128, 2 * T], BF16))
    sec = stack.enter_context(nc.sbuf_tensor("a1_se" + tag, [128, 512], BF16))
    for st4 in range(4):
        n0 = st4 * 512
        for oc in range(2):
            ppp = (pA, pB)[oc % 2]
            se_rot.acquire(nc.tensor)
            for kc in range(NC):
                mm = nc.tensor.matmul(ppp[:, 0:512],
                                      wsml[:, O_SE1 + kc * 256 + oc * 128:O_SE1 + kc * 256 + oc * 128 + 128],
                                      amid[:, kc * T + n0:kc * T + n0 + 512],
                                      start=(kc == 0), stop=(kc == NC - 1))
                if kc == NC - 1:
                    s_mm.inc(mm)
            s_mm.wait(nc.scalar)
            s_ab.inc(ctx.act.activation(seh[:, oc * T + n0:oc * T + n0 + 512], ppp[:, 0:512],
                                          AF.Silu, bias=prm["se1_b"][:, oc:oc + 1]))
            se_rot.release(s_ab)
    s_ab.wait(nc.tensor)
    for st4 in range(4):
        n0 = st4 * 512
        for oc in range(NC):
            ppp = (pA, pB)[oc % 2]
            se_rot.acquire(nc.tensor)
            for kc in range(2):
                mm = nc.tensor.matmul(ppp[:, 0:512],
                                      wsml[:, O_SE2 + kc * 1024 + oc * 128:O_SE2 + kc * 1024 + oc * 128 + 128],
                                      seh[:, kc * T + n0:kc * T + n0 + 512],
                                      start=(kc == 0), stop=(kc == 1))
                if kc == 1:
                    s_mm.inc(mm)
            s_mm.wait(nc.scalar)
            s_ab.inc(ctx.act.activation(sec[:, 0:512], ppp[:, 0:512],
                                          AF.Sigmoid, bias=prm["se2_b"][:, oc:oc + 1]))
            s_ab.wait(nc.vector)
            sl = slice(oc * T + n0, oc * T + n0 + 512)
            s_ev.inc(ctx.vec.tensor_mul(amid[:, sl], amid[:, sl], sec[:, 0:512]))
            se_rot.release(s_ev)
            s_ev.wait(nc.scalar)
    s_ev.wait(nc.tensor)
    woutw = sc["woutw"]
    wo_rot = Rot(2)
    for st4 in range(4):
        n0 = st4 * 512
        for oc in range(NC):
            slot = oc % 2
            wo_rot.acquire(nc.sync)
            s_dl[slot].inc(nc.sync.dma_start(wostream[:, slot * 1024:(slot + 1) * 1024], woutw[oc, :, :]), 16)
            s_dl[slot].wait(nc.tensor)
            ppp = (pA, pB)[slot]
            se_rot.acquire(nc.tensor)
            for kc in range(NC):
                mm = nc.tensor.matmul(ppp[:, 0:512],
                                      wostream[:, slot * 1024 + kc * 128:slot * 1024 + kc * 128 + 128],
                                      amid[:, kc * T + n0:kc * T + n0 + 512],
                                      start=(kc == 0), stop=(kc == NC - 1))
                if kc == NC - 1:
                    s_mm.inc(mm)
            wo_rot.release(s_mm)
            s_mm.wait(nc.scalar)
            s_ab.inc(ctx.act.activation(sec[:, 0:512], ppp[:, 0:512], AF.Silu))
            s_ab.wait(nc.vector)
            sl = slice(oc * T + n0, oc * T + n0 + 512)
            s_ev.inc(ctx.vec.tensor_add(xres[:, sl], xres[:, sl], sec[:, 0:512]))
            se_rot.release(s_ev)
            s_ev.wait(nc.scalar)
    s_ev.wait(nc.tensor)
    s_ev.wait(nc.scalar)
    s_ev.wait(nc.sync)
    s_ev.wait(nc.gpsimd)
    stack.close()


def build_program(phases, n_seq=NSEQ):
    nc = bass.Bass("TRN2", target_bir_lowering=False, detect_race_conditions=False)
    stack = ExitStack()
    ctx = Ctx(nc, stack)
    io = declare_io(nc, phases)
    sc = declare_scratch(nc, phases)

    ctx.vec
    ctx.act
    consts = emit_consts(ctx, io, phases)
    prm = emit_params(ctx, io, phases)
    barrier(ctx)
    emit_preprocess(ctx, io, sc, phases, consts)
    barrier(ctx)

    xres = ctx.sb("xres", [128, NC * T], F32)
    xln = ctx.sb("xln", [128, NC * T], BF16)
    bufs = {"xres": xres, "xln": xln, "consts": consts, "prm": prm, "io": io}

    sx = ctx.sem("x_dl")
    sy = ctx.sem("y_st")
    for s in range(n_seq):
        for c in range(NC):
            sx.inc(nc.sync.dma_start(xres[:, c * T:(c + 1) * T], io["x"][s, c * 128:(c + 1) * 128, :]), 16)
        for eng in (nc.vector, nc.scalar, nc.tensor, nc.gpsimd, nc.sync):
            sx.wait(eng)
        for pi, ph in enumerate(phases):
            tag = f"_s{s}p{pi}"
            emit_ln(ctx, bufs, tag)
            if ph == "l0ffn":
                emit_l0ffn(ctx, bufs, sc, tag)
            elif ph == "l0attn":
                emit_l0attn(ctx, bufs, sc, tag)
            elif ph == "l1ffn":
                emit_l1ffn(ctx, bufs, sc, tag)
            elif ph == "l1attn":
                emit_l1attn(ctx, bufs, sc, tag)
            else:
                raise ValueError(ph)
        for c in range(NC):
            sy.inc(nc.sync.dma_start(io["y"][s, c * 128:(c + 1) * 128, :], xres[:, c * T:(c + 1) * T]), 16)
        if s < n_seq - 1:
            sy.wait(nc.sync)
    sy.wait(nc.sync)
    stack.close()
    return nc


ALL_PHASES = ("l0ffn", "l0attn", "l0ffn", "l1ffn", "l1attn", "l1ffn")


def _run(phases, inputs, x_override=None, cores=8):
    x = inputs["L0_attn_x"] if x_override is None else x_override
    xt = np.ascontiguousarray(x.transpose(0, 2, 1))
    nc = build_program(phases)
    in_maps = [{"x": np.ascontiguousarray(xt[ci * NSEQ:(ci + 1) * NSEQ])} for ci in range(cores)]
    for m in in_maps:
        for n in _needed_inputs(phases):
            m[n] = np.ascontiguousarray(inputs[n])
    res = run_bass_kernel_spmd(nc, in_maps, core_ids=list(range(cores)))
    out = np.empty((cores * NSEQ, T, D), np.float32)
    for ci in range(cores):
        y = res.results[ci]["y"]
        out[ci * NSEQ:(ci + 1) * NSEQ] = y.transpose(0, 2, 1)
    return out


def kernel(**inputs):
    return _run(ALL_PHASES, inputs)


# version 17 (final)
# speedup vs baseline: 1.0959x; 1.0494x over previous
import numpy as np
import ml_dtypes
from contextlib import ExitStack

import concourse.bass as bass
import concourse.mybir as mybir
from concourse.bass_utils import run_bass_kernel_spmd
from concourse.bass import IndirectOffsetOnAxis

F32 = mybir.dt.float32
BF16 = mybir.dt.bfloat16
I32 = mybir.dt.int32
I16 = mybir.dt.int16
U8 = mybir.dt.uint8
AF = mybir.ActivationFunctionType
ALU = mybir.AluOpType
AXX = mybir.AxisListType.X

D = 1024
T = 2048
NSEQ = 4
NC = 8
HID = 3072
HC = HID // 128
R127 = float(np.float32(1.0 / 127.0))
MAGIC = 8388608.0
LNEPS = 1e-5

H1 = 8
POS = 16
NS = 33
NA = 17
WIN = 672
PAD = 272
TPAD = T + 2 * PAD
L1_LIMIT = 99
L1_NOMAC = False
L1_FIXIDX = False
L1_NGATHER = NS
L1_NSLOT = 8
L1_SPLIT = 4
L1_ILV = True


class Sem:
    def __init__(self, h):
        self.h = h
        self.v = 0

    def inc(self, ins, n=1):
        ins.then_inc(self.h, n)
        self.v += n
        return ins

    def wait(self, eng, v=None):
        eng.wait_ge(self.h, self.v if v is None else v)


class Rot:

    def __init__(self, n):
        self.n = n
        self.q = []

    def acquire(self, *engs):
        if len(self.q) >= self.n:
            sem, v = self.q.pop(0)
            for e in engs:
                e.wait_ge(sem.h, v)

    def release(self, sem):
        self.q.append((sem, sem.v))


class Ctx:
    def __init__(self, nc, stack):
        self.nc = nc
        self.stack = stack
        self._sems = {}

    def sem(self, name) -> Sem:
        if name not in self._sems:
            self._sems[name] = Sem(self.stack.enter_context(self.nc.semaphore(name)))
        return self._sems[name]

    def sb(self, name, shape, dtype):
        return self.stack.enter_context(self.nc.sbuf_tensor(name, shape, dtype))

    @property
    def vec(self):
        if not hasattr(self, "_vec"):
            self._vec = Guarded(self, self.nc.vector, "vgd")
        return self._vec

    @property
    def act(self):
        if not hasattr(self, "_act"):
            self._act = Guarded(self, self.nc.scalar, "agd")
        return self._act


_GUARD_OPS = {
    "tensor_copy", "tensor_tensor", "tensor_add", "tensor_sub", "tensor_mul",
    "tensor_max", "tensor_scalar", "tensor_single_scalar", "tensor_scalar_add",
    "tensor_scalar_sub", "tensor_scalar_mul", "tensor_scalar_max",
    "tensor_scalar_min", "tensor_relu", "tensor_reduce", "reduce_max",
    "reduce_sum", "scalar_tensor_tensor", "reciprocal", "select", "memset",
    "activation", "copy", "mul", "add", "square", "sqrt", "sign",
}


class Guarded:

    SHORT = 64

    def __init__(self, ctx, eng, name):
        self._ctx = ctx
        self._eng = eng
        self._name = name
        self._recent = []
        self._pad = ctx.sb("gpad_" + name, [128, 128], F32)
        if hasattr(eng, "memset"):
            eng.memset(self._pad[:, :], 0.0)
        else:
            eng.memzero(self._pad[:, :])

    def _gap(self):
        if self._name.startswith("a"):
            self._eng.activation(self._pad[:, 0:64], self._pad[:, 64:128], AF.Copy)
        else:
            self._eng.tensor_copy(self._pad[:, 0:64], self._pad[:, 64:128])

    def __getattr__(self, attr):
        real = getattr(self._eng, attr)
        if attr not in _GUARD_OPS:
            return real

        def wrap(*args, **kw):
            aps = [a for a in list(args) + list(kw.values()) if isinstance(a, bass.AP)]
            out = aps[0] if aps else None
            ins = aps[1:]
            innames = {a.tensor.name for a in ins}
            if out is not None and attr != "memset":
                innames.add(out.tensor.name)
            if any(n in self._recent for n in innames):
                self._gap()
                self._recent = []
            ins_res = real(*args, **kw)
            if out is not None and out.free_size() < self.SHORT:
                self._recent.append(out.tensor.name)
                self._recent = self._recent[-2:]
            else:
                self._recent = []
            return ins_res

        return wrap


def barrier(ctx):
    nc = ctx.nc
    with nc.Block() as blk:
        @blk.vector
        def _(v):
            v.engine_nop()


TRANSPOSED = {
    "wgT": ("L0_ffn_w_gate", HID, D, True),
    "wuT": ("L0_ffn_w_up", HID, D, True),
    "wdT": ("L0_ffn_w_down", D, HID, True),
    "woT": ("L0_attn_W_o", D, D, False),
    "wrT": ("L1_ffn_W_r", D, D, False),
    "wkT": ("L1_ffn_W_k", D, D, False),
    "wvT": ("L1_ffn_W_v", D, D, False),
}
NATURAL = {
    "wdq": ("L0_attn_W_dq", D, 256),
    "wuq": ("L0_attn_W_uq", 256, D),
    "wdkv": ("L0_attn_W_dkv", D, 256),
    "wukv": ("L0_attn_W_ukv", 256, 2 * D),
    "wwave": ("L1_attn_wave_w", D, 24),
    "wquery": ("L1_attn_query_w", D, 128),
    "wse1": ("L1_attn_se1_w", D, 256),
    "wse2": ("L1_attn_se2_w", 256, D),
    "woutw": ("L1_attn_out_w", D, D),
}
SHAPES = {
    "L0_attn_W_dq": ([D, 256], F32), "L0_attn_W_uq": ([256, D], F32),
    "L0_attn_q_ln_g": ([256], F32), "L0_attn_q_ln_b": ([256], F32),
    "L0_attn_W_dkv": ([D, 256], F32), "L0_attn_W_ukv": ([256, 2 * D], F32),
    "L0_attn_kv_ln_g": ([256], F32), "L0_attn_kv_ln_b": ([256], F32),
    "L0_attn_W_o": ([D, D], F32),
    "L0_ffn_w_gate": ([HID, D], I32), "L0_ffn_w_up": ([HID, D], I32),
    "L0_ffn_w_down": ([D, HID], I32),
    "L1_attn_wave_w": ([D, 24], F32), "L1_attn_wave_b": ([24], F32),
    "L1_attn_query_w": ([D, 128], F32), "L1_attn_query_b": ([128], F32),
    "L1_attn_key_w": ([POS], F32), "L1_attn_out_w": ([D, D], F32),
    "L1_attn_se1_w": ([D, 256], F32), "L1_attn_se1_b": ([256], F32),
    "L1_attn_se2_w": ([256, D], F32), "L1_attn_se2_b": ([D], F32),
    "L1_ffn_W_r": ([D, D], F32), "L1_ffn_W_k": ([D, D], F32),
    "L1_ffn_W_v": ([D, D], F32),
    "L1_ffn_mu_r": ([D], F32), "L1_ffn_mu_k": ([D], F32),
}


def _needed_inputs(phases):
    need = set()
    if "l0ffn" in phases:
        need |= {"L0_ffn_w_gate", "L0_ffn_w_up", "L0_ffn_w_down"}
    if "l0attn" in phases:
        need |= {"L0_attn_W_dq", "L0_attn_W_uq", "L0_attn_q_ln_g", "L0_attn_q_ln_b",
                 "L0_attn_W_dkv", "L0_attn_W_ukv", "L0_attn_kv_ln_g", "L0_attn_kv_ln_b",
                 "L0_attn_W_o"}
    if "l1ffn" in phases:
        need |= {"L1_ffn_W_r", "L1_ffn_W_k", "L1_ffn_W_v", "L1_ffn_mu_r", "L1_ffn_mu_k"}
    if "l1attn" in phases:
        need |= {"L1_attn_wave_w", "L1_attn_wave_b", "L1_attn_query_w", "L1_attn_query_b",
                 "L1_attn_key_w", "L1_attn_out_w", "L1_attn_se1_w", "L1_attn_se1_b",
                 "L1_attn_se2_w", "L1_attn_se2_b"}
    return sorted(need)


def declare_io(nc, phases):
    io = {}
    io["x"] = nc.dram_tensor("x", [NSEQ, D, T], F32, kind="ExternalInput").ap()
    io["y"] = nc.dram_tensor("y", [NSEQ, D, T], F32, kind="ExternalOutput").ap()
    for n in _needed_inputs(phases):
        sh, dt = SHAPES[n]
        io[n] = nc.dram_tensor(n, sh, dt, kind="ExternalInput").ap()
    return io


def declare_scratch(nc, phases):
    sc = {}
    tneed = []
    nneed = []
    if "l0ffn" in phases:
        tneed += ["wgT", "wuT", "wdT"]
    if "l0attn" in phases:
        tneed += ["woT"]
        nneed += ["wdq", "wuq", "wdkv", "wukv"]
    if "l1ffn" in phases:
        tneed += ["wrT", "wkT", "wvT"]
    if "l1attn" in phases:
        nneed += ["wwave", "wquery", "wse1", "wse2"]
        sc["woutw"] = nc.dram_tensor("sc_woutw", [D // 128, 128, D], BF16, kind="Internal").ap()
        sc["xtok"] = nc.dram_tensor("sc_xtok", [TPAD, D], BF16, kind="Internal").ap()
    for n in tneed:
        R, C = TRANSPOSED[n][1], TRANSPOSED[n][2]
        sc[n] = nc.dram_tensor("sc_" + n, [R // 128, 128, C], BF16, kind="Internal").ap()
    for n in nneed:
        K, N = NATURAL[n][1], NATURAL[n][2]
        sc[n] = nc.dram_tensor("sc_" + n, [K // 128, 128, N], BF16, kind="Internal").ap()
    return sc


def emit_consts(ctx, io, phases):
    nc = ctx.nc
    consts = {}
    ones_bf = ctx.sb("ones_bf", [128, 128], BF16)
    ident_bf = ctx.sb("ident_bf", [128, 128], BF16)
    epscol = ctx.sb("epscol", [128, 1], F32)
    consts["ones_bf"] = ones_bf
    consts["ident_bf"] = ident_bf
    consts["epscol"] = epscol
    scw = ctx.sem("c_cw")
    ctx.vec.memset(epscol[:, :], LNEPS)
    ctx.vec.memset(ident_bf[:, :], 0.0)
    scw.inc(ctx.vec.memset(ones_bf[:, :], 1.0))
    scw.wait(nc.gpsimd)
    scw.inc(nc.gpsimd.affine_select(ident_bf[:, :], ones_bf[:, :], [[-1, 128]],
                                    ALU.is_equal, 0.0, base=0, channel_multiplier=1))
    if "l0attn" in phases:
        tril = ctx.sb("tril_bf", [128, 128], BF16)
        consts["tril"] = tril
        scw.inc(nc.gpsimd.affine_select(tril[:, :], ones_bf[:, :], [[-1, 128]],
                                        ALU.is_ge, 0.0, base=0, channel_multiplier=1))
    if "l1attn" in phases:
        obc = ctx.sb("obc", [128, NS], F32)
        abc = ctx.sb("abc", [128, NA], F32)
        center = ctx.sb("center", [128, 1], F32)
        iota16 = ctx.sb("iota16", [128, 128], mybir.dt.float16)
        consts["obc"], consts["abc"], consts["center"] = obc, abc, center
        consts["iota16"] = iota16
        scw.inc(nc.gpsimd.iota(iota16[:, :], [[1, 128]], base=0, channel_multiplier=0,
                               allow_small_or_imprecise_dtypes=True))
        scw.inc(nc.gpsimd.iota(obc[:, :], [[1, NS]], base=-16, channel_multiplier=0,
                               allow_small_or_imprecise_dtypes=True))
        scw.inc(nc.gpsimd.iota(abc[:, :], [[1, NA]], base=0, channel_multiplier=0,
                               allow_small_or_imprecise_dtypes=True))
        scw.inc(nc.gpsimd.iota(center[:, :], [[0, 1]], base=0, channel_multiplier=1,
                               allow_small_or_imprecise_dtypes=True))
    scw.wait(nc.tensor)
    scw.wait(nc.vector)
    scw.wait(nc.scalar)
    return consts


def emit_params(ctx, io, phases):
    nc = ctx.nc
    prm = {}
    sdl = ctx.sem("prm_dl")

    def col(name, src_ap, n):
        t = ctx.sb("p_" + name, [128, max(1, n // 128)], F32)
        sdl.inc(nc.sync.dma_start(t[:, :], src_ap.rearrange("(c p) -> p c", p=128),
                                  allow_slow_non_contiguous=True), 16)
        prm[name] = t

    def row(name, src_ap, n):
        t = ctx.sb("r_" + name, [1, n], F32)
        sdl.inc(nc.sync.dma_start(t[0:1, :], src_ap.rearrange("(o n) -> o n", o=1)), 16)
        prm[name] = t

    if "l0attn" in phases:
        col("q_ln_g", io["L0_attn_q_ln_g"], 256)
        col("q_ln_b", io["L0_attn_q_ln_b"], 256)
        col("kv_ln_g", io["L0_attn_kv_ln_g"], 256)
        col("kv_ln_b", io["L0_attn_kv_ln_b"], 256)
    if "l1ffn" in phases:
        col("mu_r", io["L1_ffn_mu_r"], D)
        col("mu_k", io["L1_ffn_mu_k"], D)
    if "l1attn" in phases:
        row("wave_b", io["L1_attn_wave_b"], 24)
        row("query_b", io["L1_attn_query_b"], 128)
        row("key_w", io["L1_attn_key_w"], POS)
        col("se1_b", io["L1_attn_se1_b"], 256)
        col("se2_b", io["L1_attn_se2_b"], D)
    for eng in (nc.vector, nc.scalar, nc.tensor, nc.gpsimd):
        sdl.wait(eng)
    return prm


def emit_preprocess(ctx, io, sc, phases, consts):
    nc = ctx.nc
    s_ld = [ctx.sem("pp_ld0"), ctx.sem("pp_ld1")]
    s_cv = ctx.sem("pp_cv")
    s_tp = ctx.sem("pp_tp")
    s_cp = ctx.sem("pp_cp")
    s_st = [ctx.sem("pp_st0"), ctx.sem("pp_st1")]
    stack = ExitStack()
    sload = stack.enter_context(nc.sbuf_tensor("pp_load", [128, 2 * HID], F32))
    sloadi = stack.enter_context(nc.sbuf_tensor("pp_loadi", [128, 2 * HID], I32))
    sbf = stack.enter_context(nc.sbuf_tensor("pp_bf", [128, 2 * HID], BF16))
    sstage = stack.enter_context(nc.sbuf_tensor("pp_stage", [128, 2 * HID], BF16))
    ppt = [stack.enter_context(nc.psum_tensor(f"pp_pt{i}", [128, 128], BF16)) for i in range(4)]
    ident = consts["ident_bf"]

    load_rot = Rot(2)
    pt_rot = Rot(2)

    idx = 0
    for n in [t for t in TRANSPOSED if t in sc]:
        src_name, R, C, is_int = TRANSPOSED[n]
        src, dst = io[src_name], sc[n]
        for rc in range(R // 128):
            sl_i = idx % 2
            b = sl_i * HID
            load_rot.acquire(nc.sync)
            if is_int:
                s_ld[sl_i].inc(nc.sync.dma_start(sloadi[:, b:b + C], src[rc * 128:(rc + 1) * 128, :]), 16)
            else:
                s_ld[sl_i].inc(nc.sync.dma_start(sload[:, b:b + C], src[rc * 128:(rc + 1) * 128, :]), 16)
            s_ld[sl_i].wait(nc.vector)
            s_st[sl_i].wait(nc.vector)
            if is_int:
                s_cv.inc(ctx.vec.tensor_scalar_mul(sbf[:, b:b + C], sloadi[:, b:b + C], R127))
            else:
                s_cv.inc(ctx.vec.tensor_copy(sbf[:, b:b + C], sload[:, b:b + C]))
            load_rot.release(s_cv)
            s_cv.wait(nc.tensor)
            ncc = C // 128
            tmarks = {}
            for cc in range(ncc + 2):
                if cc < ncc:
                    pt = ppt[cc % 4]
                    pt_rot.acquire(nc.tensor)
                    s_tp.inc(nc.tensor.transpose(pt[:, :], sbf[:, b + cc * 128:b + (cc + 1) * 128], ident[:, :]))
                    tmarks[cc] = s_tp.v
                if cc >= 2:
                    cp = cc - 2
                    s_tp.wait(nc.vector, tmarks[cp])
                    s_cp.inc(ctx.vec.tensor_copy(sstage[:, b + cp * 128:b + (cp + 1) * 128], ppt[cp % 4][:, :]))
                    pt_rot.release(s_cp)
            s_cp.wait(nc.sync)
            s_st[sl_i].inc(nc.sync.dma_start(dst[rc, :, :], sstage[:, b:b + C]), 16)
            idx += 1

    for n in [t for t in NATURAL if t in sc]:
        src_name, K, N = NATURAL[n]
        src, dst = io[src_name], sc[n]
        grouped = (n == "woutw")
        for kc in range(K // 128):
            sl_i = idx % 2
            b = sl_i * HID
            load_rot.acquire(nc.sync)
            s_ld[sl_i].inc(nc.sync.dma_start(sload[:, b:b + N], src[kc * 128:(kc + 1) * 128, :]), 16)
            s_ld[sl_i].wait(nc.vector)
            s_st[sl_i].wait(nc.vector)
            s_cv.inc(ctx.vec.tensor_copy(sbf[:, b:b + N], sload[:, b:b + N]))
            load_rot.release(s_cv)
            s_cv.wait(nc.sync)
            if grouped:
                s_st[sl_i].inc(nc.sync.dma_start(
                    dst[:, :, kc * 128:(kc + 1) * 128].rearrange("o p j -> p o j"),
                    sbf[:, b:b + N].rearrange("p (o j) -> p o j", j=128)), 16)
            else:
                s_st[sl_i].inc(nc.sync.dma_start(dst[kc, :, :], sbf[:, b:b + N]), 16)
            idx += 1
    if "xtok" in sc:
        zt = stack.enter_context(nc.sbuf_tensor("pp_zt", [128, D], BF16))
        s_cv.inc(ctx.vec.memset(zt[:, :], 0.0))
        s_cv.wait(nc.sync)
        xt = sc["xtok"]
        s_st[0].inc(nc.sync.dma_start(xt[0:128, :], zt[:, :]), 16)
        s_st[0].inc(nc.sync.dma_start(xt[128:256, :], zt[:, :]), 16)
        s_st[0].inc(nc.sync.dma_start(xt[256:PAD, :], zt[0:16, :]), 16)
        s_st[0].inc(nc.sync.dma_start(xt[T + PAD:T + PAD + 128, :], zt[:, :]), 16)
        s_st[0].inc(nc.sync.dma_start(xt[T + PAD + 128:T + PAD + 256, :], zt[:, :]), 16)
        s_st[0].inc(nc.sync.dma_start(xt[T + PAD + 256:TPAD, :], zt[0:16, :]), 16)
    s_st[0].wait(nc.sync)
    s_st[1].wait(nc.sync)
    stack.close()


def emit_ln(ctx, bufs, tag):
    nc = ctx.nc
    xres, xln = bufs["xres"], bufs["xln"]
    ones = bufs["consts"]["ones_bf"]
    eps = bufs["consts"]["epscol"]
    s_pp = ctx.sem("ln_pp")
    s_mm = ctx.sem("ln_mm")
    s_ev = ctx.sem("ln_ev")
    s_vv = ctx.sem("ln_vv")
    stack = ExitStack()
    xbf = stack.enter_context(nc.sbuf_tensor("ln_xbf" + tag, [128, NC * T], BF16))
    xsq = stack.enter_context(nc.sbuf_tensor("ln_xsq" + tag, [128, NC * T], BF16))
    mub = stack.enter_context(nc.sbuf_tensor("ln_mu" + tag, [128, T], BF16))
    rstdb = stack.enter_context(nc.sbuf_tensor("ln_rstd" + tag, [128, T], BF16))
    muf = stack.enter_context(nc.sbuf_tensor("ln_muf" + tag, [128, 2 * 512], F32))
    tmp = stack.enter_context(nc.sbuf_tensor("ln_tmp" + tag, [128, 2 * 512], F32))
    msq = stack.enter_context(nc.sbuf_tensor("ln_ms" + tag, [128, 2 * 512], F32))
    tmpb = stack.enter_context(nc.sbuf_tensor("ln_tb" + tag, [128, T], BF16))
    ps = stack.enter_context(nc.psum_tensor("ln_ps" + tag, [128, 1024], F32))
    pq = stack.enter_context(nc.psum_tensor("ln_pq" + tag, [128, 1024], F32))
    ppmark = {}
    sqmark = {}
    for st in range(4):
        n0 = st * 512
        for c in range(NC):
            i1 = ctx.act.activation(xbf[:, c * T + n0:c * T + n0 + 512],
                                    xres[:, c * T + n0:c * T + n0 + 512], AF.Copy)
        s_pp.inc(i1)
        ppmark[st] = s_pp.v
    s_pp.wait(nc.vector, ppmark[0])
    for st in range(4):
        n0 = st * 512
        if st > 0:
            s_pp.wait(nc.vector, ppmark[st])
        for c in range(NC):
            sl = slice(c * T + n0, c * T + n0 + 512)
            i1 = ctx.vec.tensor_mul(xsq[:, sl], xbf[:, sl], xbf[:, sl])
        s_vv.inc(i1)
        sqmark[st] = s_vv.v
    emark = {}
    amark = {}
    for st in range(4):
        n0 = st * 512
        c0 = (st % 2) * 512
        p0 = (st % 2) * 512
        if st >= 2:
            s_vv.wait(nc.tensor, amark[st - 2])
        s_pp.wait(nc.tensor, ppmark[st])
        for c in range(NC):
            nc.tensor.matmul(ps[:, c0:c0 + 512], ones[:, :], xbf[:, c * T + n0:c * T + n0 + 512],
                             start=(c == 0), stop=(c == NC - 1))
        s_vv.wait(nc.tensor, sqmark[st])
        for c in range(NC):
            mm = nc.tensor.matmul(pq[:, c0:c0 + 512], ones[:, :], xsq[:, c * T + n0:c * T + n0 + 512],
                                  start=(c == 0), stop=(c == NC - 1))
            if c == NC - 1:
                s_mm.inc(mm)
        s_mm.wait(nc.vector)
        if st >= 2:
            s_ev.wait(nc.vector, emark[st - 2])
        ctx.vec.tensor_scalar_mul(muf[:, p0:p0 + 512], ps[:, c0:c0 + 512], 1.0 / D)
        ctx.vec.tensor_scalar_mul(tmp[:, p0:p0 + 512], pq[:, c0:c0 + 512], 1.0 / D)
        ctx.vec.tensor_mul(msq[:, p0:p0 + 512], muf[:, p0:p0 + 512], muf[:, p0:p0 + 512])
        i1 = ctx.vec.tensor_sub(tmp[:, p0:p0 + 512], tmp[:, p0:p0 + 512], msq[:, p0:p0 + 512])
        s_vv.inc(i1)
        s_vv.wait(nc.scalar)
        ctx.act.activation(mub[:, n0:n0 + 512], muf[:, p0:p0 + 512], AF.Copy)
        ctx.act.activation(tmp[:, p0:p0 + 512], tmp[:, p0:p0 + 512], AF.Ln, bias=eps[:, 0:1])
        i1 = ctx.act.activation(rstdb[:, n0:n0 + 512], tmp[:, p0:p0 + 512], AF.Exp, scale=-0.5)
        s_ev.inc(i1)
        emark[st] = s_ev.v
        s_ev.wait(nc.vector, emark[st])
        for c in range(NC):
            sl = slice(c * T + n0, c * T + n0 + 512)
            ctx.vec.tensor_sub(tmpb[:, n0:n0 + 512], xbf[:, sl], mub[:, n0:n0 + 512])
            i1 = ctx.vec.tensor_mul(xln[:, sl], tmpb[:, n0:n0 + 512], rstdb[:, n0:n0 + 512])
            if c == NC - 1:
                s_vv.inc(i1)
        amark[st] = s_vv.v
    s_vv.wait(nc.tensor)
    s_vv.wait(nc.scalar)
    s_vv.wait(nc.sync)
    s_vv.wait(nc.gpsimd)
    s_ev.wait(nc.tensor)
    s_pp.wait(nc.tensor)
    stack.close()


def emit_l0ffn(ctx, bufs, sc, tag):
    nc = ctx.nc
    xres, xln = bufs["xres"], bufs["xln"]
    s_dl = [ctx.sem("f0_dl0"), ctx.sem("f0_dl1")]
    s_dd = [ctx.sem("f0_dd0"), ctx.sem("f0_dd1")]
    s_mm = ctx.sem("f0_mm")
    s_ev = ctx.sem("f0_ev")
    s_ab = ctx.sem("f0_ab")
    stack = ExitStack()
    wbuf = stack.enter_context(nc.sbuf_tensor("f0_w" + tag, [128, 2 * 2048], BF16))
    wdbuf = stack.enter_context(nc.sbuf_tensor("f0_wd" + tag, [128, 2 * HID], BF16))
    hbuf = stack.enter_context(nc.sbuf_tensor("f0_h" + tag, [128, HC * 1024], BF16))
    sg = stack.enter_context(nc.sbuf_tensor("f0_sg" + tag, [128, 2 * 1024], BF16))
    pg = [stack.enter_context(nc.psum_tensor(f"f0_pg{i}" + tag, [128, 1024], F32)) for i in range(2)]
    pu = [stack.enter_context(nc.psum_tensor(f"f0_pu{i}" + tag, [128, 1024], F32)) for i in range(2)]

    wgT, wuT, wdT = sc["wgT"], sc["wuT"], sc["wdT"]
    w_rot = Rot(2)
    p_rot = Rot(2)
    sg_rot = Rot(2)
    wd_rot = Rot(2)

    for strip in range(2):
        t0 = strip * 1024
        for hc in range(HC):
            slot = hc % 2
            wofs = slot * 2048
            w_rot.acquire(nc.sync)
            s_dl[slot].inc(nc.sync.dma_start(wbuf[:, wofs:wofs + 1024], wgT[hc, :, :]), 16)
            s_dl[slot].inc(nc.sync.dma_start(wbuf[:, wofs + 1024:wofs + 2048], wuT[hc, :, :]), 16)
            s_dl[slot].wait(nc.tensor)
            p_rot.acquire(nc.tensor)
            for n0 in (0, 512):
                for kc in range(NC):
                    nc.tensor.matmul(pg[slot][:, n0:n0 + 512],
                                     wbuf[:, wofs + kc * 128:wofs + (kc + 1) * 128],
                                     xln[:, kc * T + t0 + n0:kc * T + t0 + n0 + 512],
                                     start=(kc == 0), stop=(kc == NC - 1))
            for n0 in (0, 512):
                for kc in range(NC):
                    mm = nc.tensor.matmul(pu[slot][:, n0:n0 + 512],
                                          wbuf[:, wofs + 1024 + kc * 128:wofs + 1024 + (kc + 1) * 128],
                                          xln[:, kc * T + t0 + n0:kc * T + t0 + n0 + 512],
                                          start=(kc == 0), stop=(kc == NC - 1))
                    if n0 == 512 and kc == NC - 1:
                        s_mm.inc(mm)
            w_rot.release(s_mm)
            s_mm.wait(nc.scalar)
            sg_rot.acquire(nc.scalar)
            s_ab.inc(ctx.act.activation(sg[:, slot * 1024:(slot + 1) * 1024], pg[slot][:, :], AF.Silu))
            s_ab.wait(nc.vector)
            s_ev.inc(ctx.vec.tensor_mul(hbuf[:, hc * 1024:(hc + 1) * 1024],
                                          sg[:, slot * 1024:(slot + 1) * 1024], pu[slot][:, :]))
            sg_rot.release(s_ev)
            p_rot.release(s_ev)
        for oc in range(NC):
            slot = oc % 2
            wofs = slot * HID
            wd_rot.acquire(nc.sync)
            s_dd[slot].inc(nc.sync.dma_start(wdbuf[:, wofs:wofs + HID], wdT[oc, :, :]), 16)
            s_dd[slot].wait(nc.tensor)
            s_ev.wait(nc.tensor)
            p_rot.acquire(nc.tensor)
            for n0 in (0, 512):
                for kc in range(HC):
                    mm = nc.tensor.matmul(pg[slot][:, n0:n0 + 512],
                                          wdbuf[:, wofs + kc * 128:wofs + (kc + 1) * 128],
                                          hbuf[:, kc * 1024 + n0:kc * 1024 + n0 + 512],
                                          start=(kc == 0), stop=(kc == HC - 1))
                    if n0 == 512 and kc == HC - 1:
                        s_mm.inc(mm)
            wd_rot.release(s_mm)
            s_mm.wait(nc.vector)
            sl = slice(oc * T + t0, oc * T + t0 + 1024)
            s_ev.inc(ctx.vec.scalar_tensor_tensor(xres[:, sl], pg[slot][:, :], 0.5, xres[:, sl],
                                                    ALU.mult, ALU.add))
            p_rot.release(s_ev)
    s_ev.wait(nc.tensor)
    s_ev.wait(nc.scalar)
    s_ev.wait(nc.sync)
    s_ev.wait(nc.gpsimd)
    stack.close()


def emit_l0attn(ctx, bufs, sc, tag):
    nc = ctx.nc
    xres, xln = bufs["xres"], bufs["xln"]
    prm, consts = bufs["prm"], bufs["consts"]
    ones, tril = consts["ones_bf"], consts["tril"]
    s_dl = [ctx.sem("a0_dl0"), ctx.sem("a0_dl1")]
    s_mm = ctx.sem("a0_mm")
    s_ev = ctx.sem("a0_ev")
    s_ab = ctx.sem("a0_ab")
    stack = ExitStack()
    pA = stack.enter_context(nc.psum_tensor("a0_pA" + tag, [128, 1024], F32))
    pB = stack.enter_context(nc.psum_tensor("a0_pB" + tag, [128, 1024], F32))
    pS = stack.enter_context(nc.psum_tensor("a0_pS" + tag, [128, 1024], F32))
    pO = stack.enter_context(nc.psum_tensor("a0_pO" + tag, [128, 1024], F32))
    cqln = stack.enter_context(nc.sbuf_tensor("a0_cq" + tag, [128, 2 * T], BF16))
    mcol = stack.enter_context(nc.sbuf_tensor("a0_mc" + tag, [128, 8], F32))
    sqt = stack.enter_context(nc.sbuf_tensor("a0_sq" + tag, [128, 2 * 512], BF16))
    wuq = stack.enter_context(nc.sbuf_tensor("a0_wuq" + tag, [128, 2048], BF16))
    s_dl[1].inc(nc.sync.dma_start(wuq[:, :].rearrange("p (k n) -> p k n", k=NATURAL["wuq"][1] // 128), sc["wuq"].rearrange("k p n -> p k n")), 16)

    def comp_proj(wdown, g_col, b_col, dst_ln, sub, wofs):
        cbf = sub.enter_context(nc.sbuf_tensor(f"a0_cbf{wofs}" + tag, [128, 2 * T], BF16))
        csq = sub.enter_context(nc.sbuf_tensor(f"a0_csq{wofs}" + tag, [128, 2 * T], BF16))
        stat = sub.enter_context(nc.sbuf_tensor(f"a0_stt{wofs}" + tag, [128, 2 * T], F32))
        tmpf = sub.enter_context(nc.sbuf_tensor(f"a0_tf{wofs}" + tag, [128, T], F32))
        mu_t, m2_t = stat[:, 0:T], stat[:, T:2 * T]
        prj_rot = Rot(2)
        for j, (oc, st4) in enumerate([(o, st) for o in range(2) for st in range(4)]):
            n0 = st4 * 512
            ppp = (pA, pB)[j % 2]
            prj_rot.acquire(nc.tensor)
            for kc in range(NC):
                mm = nc.tensor.matmul(ppp[:, 0:512],
                                      wdown[:, kc * 256 + oc * 128:kc * 256 + oc * 128 + 128],
                                      xln[:, kc * T + n0:kc * T + n0 + 512],
                                      start=(kc == 0), stop=(kc == NC - 1))
                if kc == NC - 1:
                    s_mm.inc(mm)
            s_mm.wait(nc.scalar)
            ctx.act.activation(cbf[:, oc * T + n0:oc * T + n0 + 512], ppp[:, 0:512], AF.Copy)
            s_ab.inc(ctx.act.activation(csq[:, oc * T + n0:oc * T + n0 + 512], ppp[:, 0:512], AF.Square))
            prj_rot.release(s_ab)
        s_ab.wait(nc.tensor)
        st_rot = Rot(1)
        for st4 in range(4):
            n0 = st4 * 512
            st_rot.acquire(nc.tensor)
            for oc in range(2):
                nc.tensor.matmul(pS[:, 0:512], ones[:, :], cbf[:, oc * T + n0:oc * T + n0 + 512],
                                 start=(oc == 0), stop=(oc == 1))
            for oc in range(2):
                mm = nc.tensor.matmul(pS[:, 512:1024], ones[:, :], csq[:, oc * T + n0:oc * T + n0 + 512],
                                      start=(oc == 0), stop=(oc == 1))
                if oc == 1:
                    s_mm.inc(mm)
            s_mm.wait(nc.scalar)
            ctx.act.activation(mu_t[:, n0:n0 + 512], pS[:, 0:512], AF.Copy, scale=1.0 / 256)
            s_ab.inc(ctx.act.activation(m2_t[:, n0:n0 + 512], pS[:, 512:1024], AF.Copy, scale=1.0 / 256))
            st_rot.release(s_ab)
        s_ab.wait(nc.vector)
        ctx.vec.tensor_mul(tmpf[:, :], mu_t, mu_t)
        s_ev.inc(ctx.vec.tensor_sub(tmpf[:, :], m2_t, tmpf[:, :]))
        s_ev.wait(nc.scalar)
        ctx.act.activation(tmpf[:, :], tmpf[:, :], AF.Ln, bias=consts["epscol"][:, 0:1])
        s_ab.inc(ctx.act.activation(tmpf[:, :], tmpf[:, :], AF.Exp, scale=-0.5))
        s_ab.wait(nc.vector)
        for oc in range(2):
            ctx.vec.tensor_sub(m2_t, cbf[:, oc * T:(oc + 1) * T], mu_t)
            ctx.vec.tensor_mul(m2_t, m2_t, tmpf[:, :])
            i1 = ctx.vec.tensor_scalar(dst_ln[:, oc * T:(oc + 1) * T], m2_t,
                                         g_col[:, oc:oc + 1], b_col[:, oc:oc + 1], ALU.mult, ALU.add)
            if oc == 1:
                s_ev.inc(i1)
        s_ev.wait(nc.tensor)
        s_ev.wait(nc.scalar)
        s_ev.wait(nc.sync)

    kv_stack = ExitStack()
    ckvln = kv_stack.enter_context(nc.sbuf_tensor("a0_ckv" + tag, [128, 2 * T], BF16, side="right"))
    with ExitStack() as sub:
        wdq = sub.enter_context(nc.sbuf_tensor("a0_wdq" + tag, [128, 2048], BF16))
        s_dl[0].inc(nc.sync.dma_start(wdq[:, :].rearrange("p (k n) -> p k n", k=NATURAL["wdq"][1] // 128), sc["wdq"].rearrange("k p n -> p k n")), 16)
        s_dl[0].wait(nc.tensor)
        comp_proj(wdq, prm["q_ln_g"], prm["q_ln_b"], cqln, sub, 0)
    with ExitStack() as sub:
        wdkv = sub.enter_context(nc.sbuf_tensor("a0_wdkv" + tag, [128, 2048], BF16))
        s_dl[0].inc(nc.sync.dma_start(wdkv[:, :].rearrange("p (k n) -> p k n", k=NATURAL["wdkv"][1] // 128), sc["wdkv"].rearrange("k p n -> p k n")), 16)
        s_dl[0].wait(nc.tensor)
        comp_proj(wdkv, prm["kv_ln_g"], prm["kv_ln_b"], ckvln, sub, 1)

    kbf = stack.enter_context(nc.sbuf_tensor("a0_k" + tag, [128, NC * T], BF16))
    vaug = stack.enter_context(nc.sbuf_tensor("a0_v" + tag, [128, 16 * 1040], BF16))
    wukv = kv_stack.enter_context(nc.sbuf_tensor("a0_wukv" + tag, [128, 4096], BF16, side="right"))
    s_dl[0].inc(nc.sync.dma_start(wukv[:, :].rearrange("p (k n) -> p k n", k=NATURAL["wukv"][1] // 128), sc["wukv"].rearrange("k p n -> p k n")), 16)
    s_dl[0].wait(nc.tensor)

    k_rot = Rot(2)
    for j, (oc, st4) in enumerate([(o, st) for o in range(NC) for st in range(4)]):
        n0 = st4 * 512
        ppp = (pA, pB)[j % 2]
        k_rot.acquire(nc.tensor)
        for kc in range(2):
            mm = nc.tensor.matmul(ppp[:, 0:512],
                                  wukv[:, kc * 2048 + oc * 128:kc * 2048 + oc * 128 + 128],
                                  ckvln[:, kc * T + n0:kc * T + n0 + 512],
                                  start=(kc == 0), stop=(kc == 1))
            if kc == 1:
                s_mm.inc(mm)
        s_mm.wait(nc.scalar)
        s_ab.inc(ctx.act.activation(kbf[:, oc * T + n0:oc * T + n0 + 512], ppp[:, 0:512], AF.Copy))
        k_rot.release(s_ab)
    s_ab.wait(nc.tensor)

    def sumsq_max(src, dstcol, tmpcol):
        sq_rot = Rot(2)
        for st4 in range(4):
            n0 = st4 * 512
            for oc in range(NC):
                sq_rot.acquire(nc.scalar)
                s_ab.inc(ctx.act.activation(sqt[:, (oc % 2) * 512:(oc % 2) * 512 + 512],
                                              src[:, oc * T + n0:oc * T + n0 + 512], AF.Square))
                s_ab.wait(nc.tensor)
                mm = nc.tensor.matmul(pS[:, 0:512], ones[:, :], sqt[:, (oc % 2) * 512:(oc % 2) * 512 + 512],
                                      start=(oc == 0), stop=(oc == NC - 1))
                if oc == NC - 1:
                    s_mm.inc(mm)
                sq_rot.release(s_mm)
            s_mm.wait(nc.vector)
            if st4 == 0:
                s_ev.inc(ctx.vec.tensor_reduce(dstcol, pS[:, 0:512], AXX, ALU.max))
            else:
                ctx.vec.tensor_reduce(tmpcol, pS[:, 0:512], AXX, ALU.max)
                s_ev.inc(ctx.vec.tensor_max(dstcol, dstcol, tmpcol))
            s_ev.wait(nc.tensor)
    sumsq_max(kbf, mcol[:, 0:1], mcol[:, 4:5])

    vview = vaug[:, :].rearrange("p (k h e) -> p k h e", k=16, h=16)
    s_vo = ctx.sem("a0_vo")
    s_vo.inc(ctx.vec.memset(vview[:, :, :, 64:65], 1.0))
    v_rot = Rot(2)
    for j in range(32):
        kt, half = j // 2, j % 2
        n0 = half * 512
        ppp = (pA, pB)[j % 2]
        v_rot.acquire(nc.tensor)
        for kc in range(2):
            mm = nc.tensor.matmul(ppp[:, 0:512],
                                  ckvln[:, kc * T + kt * 128:kc * T + kt * 128 + 128],
                                  wukv[:, kc * 2048 + 1024 + n0:kc * 2048 + 1024 + n0 + 512],
                                  start=(kc == 0), stop=(kc == 1))
            if kc == 1:
                s_mm.inc(mm)
        s_mm.wait(nc.vector)
        s_vo.wait(nc.vector)
        dst = vview[:, kt, half * 8:half * 8 + 8, 0:64]
        src = ppp[:, 0:512].rearrange("p (h d) -> p h d", d=64)
        s_ev.inc(ctx.vec.tensor_copy(dst, src))
        v_rot.release(s_ev)
    s_ev.wait(nc.tensor)
    s_ev.wait(nc.scalar)
    s_ev.wait(nc.sync)
    kv_stack.close()

    qst = stack.enter_context(nc.sbuf_tensor("a0_qs" + tag, [128, 8 * 512], BF16))
    ptb = stack.enter_context(nc.sbuf_tensor("a0_pt" + tag, [128, 6 * 512], BF16))
    amid = stack.enter_context(nc.sbuf_tensor("a0_am" + tag, [128, 8 * 512], BF16))
    rden = stack.enter_context(nc.sbuf_tensor("a0_rd" + tag, [128, 2 * 512], F32))
    wostream = stack.enter_context(nc.sbuf_tensor("a0_wo" + tag, [128, 2 * 1024], BF16))

    woT = sc["woT"]
    wo_rot = Rot(2)
    qp_rot = Rot(2)
    pt_rot = Rot(3)
    sc_rot = Rot(2)
    po_rot = Rot(2)
    rd_rot = Rot(2)
    for st4 in range(4):
        q0 = st4 * 512
        ktmax = (q0 + 511) // 128
        for oc in range(NC):
            ppp = (pA, pB)[oc % 2]
            qp_rot.acquire(nc.tensor)
            for kc in range(2):
                mm = nc.tensor.matmul(ppp[:, 0:512],
                                      wuq[:, kc * 1024 + oc * 128:kc * 1024 + oc * 128 + 128],
                                      cqln[:, kc * T + q0:kc * T + q0 + 512],
                                      start=(kc == 0), stop=(kc == 1))
                if kc == 1:
                    s_mm.inc(mm)
            s_mm.wait(nc.scalar)
            s_ab.inc(ctx.act.activation(qst[:, oc * 512:(oc + 1) * 512], ppp[:, 0:512], AF.Copy))
            qp_rot.release(s_ab)
        s_ab.wait(nc.tensor)
        sq_rot = Rot(2)
        for oc in range(NC):
            sq_rot.acquire(nc.scalar)
            s_ab.inc(ctx.act.activation(sqt[:, (oc % 2) * 512:(oc % 2) * 512 + 512],
                                          qst[:, oc * 512:(oc + 1) * 512], AF.Square))
            s_ab.wait(nc.tensor)
            mm = nc.tensor.matmul(pS[:, 0:512], ones[:, :], sqt[:, (oc % 2) * 512:(oc % 2) * 512 + 512],
                                  start=(oc == 0), stop=(oc == NC - 1))
            if oc == NC - 1:
                s_mm.inc(mm)
            sq_rot.release(s_mm)
        s_mm.wait(nc.vector)
        s_ev.inc(ctx.vec.tensor_reduce(mcol[:, 1:2], pS[:, 0:512], AXX, ALU.max))
        s_ev.wait(nc.tensor)
        s_ev.wait(nc.vector)
        s_ab.wait(nc.vector)
        s_ev.inc(ctx.vec.tensor_mul(mcol[:, 2:3], mcol[:, 0:1], mcol[:, 1:2]))
        s_ev.wait(nc.scalar)
        s_ab.inc(ctx.act.activation(mcol[:, 2:3], mcol[:, 2:3], AF.Sqrt, scale=1.0 / 64))
        s_ab.wait(nc.vector)
        s_ev.inc(ctx.vec.tensor_scalar_mul(mcol[:, 3:4], mcol[:, 2:3], -1.0))
        s_ev.wait(nc.scalar)
        s_ev.wait(nc.tensor)
        denmark = {}
        po_of = {}

        def normalize(hh):
            pbb = (hh % 2) * 64
            hcc = hh // 2
            uu = st4 * 16 + hh
            poo, pdd = po_of[hh]
            nc.scalar.wait_ge(s_mm.h, denmark[hh])
            s_ev.wait(nc.scalar)
            rds = rden[pbb:pbb + 64, (uu % 2) * 512:(uu % 2) * 512 + 512]
            ctx.act.activation(rds, pdd, AF.Ln)
            s_ab.inc(ctx.act.activation(rds, rds, AF.Exp, scale=-1.0))
            s_ab.wait(nc.vector)
            i1 = ctx.vec.tensor_mul(amid[pbb:pbb + 64, hcc * 512:hcc * 512 + 512], poo, rds)
            s_ev.inc(i1)
            po_rot.release(s_ev)

        for h in range(16):
            pb = (h % 2) * 64
            hc = h // 2
            u = st4 * 16 + h
            po = pO[pb:pb + 64, (u % 2) * 512:(u % 2) * 512 + 512]
            pd = pA[pb:pb + 64, (u % 2) * 512:(u % 2) * 512 + 512]
            po_of[h] = (po, pd)
            po_rot.acquire(nc.tensor)

            def emit_qk(p):
                kt2 = p * 2
                npair = min(2, ktmax + 1 - kt2)
                psc = (pS, pB)[p % 2]
                sc_rot.acquire(nc.tensor)
                for j in range(npair):
                    kt = kt2 + j
                    mm = nc.tensor.matmul(psc[:, j * 512:j * 512 + 512],
                                          kbf[pb:pb + 64, hc * T + kt * 128:hc * T + kt * 128 + 128],
                                          qst[pb:pb + 64, hc * 512:hc * 512 + 512],
                                          start=True, stop=True)
                    if j == npair - 1:
                        s_mm.inc(mm)
                s_mm.wait(nc.scalar)
                pt_rot.acquire(nc.scalar)
                pslot = ptb[:, (p % 3) * 1024:(p % 3) * 1024 + npair * 512]
                s_ab.inc(ctx.act.activation(pslot, psc[:, 0:npair * 512], AF.Exp,
                                            scale=0.125, bias=mcol[:, 3:4]))
                sc_rot.release(s_ab)
                need_mask = any(q0 - (kt2 + j) * 128 < 128 for j in range(npair))
                if need_mask:
                    s_ab.wait(nc.vector)
                    last_i = None
                    for j in range(npair):
                        kt = kt2 + j
                        d = q0 - kt * 128
                        if d >= 128:
                            continue
                        sl = ptb[:, (p % 3) * 1024 + j * 512:(p % 3) * 1024 + (j + 1) * 512]
                        if d < 0:
                            ctx.vec.memset(sl[:, 0:-d], 0.0)
                        mw0 = max(0, -d)
                        last_i = ctx.vec.tensor_mul(sl[:, mw0:mw0 + 128], sl[:, mw0:mw0 + 128], tril[:, :])
                    s_ev.inc(last_i)
                return need_mask

            def emit_pv(p, masked):
                kt2 = p * 2
                npair = min(2, ktmax + 1 - kt2)
                if masked:
                    s_ev.wait(nc.tensor, evmark[p])
                else:
                    s_ab.wait(nc.tensor, abmark[p])
                for j in range(npair):
                    kt = kt2 + j
                    sl = ptb[:, (p % 3) * 1024 + j * 512:(p % 3) * 1024 + (j + 1) * 512]
                    nc.tensor.matmul(po, vaug[:, kt * 1040 + h * 65:kt * 1040 + h * 65 + 64],
                                     sl, start=(kt == 0), stop=(kt == ktmax),
                                     tile_position=(0, pb), skip_group_check=True)
                    mm2 = nc.tensor.matmul(pd, ones[:, 0:64],
                                           sl, start=(kt == 0), stop=(kt == ktmax),
                                           tile_position=(0, pb), skip_group_check=True)
                    if j == npair - 1:
                        s_mm.inc(mm2)
                pt_rot.release(s_mm)

            npairs = (ktmax + 2) // 2
            maskedp = {}
            abmark = {}
            evmark = {}
            for p in range(npairs):
                maskedp[p] = emit_qk(p)
                abmark[p] = s_ab.v
                evmark[p] = s_ev.v
                if p >= 2:
                    emit_pv(p - 2, maskedp[p - 2])
            if npairs >= 2:
                emit_pv(npairs - 2, maskedp[npairs - 2])
            emit_pv(npairs - 1, maskedp[npairs - 1])
            denmark[h] = s_mm.v
            if h >= 1:
                normalize(h - 1)
        normalize(15)
        s_ev.wait(nc.tensor)
        for oc in range(NC):
            slot = oc % 2
            wo_rot.acquire(nc.sync)
            s_dl[slot].inc(nc.sync.dma_start(wostream[:, slot * 1024:slot * 1024 + 1024], woT[oc, :, :]), 16)
            s_dl[slot].wait(nc.tensor)
            ppp = (pA, pB)[slot]
            qp_rot.acquire(nc.tensor)
            for kc in range(NC):
                mm = nc.tensor.matmul(ppp[:, 512:1024],
                                      wostream[:, slot * 1024 + kc * 128:slot * 1024 + kc * 128 + 128],
                                      amid[:, kc * 512:(kc + 1) * 512],
                                      start=(kc == 0), stop=(kc == NC - 1))
                if kc == NC - 1:
                    s_mm.inc(mm)
            wo_rot.release(s_mm)
            s_mm.wait(nc.vector)
            sl = slice(oc * T + q0, oc * T + q0 + 512)
            s_ev.inc(ctx.vec.tensor_add(xres[:, sl], xres[:, sl], ppp[:, 512:1024]))
            qp_rot.release(s_ev)
        s_ev.wait(nc.tensor)
        s_ev.wait(nc.scalar)
    s_ev.wait(nc.tensor)
    s_ev.wait(nc.scalar)
    s_ev.wait(nc.sync)
    s_ev.wait(nc.gpsimd)
    stack.close()


def emit_l1ffn(ctx, bufs, sc, tag):
    nc = ctx.nc
    xres, xln = bufs["xres"], bufs["xln"]
    mu_r, mu_k = bufs["prm"]["mu_r"], bufs["prm"]["mu_k"]
    s_dl = [ctx.sem("f1_dl0"), ctx.sem("f1_dl1")]
    s_mm = ctx.sem("f1_mm")
    s_ev = ctx.sem("f1_ev")
    s_ab = ctx.sem("f1_ab")
    s_mx = ctx.sem("f1_mx")
    stack = ExitStack()
    TS = 1024
    wbuf = stack.enter_context(nc.sbuf_tensor("f1_w" + tag, [128, 2 * 1024], BF16))
    xr = stack.enter_context(nc.sbuf_tensor("f1_xr" + tag, [128, NC * TS], BF16))
    xk = stack.enter_context(nc.sbuf_tensor("f1_xk" + tag, [128, NC * TS], BF16))
    rr = stack.enter_context(nc.sbuf_tensor("f1_r" + tag, [128, NC * TS], BF16))
    ka = stack.enter_context(nc.sbuf_tensor("f1_ka" + tag, [128, NC * TS], BF16))
    tv = stack.enter_context(nc.sbuf_tensor("f1_tv" + tag, [128, 1024], F32))
    pp = [stack.enter_context(nc.psum_tensor(f"f1_pp{i}" + tag, [128, 1024], F32)) for i in range(2)]
    wrT, wkT, wvT = sc["wrT"], sc["wkT"], sc["wvT"]
    w_rot = Rot(2)
    p_rot = Rot(2)

    for strip in range(2):
        t0 = strip * TS
        for c in range(NC):
            slc = slice(c * T + t0, c * T + t0 + TS)
            for dst, mu in ((xr, mu_r), (xk, mu_k)):
                dsl = dst[:, c * TS:(c + 1) * TS]
                if t0 == 0:
                    ctx.vec.tensor_scalar_mul(dsl[:, 0:1], xln[:, c * T:c * T + 1], mu[:, c:c + 1])
                    prev = xln[:, c * T:c * T + TS - 1]
                    cur = xln[:, c * T + 1:c * T + TS]
                    out = dsl[:, 1:TS]
                else:
                    prev = xln[:, c * T + t0 - 1:c * T + t0 + TS - 1]
                    cur = xln[:, slc]
                    out = dsl[:, :]
                ctx.vec.scalar_tensor_tensor(tv[:, :out.shape[-1]], prev, mu[:, c:c + 1], prev,
                                               ALU.mult, ALU.subtract_rev if False else ALU.subtract)
                i1 = ctx.vec.scalar_tensor_tensor(out, cur, mu[:, c:c + 1], tv[:, :out.shape[-1]],
                                                    ALU.mult, ALU.subtract)
                if c == NC - 1 and dst is xk:
                    s_mx.inc(i1)
        s_mx.wait(nc.tensor)
        for (wT, src, dst, kind) in ((wrT, xr, rr, "r"), (wkT, xk, ka, "k")):
            for oc in range(NC):
                slot = oc % 2
                wofs = slot * 1024
                w_rot.acquire(nc.sync)
                s_dl[slot].inc(nc.sync.dma_start(wbuf[:, wofs:wofs + 1024], wT[oc, :, :]), 16)
                s_dl[slot].wait(nc.tensor)
                p_rot.acquire(nc.tensor)
                for n0 in (0, 512):
                    for kc in range(NC):
                        mm = nc.tensor.matmul(pp[slot][:, n0:n0 + 512],
                                              wbuf[:, wofs + kc * 128:wofs + (kc + 1) * 128],
                                              src[:, kc * TS + n0:kc * TS + n0 + 512],
                                              start=(kc == 0), stop=(kc == NC - 1))
                        if n0 == 512 and kc == NC - 1:
                            s_mm.inc(mm)
                w_rot.release(s_mm)
                osl = dst[:, oc * TS:(oc + 1) * TS]
                if kind == "r":
                    s_mm.wait(nc.scalar)
                    s_ab.inc(ctx.act.activation(osl, pp[slot][:, :], AF.Sigmoid))
                    p_rot.release(s_ab)
                else:
                    s_mm.wait(nc.vector)
                    s_ab.wait(nc.vector)
                    s_ev.inc(ctx.vec.tensor_scalar_max(tv[:, :], pp[slot][:, :], 0.0))
                    p_rot.release(s_ev)
                    s_ev.wait(nc.scalar)
                    s_ab.inc(ctx.act.activation(osl, tv[:, :], AF.Square))
        s_ab.wait(nc.tensor)
        for oc in range(NC):
            slot = oc % 2
            wofs = slot * 1024
            w_rot.acquire(nc.sync)
            s_dl[slot].inc(nc.sync.dma_start(wbuf[:, wofs:wofs + 1024], wvT[oc, :, :]), 16)
            s_dl[slot].wait(nc.tensor)
            p_rot.acquire(nc.tensor)
            for n0 in (0, 512):
                for kc in range(NC):
                    mm = nc.tensor.matmul(pp[slot][:, n0:n0 + 512],
                                          wbuf[:, wofs + kc * 128:wofs + (kc + 1) * 128],
                                          ka[:, kc * TS + n0:kc * TS + n0 + 512],
                                          start=(kc == 0), stop=(kc == NC - 1))
                    if n0 == 512 and kc == NC - 1:
                        s_mm.inc(mm)
            w_rot.release(s_mm)
            s_mm.wait(nc.vector)
            if oc == 0:
                s_ab.wait(nc.vector)
            sl = slice(oc * T + t0, oc * T + t0 + TS)
            ctx.vec.tensor_mul(tv[:, :], rr[:, oc * TS:(oc + 1) * TS], pp[slot][:, :])
            s_ev.inc(ctx.vec.scalar_tensor_tensor(xres[:, sl], tv[:, :], 0.5, xres[:, sl],
                                                    ALU.mult, ALU.add))
            p_rot.release(s_ev)
    s_ev.wait(nc.tensor)
    s_ev.wait(nc.scalar)
    s_ev.wait(nc.sync)
    s_ev.wait(nc.gpsimd)
    stack.close()


def emit_l1attn(ctx, bufs, sc, tag):
    nc = ctx.nc
    xres, xln = bufs["xres"], bufs["xln"]
    prm, consts = bufs["prm"], bufs["consts"]
    ones = consts["ones_bf"]
    ident = consts["ident_bf"]
    obc, abc, center = consts["obc"], consts["abc"], consts["center"]
    xtok = sc["xtok"]
    s_dl = [ctx.sem("a1_dl0"), ctx.sem("a1_dl1")]
    s_mm = ctx.sem("a1_mm")
    s_ev = ctx.sem("a1_ev")
    s_ab = ctx.sem("a1_ab")
    NSLOT = L1_NSLOT
    s_gd = [ctx.sem(f"a1_gd{i}") for i in range(NSLOT)]
    s_xt = [ctx.sem(f"a1_xt{i}") for i in range(2)]
    s_gp = ctx.sem("a1_gp")
    stack = ExitStack()
    pA = stack.enter_context(nc.psum_tensor("a1_pA" + tag, [128, 1024], F32))
    pB = stack.enter_context(nc.psum_tensor("a1_pB" + tag, [128, 1024], F32))
    pT = [stack.enter_context(nc.psum_tensor(f"a1_pT{i}" + tag, [128, 128], BF16)) for i in range(2)]
    amid = stack.enter_context(nc.sbuf_tensor("a1_am" + tag, [128, NC * T], BF16))
    wqt = stack.enter_context(nc.sbuf_tensor("a1_wq" + tag, [128, 16 * 152], BF16))
    fdd = stack.enter_context(nc.sbuf_tensor("a1_fd" + tag, [128, 16 * 24], F32))
    fav = stack.enter_context(nc.sbuf_tensor("a1_fa" + tag, [128, 16 * 2], F32))
    karg = stack.enter_context(nc.sbuf_tensor("a1_ka" + tag, [128, 2176], BF16))
    scr = stack.enter_context(nc.sbuf_tensor("a1_sc" + tag, [128, 264 * 6], F32))
    sml = stack.enter_context(nc.sbuf_tensor("a1_sm" + tag, [128, 64], F32))
    la = stack.enter_context(nc.sbuf_tensor("a1_la" + tag, [128, 2 * 136], F32))
    idx32 = stack.enter_context(nc.sbuf_tensor("a1_ix" + tag, [128, 4 * NS], I32))
    vg = stack.enter_context(nc.sbuf_tensor("a1_vg" + tag, [128, L1_NSLOT * 1024], BF16))
    acc = stack.enter_context(nc.sbuf_tensor("a1_ac" + tag, [128, 1024], F32))
    xts = stack.enter_context(nc.sbuf_tensor("a1_xs" + tag, [128, 2 * 1024], BF16))
    wsml = stack.enter_context(nc.sbuf_tensor("a1_ws" + tag, [128, 192 + 1024 + 2048 + 2048], BF16))
    O_WAVE, O_QUERY, O_SE1, O_SE2 = 0, 192, 192 + 1024, 192 + 1024 + 2048
    wostream = stack.enter_context(nc.sbuf_tensor("a1_wo" + tag, [128, 2 * 1024], BF16))
    brow = stack.enter_context(nc.sbuf_tensor("a1_br" + tag, [1, 152], BF16))
    kcb = stack.enter_context(nc.sbuf_tensor("a1_kc" + tag, [128, 272], F32))
    kcr = stack.enter_context(nc.sbuf_tensor("a1_kr" + tag, [1, 272], F32))

    s_dl[0].inc(nc.sync.dma_start(wsml[:, O_WAVE:O_WAVE + 192].rearrange("p (k n) -> p k n", k=8),
                                  sc["wwave"].rearrange("k p n -> p k n")), 16)
    s_dl[0].inc(nc.sync.dma_start(wsml[:, O_QUERY:O_QUERY + 1024].rearrange("p (k n) -> p k n", k=8),
                                  sc["wquery"].rearrange("k p n -> p k n")), 16)
    s_dl[0].inc(nc.sync.dma_start(wsml[:, O_SE1:O_SE1 + 2048].rearrange("p (k n) -> p k n", k=8),
                                  sc["wse1"].rearrange("k p n -> p k n")), 16)
    s_dl[0].inc(nc.sync.dma_start(wsml[:, O_SE2:O_SE2 + 2048].rearrange("p (k n) -> p k n", k=2),
                                  sc["wse2"].rearrange("k p n -> p k n")), 16)
    s_dl[0].wait(nc.vector)
    s_dl[0].wait(nc.tensor)
    ctx.vec.tensor_copy(brow[0:1, 0:24], prm["wave_b"][0:1, :])
    s_ev.inc(ctx.vec.tensor_copy(brow[0:1, 24:152], prm["query_b"][0:1, :]))
    ctx.vec.tensor_tensor(kcr[0:1, :],
                            abc[0:1, :].rearrange("p (a o) -> p a o", o=1).broadcast_to([1, NA, POS]),
                            prm["key_w"][0:1, :].rearrange("p (o d) -> p o d", o=1).broadcast_to([1, NA, POS]),
                            ALU.mult)
    s_ev.inc(ctx.vec.tensor_copy(xts[0:1, 0:272], kcr[0:1, :]))
    s_ev.wait(nc.tensor)
    mm = nc.tensor.matmul(pA[:, 0:272], ones[0:1, :], xts[0:1, 0:272],
                          start=True, stop=True)
    s_mm.inc(mm)
    s_mm.wait(nc.vector)
    s_ev.inc(ctx.vec.tensor_copy(kcb[:, :], pA[:, 0:272]))
    s_ev.wait(nc.tensor)

    xt_rot = Rot(2)
    ptrot = Rot(2)
    for tc in range(16):
        sl = tc % 2
        xt_rot.acquire(nc.vector)
        s_xt[sl].wait(nc.vector)
        for c in range(NC):
            pt = pT[c % 2]
            ptrot.acquire(nc.tensor)
            s_mm.inc(nc.tensor.transpose(pt[:, :], xln[:, c * T + tc * 128:c * T + tc * 128 + 128], ident[:, :]))
            s_mm.wait(nc.vector)
            i1 = ctx.vec.tensor_copy(xts[:, sl * 1024 + c * 128:sl * 1024 + c * 128 + 128], pt[:, :])
            s_ev.inc(i1)
            ptrot.release(s_ev)
        s_ev.wait(nc.sync)
        s_xt[sl].inc(nc.sync.dma_start(xtok[PAD + tc * 128:PAD + tc * 128 + 128, :], xts[:, sl * 1024:sl * 1024 + 1024]), 16)
        xt_rot.release(s_xt[sl])
    if L1_LIMIT < 2:
        s_xt[0].wait(nc.sync)
        s_xt[1].wait(nc.sync)
        s_ev.wait(nc.tensor); s_ev.wait(nc.scalar); s_ev.wait(nc.sync); s_ev.wait(nc.gpsimd)
        stack.close()
        return
    s_ev.wait(nc.tensor)
    wq_rot = Rot(2)
    for tc in range(16):
        ppp = (pA, pB)[tc % 2]
        wq_rot.acquire(nc.tensor)
        for kc in range(NC):
            nc.tensor.matmul(ppp[:, 0:24], xln[:, kc * T + tc * 128:kc * T + tc * 128 + 128],
                             wsml[:, O_WAVE + kc * 24:O_WAVE + kc * 24 + 24],
                             start=(kc == 0), stop=False, skip_group_check=True)
            nc.tensor.matmul(ppp[:, 24:152], xln[:, kc * T + tc * 128:kc * T + tc * 128 + 128],
                             wsml[:, O_QUERY + kc * 128:O_QUERY + kc * 128 + 128],
                             start=False, stop=False, skip_group_check=True)
        mm = nc.tensor.matmul(ppp[:, 0:152], ones[0:1, :], brow[0:1, :],
                              start=False, stop=True, skip_group_check=True)
        s_mm.inc(mm)
        s_mm.wait(nc.scalar)
        s_ab.inc(ctx.act.activation(wqt[:, tc * 152:(tc + 1) * 152], ppp[:, 0:152], AF.Silu))
        wq_rot.release(s_ab)
    if L1_LIMIT < 3:
        s_ab.wait(nc.tensor); s_ab.wait(nc.vector); s_ab.wait(nc.sync); s_ab.wait(nc.gpsimd)
        stack.close()
        return
    for tc in range(16):
        w0 = tc * 152
        f0 = tc * 24
        ctx.act.activation(fdd[:, f0:f0 + 24].rearrange("p (g e) -> p g e", g=3)[:, 0:3:2, :],
                             wqt[:, w0:w0 + 24].rearrange("p (g e) -> p g e", g=3)[:, 0:3:2, :],
                             AF.Sigmoid)
        i1 = ctx.act.activation(fdd[:, f0 + 8:f0 + 16], wqt[:, w0 + 8:w0 + 16], AF.Tanh)
        if tc == 15:
            s_ab.inc(i1)
    s_ab.wait(nc.vector)
    fview = fdd[:, :].rearrange("p (t g e) -> p t g e", t=16, g=3)
    ctx.vec.tensor_scalar(fview[:, :, 0, :], fview[:, :, 0, :], 15.0, 1.0, ALU.mult, ALU.add)
    ctx.vec.tensor_scalar_mul(fview[:, :, 1, :], fview[:, :, 1, :], 16.0)
    ctx.vec.tensor_scalar(fview[:, :, 2, :], fview[:, :, 2, :], 9.5, 0.5, ALU.mult, ALU.add)
    ctx.vec.tensor_reduce(fav[:, 0:16], fview[:, :, 0, :], AXX, ALU.add)
    ctx.vec.tensor_reduce(fav[:, 16:32], fview[:, :, 1, :], AXX, ALU.add)
    s_ev.inc(ctx.vec.tensor_scalar_mul(fav[:, 0:32], fav[:, 0:32], 0.125))
    s_ev.wait(nc.gpsimd)
    s_xt[0].wait(nc.gpsimd)
    s_xt[1].wait(nc.gpsimd)

    if L1_LIMIT < 4:
        s_ev.wait(nc.tensor); s_ev.wait(nc.scalar); s_ev.wait(nc.sync); s_ev.wait(nc.gpsimd)
        stack.close()
        return
    LOOK = 4
    NMACG = 1
    wwall = stack.enter_context(nc.sbuf_tensor("a1_ww" + tag, [128, LOOK * 264], F32))
    s_cblk = ctx.sem("a1_cb")
    s_mg = ctx.sem("a1_mg")
    s_tp2 = ctx.sem("a1_t2")
    cmark = []
    gmark = {}
    LG, ENV, WE, T33 = 0, 264, 528, 792

    def stage_c_gen(tc):
        f0 = tc * 24
        ws = (tc % LOOK) * 264
        ww = wwall[:, ws:ws + 264]
        ctx.vec.tensor_scalar_add(sml[:, 48:49], center[:, 0:1], float(tc * 128))
        ctx.vec.tensor_add(sml[:, 48:49], sml[:, 48:49], fav[:, 16 + tc:17 + tc])
        ctx.vec.scalar_tensor_tensor(scr[:, T33:T33 + NS], obc[:, :], fav[:, tc:tc + 1],
                                     sml[:, 48:49].broadcast_to([128, NS]), ALU.mult, ALU.add)
        sp = scr[:, T33:T33 + NS]
        yield
        ctx.vec.tensor_single_scalar(scr[:, T33 + NS:T33 + 2 * NS], sp, 0.0, ALU.is_ge)
        ctx.vec.tensor_single_scalar(scr[:, T33 + 2 * NS:T33 + 3 * NS], sp, float(T), ALU.is_lt)
        vm = scr[:, T33 + NS:T33 + 2 * NS]
        ctx.vec.tensor_mul(vm, vm, scr[:, T33 + 2 * NS:T33 + 3 * NS])
        yield
        fl = scr[:, T33 + 3 * NS:T33 + 4 * NS]
        ctx.vec.tensor_scalar_add(fl, sp, MAGIC)
        ctx.vec.tensor_scalar_add(fl, fl, -MAGIC)
        gt = scr[:, T33 + 4 * NS:T33 + 5 * NS]
        ctx.vec.tensor_tensor(gt, fl, sp, ALU.is_gt)
        ctx.vec.tensor_sub(fl, fl, gt)
        ctx.vec.tensor_scalar_add(fl, fl, float(PAD))
        yield
        ctx.vec.tensor_tensor(
            karg[:, :].rearrange("p (h ad) -> p h ad", h=H1),
            kcb[:, :].rearrange("p (o ad) -> p o ad", o=1).broadcast_to([128, H1, 272]),
            fdd[:, f0:f0 + 8].rearrange("p (h o) -> p h o", o=1).broadcast_to([128, H1, 272]),
            ALU.mult)
        s_ev.inc(ctx.vec.engine_nop())
        yield
        s_ev.wait(nc.scalar)
        s_ab.inc(ctx.act.activation(karg[:, :], karg[:, :], AF.Silu))
        yield
        s_ab.wait(nc.vector)
        for h in range(H1):
            q = wqt[:, tc * 152 + 24 + h * 16:tc * 152 + 24 + (h + 1) * 16]
            ctx.vec.tensor_tensor(karg[:, h * 272:(h + 1) * 272],
                                  karg[:, h * 272:(h + 1) * 272].rearrange("p (a d) -> p a d", d=POS),
                                  q.rearrange("p (o d) -> p o d", o=1).broadcast_to([128, NA, POS]),
                                  ALU.mult)
            if h == 3:
                yield
        yield
        ctx.vec.tensor_reduce(la[:, 0:136], karg[:, :].rearrange("p (ha d) -> p ha d", d=POS),
                              AXX, ALU.add)
        yield
        ctx.vec.tensor_scalar_max(sml[:, 32:40], fdd[:, f0 + 16:f0 + 24], 0.1)
        ctx.vec.reciprocal(sml[:, 32:40], sml[:, 32:40])
        ctx.vec.tensor_tensor(
            la[:, 136:272].rearrange("p (h a) -> p h a", h=H1),
            abc[:, :].rearrange("p (o a) -> p o a", o=1).broadcast_to([128, H1, NA]),
            fdd[:, f0:f0 + 8].rearrange("p (h o) -> p h o", o=1).broadcast_to([128, H1, NA]),
            ALU.mult)
        ctx.vec.tensor_tensor(
            la[:, 136:272].rearrange("p (h a) -> p h a", h=H1),
            la[:, 136:272].rearrange("p (h a) -> p h a", h=H1),
            sml[:, 32:40].rearrange("p (h o) -> p h o", o=1).broadcast_to([128, H1, NA]),
            ALU.mult)
        yield
        lg = scr[:, LG:LG + 264]
        env = scr[:, ENV:ENV + 264]
        for (srcoff, dst) in ((0, lg), (136, env)):
            ctx.vec.tensor_copy(dst[:, :].rearrange("p (h s) -> p h s", h=H1)[:, :, 0:16],
                                la[:, srcoff:srcoff + 136].rearrange("p (h a) -> p h a", h=H1)[:, :, 16:0:-1])
            ctx.vec.tensor_copy(dst[:, :].rearrange("p (h s) -> p h s", h=H1)[:, :, 16:33],
                                la[:, srcoff:srcoff + 136].rearrange("p (h a) -> p h a", h=H1)[:, :, 0:17])
            yield
        BIG = 1e30
        ctx.vec.tensor_scalar(scr[:, T33 + 2 * NS:T33 + 3 * NS], vm, BIG, -BIG, ALU.mult, ALU.add)
        mneg = scr[:, T33 + 2 * NS:T33 + 3 * NS]
        ctx.vec.tensor_tensor(lg, lg, vm.rearrange("p (o s) -> p o s", o=1).broadcast_to([128, H1, NS]), ALU.mult)
        ctx.vec.tensor_tensor(lg, lg, mneg.rearrange("p (o s) -> p o s", o=1).broadcast_to([128, H1, NS]), ALU.add)
        yield
        ctx.vec.tensor_reduce(sml[:, 0:8], lg.rearrange("p (h s) -> p h s", h=H1), AXX, ALU.max)
        ctx.vec.tensor_scalar_mul(sml[:, 0:8], sml[:, 0:8], -0.25)
        i1 = ctx.vec.scalar_tensor_tensor(
            lg, lg, 0.25,
            sml[:, 0:8].rearrange("p (h o) -> p h o", o=1).broadcast_to([128, H1, NS]),
            ALU.mult, ALU.add)
        s_ev.inc(i1)
        yield
        s_ev.wait(nc.scalar)
        ctx.act.activation(lg, lg, AF.Exp)
        s_ab.inc(ctx.act.activation(env, env, AF.Exp, scale=-1.0))
        yield
        s_ab.wait(nc.vector)
        we = scr[:, WE:WE + 264]
        ctx.vec.tensor_reduce(sml[:, 8:16], lg.rearrange("p (h s) -> p h s", h=H1), AXX, ALU.add)
        ctx.vec.tensor_mul(we, lg, env)
        yield
        ctx.vec.tensor_reduce(sml[:, 16:24], we.rearrange("p (h s) -> p h s", h=H1), AXX, ALU.add)
        ctx.vec.scalar_tensor_tensor(sml[:, 24:32], sml[:, 8:16], 1e-8, sml[:, 16:24], ALU.mult, ALU.add)
        ctx.vec.reciprocal(sml[:, 24:32], sml[:, 24:32])
        yield "hold"
        s_ab.wait(nc.vector)
        i1 = ctx.vec.tensor_tensor(
            ww[:, :].rearrange("p (h s) -> p h s", h=H1),
            we[:, :].rearrange("p (h s) -> p h s", h=H1),
            sml[:, 24:32].rearrange("p (h o) -> p h o", o=1).broadcast_to([128, H1, NS]),
            ALU.mult)
        if tc >= LOOK:
            s_gp.wait(nc.vector, gmark[tc - LOOK])
        i1 = ctx.vec.tensor_copy(idx32[:, (tc % LOOK) * NS:(tc % LOOK) * NS + NS],
                                 scr[:, T33 + 3 * NS:T33 + 4 * NS])
        s_cblk.inc(i1)
        cmark.append(s_cblk.v)

    s_mac = [ctx.sem(f"a1_mc{i}") for i in range(NSLOT)]
    slot_free_at = [0] * NSLOT
    stream = [(t, si) for t in range(16) for si in range(NS)]
    gnext = [0]

    def issue_gather_upto(k):
        while gnext[0] <= min(k, len(stream) - 1):
            tg, sg = stream[gnext[0]]
            slot = gnext[0] % NSLOT
            if sg == 0:
                s_cblk.wait(nc.gpsimd, cmark[tg])
            if slot_free_at[slot] > 0:
                nc.gpsimd.wait_ge(s_mac[slot].h, slot_free_at[slot])
            ins = nc.gpsimd.indirect_dma_start(
                vg[:, slot * 1024:(slot + 1) * 1024], None,
                xtok, IndirectOffsetOnAxis(idx32[:, (tg % LOOK) * NS + sg:(tg % LOOK) * NS + sg + 1], 0))
            s_gd[slot].inc(ins, 16)
            if sg == NS - 1:
                s_gp.inc(nc.gpsimd.engine_nop())
                gmark[tg] = s_gp.v
            gnext[0] += 1

    scl_ready = [None] * NSLOT

    def macs(tc, gen):
        held = [gen is None]

        def pump():
            if held[0]:
                return
            try:
                if next(gen) == "hold":
                    held[0] = True
            except StopIteration:
                held[0] = True

        ws = (tc % LOOK) * 264
        ww = wwall[:, ws:ws + 264]
        pacc = (pA, pB)[tc % 2]
        if tc >= 2:
            s_ab.wait(nc.tensor, tpmark[tc - 2])
        for si in range(NS):
            k = tc * NS + si
            issue_gather_upto(k + NSLOT - 1)
            slot = k % NSLOT
            on_act = (L1_SPLIT > 0 and si % L1_SPLIT == L1_SPLIT - 1)
            weng = nc.scalar if on_act else nc.vector
            s_gd[slot].wait(weng)
            vgv = vg[:, slot * 1024:(slot + 1) * 1024].rearrange("p (h d) -> p h d", h=H1)
            wbc = ww[:, :].rearrange("p (h s) -> p h s", h=H1)[:, :, si:si + 1].broadcast_to([128, H1, 128])
            if on_act:
                for h in range(H1):
                    wcol = ww[:, h * NS + si:h * NS + si + 1]
                    vs = vg[:, slot * 1024 + h * 128:slot * 1024 + (h + 1) * 128]
                    i1 = ctx.act.activation(vs, vs, AF.Copy, scale=wcol)
                s_ab.inc(i1)
                scl_ready[slot] = (s_ab, s_ab.v)
            else:
                i1 = ctx.vec.tensor_tensor(vgv, vgv, wbc, ALU.mult)
                s_ev.inc(i1)
                scl_ready[slot] = (s_ev, s_ev.v)
            sem, v = scl_ready[slot]
            nc.tensor.wait_ge(sem.h, v)
            for half in range(2):
                mm = nc.tensor.matmul(pacc[:, half * 512:(half + 1) * 512],
                                      consts["ident_bf"][:, :],
                                      vg[:, slot * 1024 + half * 512:slot * 1024 + (half + 1) * 512],
                                      start=(si == 0), stop=(si == NS - 1),
                                      skip_group_check=True)
                if half == 1:
                    s_mac[slot].inc(mm)
                    slot_free_at[slot] = s_mac[slot].v
            pump()

    tpmark = {}

    def evac(tc):
        pacc = (pA, pB)[tc % 2]
        xsl = xts[:, (tc % 2) * 1024:(tc % 2) * 1024 + 1024]
        s_mm.wait(nc.scalar)
        s_mac[(tc * NS + NS - 1) % NSLOT].wait(nc.scalar)
        i1 = ctx.act.activation(xsl, pacc[:, 0:1024], AF.Copy)
        s_ab.inc(i1)
        tpmark[tc] = s_ab.v
        s_ab.wait(nc.tensor)
        for c in range(NC):
            pt = pT[c % 2]
            ptrot.acquire(nc.tensor)
            s_mm.inc(nc.tensor.transpose(pt[:, :], xsl[:, c * 128:c * 128 + 128], ident[:, :]))
            s_mm.wait(nc.scalar)
            i1 = ctx.act.activation(amid[:, c * T + tc * 128:c * T + tc * 128 + 128], pt[:, :], AF.Copy)
            s_ab.inc(i1)
            ptrot.release(s_ab)

    for t in range(LOOK):
        for _ in stage_c_gen(t):
            pass
    for t in range(16):
        gen = stage_c_gen(t + LOOK) if t + LOOK < 16 else None
        if not L1_ILV and gen is not None:
            while next(gen, "hold") != "hold":
                pass
            macs(t, None)
        else:
            macs(t, gen)
        if gen is not None:
            for _ in gen:
                pass
        evac(t)
    s_ev.wait(nc.tensor)
    s_ab.wait(nc.tensor)
    s_ev.wait(nc.gpsimd)
    s_ab.wait(nc.gpsimd)

    if L1_LIMIT < 7:
        s_ev.wait(nc.tensor); s_ev.wait(nc.scalar); s_ev.wait(nc.sync); s_ev.wait(nc.gpsimd)
        s_gd[0].wait(nc.vector); s_gd[1].wait(nc.vector); s_gd[2].wait(nc.vector); s_gd[3].wait(nc.vector)
        stack.close()
        return
    se_rot = Rot(2)
    seh = stack.enter_context(nc.sbuf_tensor("a1_sh" + tag, [128, 2 * T], BF16))
    sec = stack.enter_context(nc.sbuf_tensor("a1_se" + tag, [128, 512], BF16))
    for st4 in range(4):
        n0 = st4 * 512
        for oc in range(2):
            ppp = (pA, pB)[oc % 2]
            se_rot.acquire(nc.tensor)
            for kc in range(NC):
                mm = nc.tensor.matmul(ppp[:, 0:512],
                                      wsml[:, O_SE1 + kc * 256 + oc * 128:O_SE1 + kc * 256 + oc * 128 + 128],
                                      amid[:, kc * T + n0:kc * T + n0 + 512],
                                      start=(kc == 0), stop=(kc == NC - 1))
                if kc == NC - 1:
                    s_mm.inc(mm)
            s_mm.wait(nc.scalar)
            s_ab.inc(ctx.act.activation(seh[:, oc * T + n0:oc * T + n0 + 512], ppp[:, 0:512],
                                          AF.Silu, bias=prm["se1_b"][:, oc:oc + 1]))
            se_rot.release(s_ab)
    s_ab.wait(nc.tensor)
    for st4 in range(4):
        n0 = st4 * 512
        for oc in range(NC):
            ppp = (pA, pB)[oc % 2]
            se_rot.acquire(nc.tensor)
            for kc in range(2):
                mm = nc.tensor.matmul(ppp[:, 0:512],
                                      wsml[:, O_SE2 + kc * 1024 + oc * 128:O_SE2 + kc * 1024 + oc * 128 + 128],
                                      seh[:, kc * T + n0:kc * T + n0 + 512],
                                      start=(kc == 0), stop=(kc == 1))
                if kc == 1:
                    s_mm.inc(mm)
            s_mm.wait(nc.scalar)
            s_ab.inc(ctx.act.activation(sec[:, 0:512], ppp[:, 0:512],
                                          AF.Sigmoid, bias=prm["se2_b"][:, oc:oc + 1]))
            s_ab.wait(nc.vector)
            sl = slice(oc * T + n0, oc * T + n0 + 512)
            s_ev.inc(ctx.vec.tensor_mul(amid[:, sl], amid[:, sl], sec[:, 0:512]))
            se_rot.release(s_ev)
            s_ev.wait(nc.scalar)
    s_ev.wait(nc.tensor)
    woutw = sc["woutw"]
    wo_rot = Rot(2)
    for st4 in range(4):
        n0 = st4 * 512
        for oc in range(NC):
            slot = oc % 2
            wo_rot.acquire(nc.sync)
            s_dl[slot].inc(nc.sync.dma_start(wostream[:, slot * 1024:(slot + 1) * 1024], woutw[oc, :, :]), 16)
            s_dl[slot].wait(nc.tensor)
            ppp = (pA, pB)[slot]
            se_rot.acquire(nc.tensor)
            for kc in range(NC):
                mm = nc.tensor.matmul(ppp[:, 0:512],
                                      wostream[:, slot * 1024 + kc * 128:slot * 1024 + kc * 128 + 128],
                                      amid[:, kc * T + n0:kc * T + n0 + 512],
                                      start=(kc == 0), stop=(kc == NC - 1))
                if kc == NC - 1:
                    s_mm.inc(mm)
            wo_rot.release(s_mm)
            s_mm.wait(nc.scalar)
            s_ab.inc(ctx.act.activation(sec[:, 0:512], ppp[:, 0:512], AF.Silu))
            s_ab.wait(nc.vector)
            sl = slice(oc * T + n0, oc * T + n0 + 512)
            s_ev.inc(ctx.vec.tensor_add(xres[:, sl], xres[:, sl], sec[:, 0:512]))
            se_rot.release(s_ev)
            s_ev.wait(nc.scalar)
    s_ev.wait(nc.tensor)
    s_ev.wait(nc.scalar)
    s_ev.wait(nc.sync)
    s_ev.wait(nc.gpsimd)
    stack.close()


def build_program(phases, n_seq=NSEQ):
    nc = bass.Bass("TRN2", target_bir_lowering=False, detect_race_conditions=False)
    stack = ExitStack()
    ctx = Ctx(nc, stack)
    io = declare_io(nc, phases)
    sc = declare_scratch(nc, phases)

    ctx.vec
    ctx.act
    consts = emit_consts(ctx, io, phases)
    prm = emit_params(ctx, io, phases)
    barrier(ctx)
    emit_preprocess(ctx, io, sc, phases, consts)
    barrier(ctx)

    xres = ctx.sb("xres", [128, NC * T], F32)
    xln = ctx.sb("xln", [128, NC * T], BF16)
    bufs = {"xres": xres, "xln": xln, "consts": consts, "prm": prm, "io": io}

    sx = ctx.sem("x_dl")
    sy = ctx.sem("y_st")
    for s in range(n_seq):
        for c in range(NC):
            sx.inc(nc.sync.dma_start(xres[:, c * T:(c + 1) * T], io["x"][s, c * 128:(c + 1) * 128, :]), 16)
        for eng in (nc.vector, nc.scalar, nc.tensor, nc.gpsimd, nc.sync):
            sx.wait(eng)
        for pi, ph in enumerate(phases):
            tag = f"_s{s}p{pi}"
            emit_ln(ctx, bufs, tag)
            if ph == "l0ffn":
                emit_l0ffn(ctx, bufs, sc, tag)
            elif ph == "l0attn":
                emit_l0attn(ctx, bufs, sc, tag)
            elif ph == "l1ffn":
                emit_l1ffn(ctx, bufs, sc, tag)
            elif ph == "l1attn":
                emit_l1attn(ctx, bufs, sc, tag)
            else:
                raise ValueError(ph)
        for c in range(NC):
            sy.inc(nc.sync.dma_start(io["y"][s, c * 128:(c + 1) * 128, :], xres[:, c * T:(c + 1) * T]), 16)
        if s < n_seq - 1:
            sy.wait(nc.sync)
    sy.wait(nc.sync)
    stack.close()
    return nc


ALL_PHASES = ("l0ffn", "l0attn", "l0ffn", "l1ffn", "l1attn", "l1ffn")


def _run(phases, inputs, x_override=None, cores=8):
    x = inputs["L0_attn_x"] if x_override is None else x_override
    xt = np.ascontiguousarray(x.transpose(0, 2, 1))
    nc = build_program(phases)
    in_maps = [{"x": np.ascontiguousarray(xt[ci * NSEQ:(ci + 1) * NSEQ])} for ci in range(cores)]
    for m in in_maps:
        for n in _needed_inputs(phases):
            m[n] = np.ascontiguousarray(inputs[n])
    res = run_bass_kernel_spmd(nc, in_maps, core_ids=list(range(cores)))
    out = np.empty((cores * NSEQ, T, D), np.float32)
    for ci in range(cores):
        y = res.results[ci]["y"]
        out[ci * NSEQ:(ci + 1) * NSEQ] = y.transpose(0, 2, 1)
    return out


def kernel(**inputs):
    return _run(ALL_PHASES, inputs)


# version 18
# speedup vs baseline: 1.1251x; 1.0266x over previous
import numpy as np
import ml_dtypes
from contextlib import ExitStack

import concourse.bass as bass
import concourse.mybir as mybir
from concourse.bass_utils import run_bass_kernel_spmd
from concourse.bass import IndirectOffsetOnAxis

F32 = mybir.dt.float32
BF16 = mybir.dt.bfloat16
I32 = mybir.dt.int32
I16 = mybir.dt.int16
U8 = mybir.dt.uint8
AF = mybir.ActivationFunctionType
ALU = mybir.AluOpType
AXX = mybir.AxisListType.X

D = 1024
T = 2048
NSEQ = 4
NC = 8
HID = 3072
HC = HID // 128
R127 = float(np.float32(1.0 / 127.0))
MAGIC = 8388608.0
LNEPS = 1e-5

H1 = 8
POS = 16
NS = 33
NA = 17
WIN = 672
PAD = 272
TPAD = T + 2 * PAD
L1_LIMIT = 99
L1_NOMAC = False
L1_FIXIDX = False
L1_NGATHER = NS
L1_NSLOT = 8
L1_SPLIT = 4
L1_ILV = True


class Sem:
    def __init__(self, h):
        self.h = h
        self.v = 0

    def inc(self, ins, n=1):
        ins.then_inc(self.h, n)
        self.v += n
        return ins

    def wait(self, eng, v=None):
        eng.wait_ge(self.h, self.v if v is None else v)


class Rot:

    def __init__(self, n):
        self.n = n
        self.q = []

    def acquire(self, *engs):
        if len(self.q) >= self.n:
            sem, v = self.q.pop(0)
            for e in engs:
                e.wait_ge(sem.h, v)

    def release(self, sem):
        self.q.append((sem, sem.v))


class Ctx:
    def __init__(self, nc, stack):
        self.nc = nc
        self.stack = stack
        self._sems = {}

    def sem(self, name) -> Sem:
        if name not in self._sems:
            self._sems[name] = Sem(self.stack.enter_context(self.nc.semaphore(name)))
        return self._sems[name]

    def sb(self, name, shape, dtype):
        return self.stack.enter_context(self.nc.sbuf_tensor(name, shape, dtype))

    @property
    def vec(self):
        if not hasattr(self, "_vec"):
            self._vec = Guarded(self, self.nc.vector, "vgd")
        return self._vec

    @property
    def act(self):
        if not hasattr(self, "_act"):
            self._act = Guarded(self, self.nc.scalar, "agd")
        return self._act


_GUARD_OPS = {
    "tensor_copy", "tensor_tensor", "tensor_add", "tensor_sub", "tensor_mul",
    "tensor_max", "tensor_scalar", "tensor_single_scalar", "tensor_scalar_add",
    "tensor_scalar_sub", "tensor_scalar_mul", "tensor_scalar_max",
    "tensor_scalar_min", "tensor_relu", "tensor_reduce", "reduce_max",
    "reduce_sum", "scalar_tensor_tensor", "reciprocal", "select", "memset",
    "activation", "copy", "mul", "add", "square", "sqrt", "sign",
}


class Guarded:

    SHORT = 64

    def __init__(self, ctx, eng, name):
        self._ctx = ctx
        self._eng = eng
        self._name = name
        self._recent = []
        self._pad = ctx.sb("gpad_" + name, [128, 128], F32)
        if hasattr(eng, "memset"):
            eng.memset(self._pad[:, :], 0.0)
        else:
            eng.memzero(self._pad[:, :])

    def _gap(self):
        if self._name.startswith("a"):
            self._eng.activation(self._pad[:, 0:64], self._pad[:, 64:128], AF.Copy)
        else:
            self._eng.tensor_copy(self._pad[:, 0:64], self._pad[:, 64:128])

    def __getattr__(self, attr):
        real = getattr(self._eng, attr)
        if attr not in _GUARD_OPS:
            return real

        def wrap(*args, **kw):
            aps = [a for a in list(args) + list(kw.values()) if isinstance(a, bass.AP)]
            out = aps[0] if aps else None
            ins = aps[1:]
            innames = {a.tensor.name for a in ins}
            if out is not None and attr != "memset":
                innames.add(out.tensor.name)
            if any(n in self._recent for n in innames):
                self._gap()
                self._recent = []
            ins_res = real(*args, **kw)
            if out is not None and out.free_size() < self.SHORT:
                self._recent.append(out.tensor.name)
                self._recent = self._recent[-2:]
            else:
                self._recent = []
            return ins_res

        return wrap


def barrier(ctx):
    nc = ctx.nc
    with nc.Block() as blk:
        @blk.vector
        def _(v):
            v.engine_nop()


TRANSPOSED = {
    "wgT": ("L0_ffn_w_gate", HID, D, True),
    "wuT": ("L0_ffn_w_up", HID, D, True),
    "wdT": ("L0_ffn_w_down", D, HID, True),
    "woT": ("L0_attn_W_o", D, D, False),
    "wrT": ("L1_ffn_W_r", D, D, False),
    "wkT": ("L1_ffn_W_k", D, D, False),
    "wvT": ("L1_ffn_W_v", D, D, False),
}
NATURAL = {
    "wdq": ("L0_attn_W_dq", D, 256),
    "wuq": ("L0_attn_W_uq", 256, D),
    "wdkv": ("L0_attn_W_dkv", D, 256),
    "wukv": ("L0_attn_W_ukv", 256, 2 * D),
    "wwave": ("L1_attn_wave_w", D, 24),
    "wquery": ("L1_attn_query_w", D, 128),
    "wse1": ("L1_attn_se1_w", D, 256),
    "wse2": ("L1_attn_se2_w", 256, D),
    "woutw": ("L1_attn_out_w", D, D),
}
SHAPES = {
    "L0_attn_W_dq": ([D, 256], F32), "L0_attn_W_uq": ([256, D], F32),
    "L0_attn_q_ln_g": ([256], F32), "L0_attn_q_ln_b": ([256], F32),
    "L0_attn_W_dkv": ([D, 256], F32), "L0_attn_W_ukv": ([256, 2 * D], F32),
    "L0_attn_kv_ln_g": ([256], F32), "L0_attn_kv_ln_b": ([256], F32),
    "L0_attn_W_o": ([D, D], F32),
    "L0_ffn_w_gate": ([HID, D], I32), "L0_ffn_w_up": ([HID, D], I32),
    "L0_ffn_w_down": ([D, HID], I32),
    "L1_attn_wave_w": ([D, 24], F32), "L1_attn_wave_b": ([24], F32),
    "L1_attn_query_w": ([D, 128], F32), "L1_attn_query_b": ([128], F32),
    "L1_attn_key_w": ([POS], F32), "L1_attn_out_w": ([D, D], F32),
    "L1_attn_se1_w": ([D, 256], F32), "L1_attn_se1_b": ([256], F32),
    "L1_attn_se2_w": ([256, D], F32), "L1_attn_se2_b": ([D], F32),
    "L1_ffn_W_r": ([D, D], F32), "L1_ffn_W_k": ([D, D], F32),
    "L1_ffn_W_v": ([D, D], F32),
    "L1_ffn_mu_r": ([D], F32), "L1_ffn_mu_k": ([D], F32),
}


def _needed_inputs(phases):
    need = set()
    if "l0ffn" in phases:
        need |= {"L0_ffn_w_gate", "L0_ffn_w_up", "L0_ffn_w_down"}
    if "l0attn" in phases:
        need |= {"L0_attn_W_dq", "L0_attn_W_uq", "L0_attn_q_ln_g", "L0_attn_q_ln_b",
                 "L0_attn_W_dkv", "L0_attn_W_ukv", "L0_attn_kv_ln_g", "L0_attn_kv_ln_b",
                 "L0_attn_W_o"}
    if "l1ffn" in phases:
        need |= {"L1_ffn_W_r", "L1_ffn_W_k", "L1_ffn_W_v", "L1_ffn_mu_r", "L1_ffn_mu_k"}
    if "l1attn" in phases:
        need |= {"L1_attn_wave_w", "L1_attn_wave_b", "L1_attn_query_w", "L1_attn_query_b",
                 "L1_attn_key_w", "L1_attn_out_w", "L1_attn_se1_w", "L1_attn_se1_b",
                 "L1_attn_se2_w", "L1_attn_se2_b"}
    return sorted(need)


def declare_io(nc, phases):
    io = {}
    io["x"] = nc.dram_tensor("x", [NSEQ, D, T], F32, kind="ExternalInput").ap()
    io["y"] = nc.dram_tensor("y", [NSEQ, D, T], F32, kind="ExternalOutput").ap()
    for n in _needed_inputs(phases):
        sh, dt = SHAPES[n]
        io[n] = nc.dram_tensor(n, sh, dt, kind="ExternalInput").ap()
    return io


def declare_scratch(nc, phases):
    sc = {}
    tneed = []
    nneed = []
    if "l0ffn" in phases:
        tneed += ["wgT", "wuT", "wdT"]
    if "l0attn" in phases:
        tneed += ["woT"]
        nneed += ["wdq", "wuq", "wdkv", "wukv"]
    if "l1ffn" in phases:
        tneed += ["wrT", "wkT", "wvT"]
    if "l1attn" in phases:
        nneed += ["wwave", "wquery", "wse1", "wse2"]
        sc["woutw"] = nc.dram_tensor("sc_woutw", [D // 128, 128, D], BF16, kind="Internal").ap()
        sc["xtok"] = nc.dram_tensor("sc_xtok", [TPAD, D], BF16, kind="Internal").ap()
    for n in tneed:
        R, C = TRANSPOSED[n][1], TRANSPOSED[n][2]
        sc[n] = nc.dram_tensor("sc_" + n, [R // 128, 128, C], BF16, kind="Internal").ap()
    for n in nneed:
        K, N = NATURAL[n][1], NATURAL[n][2]
        sc[n] = nc.dram_tensor("sc_" + n, [K // 128, 128, N], BF16, kind="Internal").ap()
    return sc


def emit_consts(ctx, io, phases):
    nc = ctx.nc
    consts = {}
    ones_bf = ctx.sb("ones_bf", [128, 128], BF16)
    ident_bf = ctx.sb("ident_bf", [128, 128], BF16)
    epscol = ctx.sb("epscol", [128, 1], F32)
    consts["ones_bf"] = ones_bf
    consts["ident_bf"] = ident_bf
    consts["epscol"] = epscol
    scw = ctx.sem("c_cw")
    ctx.vec.memset(epscol[:, :], LNEPS)
    ctx.vec.memset(ident_bf[:, :], 0.0)
    scw.inc(ctx.vec.memset(ones_bf[:, :], 1.0))
    scw.wait(nc.gpsimd)
    scw.inc(nc.gpsimd.affine_select(ident_bf[:, :], ones_bf[:, :], [[-1, 128]],
                                    ALU.is_equal, 0.0, base=0, channel_multiplier=1))
    if "l0attn" in phases:
        tril = ctx.sb("tril_bf", [128, 128], BF16)
        consts["tril"] = tril
        scw.inc(nc.gpsimd.affine_select(tril[:, :], ones_bf[:, :], [[-1, 128]],
                                        ALU.is_ge, 0.0, base=0, channel_multiplier=1))
    if "l1attn" in phases:
        obc = ctx.sb("obc", [128, NS], F32)
        abc = ctx.sb("abc", [128, NA], F32)
        center = ctx.sb("center", [128, 1], F32)
        iota16 = ctx.sb("iota16", [128, 128], mybir.dt.float16)
        consts["obc"], consts["abc"], consts["center"] = obc, abc, center
        consts["iota16"] = iota16
        scw.inc(nc.gpsimd.iota(iota16[:, :], [[1, 128]], base=0, channel_multiplier=0,
                               allow_small_or_imprecise_dtypes=True))
        scw.inc(nc.gpsimd.iota(obc[:, :], [[1, NS]], base=-16, channel_multiplier=0,
                               allow_small_or_imprecise_dtypes=True))
        scw.inc(nc.gpsimd.iota(abc[:, :], [[1, NA]], base=0, channel_multiplier=0,
                               allow_small_or_imprecise_dtypes=True))
        scw.inc(nc.gpsimd.iota(center[:, :], [[0, 1]], base=0, channel_multiplier=1,
                               allow_small_or_imprecise_dtypes=True))
    scw.wait(nc.tensor)
    scw.wait(nc.vector)
    scw.wait(nc.scalar)
    return consts


def emit_params(ctx, io, phases):
    nc = ctx.nc
    prm = {}
    sdl = ctx.sem("prm_dl")

    def col(name, src_ap, n):
        t = ctx.sb("p_" + name, [128, max(1, n // 128)], F32)
        sdl.inc(nc.sync.dma_start(t[:, :], src_ap.rearrange("(c p) -> p c", p=128),
                                  allow_slow_non_contiguous=True), 16)
        prm[name] = t

    def row(name, src_ap, n):
        t = ctx.sb("r_" + name, [1, n], F32)
        sdl.inc(nc.sync.dma_start(t[0:1, :], src_ap.rearrange("(o n) -> o n", o=1)), 16)
        prm[name] = t

    if "l0attn" in phases:
        col("q_ln_g", io["L0_attn_q_ln_g"], 256)
        col("q_ln_b", io["L0_attn_q_ln_b"], 256)
        col("kv_ln_g", io["L0_attn_kv_ln_g"], 256)
        col("kv_ln_b", io["L0_attn_kv_ln_b"], 256)
    if "l1ffn" in phases:
        col("mu_r", io["L1_ffn_mu_r"], D)
        col("mu_k", io["L1_ffn_mu_k"], D)
    if "l1attn" in phases:
        row("wave_b", io["L1_attn_wave_b"], 24)
        row("query_b", io["L1_attn_query_b"], 128)
        row("key_w", io["L1_attn_key_w"], POS)
        col("se1_b", io["L1_attn_se1_b"], 256)
        col("se2_b", io["L1_attn_se2_b"], D)
    for eng in (nc.vector, nc.scalar, nc.tensor, nc.gpsimd):
        sdl.wait(eng)
    return prm


def emit_preprocess(ctx, io, sc, phases, consts):
    nc = ctx.nc
    s_ld = [ctx.sem("pp_ld0"), ctx.sem("pp_ld1")]
    s_cv = ctx.sem("pp_cv")
    s_tp = ctx.sem("pp_tp")
    s_cp = ctx.sem("pp_cp")
    s_st = [ctx.sem("pp_st0"), ctx.sem("pp_st1")]
    stack = ExitStack()
    sload = stack.enter_context(nc.sbuf_tensor("pp_load", [128, 2 * HID], F32))
    sloadi = stack.enter_context(nc.sbuf_tensor("pp_loadi", [128, 2 * HID], I32))
    sbf = stack.enter_context(nc.sbuf_tensor("pp_bf", [128, 2 * HID], BF16))
    sstage = stack.enter_context(nc.sbuf_tensor("pp_stage", [128, 2 * HID], BF16))
    ppt = [stack.enter_context(nc.psum_tensor(f"pp_pt{i}", [128, 128], BF16)) for i in range(4)]
    ident = consts["ident_bf"]

    load_rot = Rot(2)
    pt_rot = Rot(2)

    idx = 0
    for n in [t for t in TRANSPOSED if t in sc]:
        src_name, R, C, is_int = TRANSPOSED[n]
        src, dst = io[src_name], sc[n]
        for rc in range(R // 128):
            sl_i = idx % 2
            b = sl_i * HID
            qld = (nc.sync, nc.scalar)[sl_i]
            load_rot.acquire(qld)
            if is_int:
                s_ld[sl_i].inc(qld.dma_start(sloadi[:, b:b + C], src[rc * 128:(rc + 1) * 128, :]), 16)
            else:
                s_ld[sl_i].inc(qld.dma_start(sload[:, b:b + C], src[rc * 128:(rc + 1) * 128, :]), 16)
            s_ld[sl_i].wait(nc.vector)
            s_st[sl_i].wait(nc.vector)
            if is_int:
                s_cv.inc(ctx.vec.tensor_scalar_mul(sbf[:, b:b + C], sloadi[:, b:b + C], R127))
            else:
                s_cv.inc(ctx.vec.tensor_copy(sbf[:, b:b + C], sload[:, b:b + C]))
            load_rot.release(s_cv)
            s_cv.wait(nc.tensor)
            ncc = C // 128
            tmarks = {}
            for cc in range(ncc + 2):
                if cc < ncc:
                    pt = ppt[cc % 4]
                    pt_rot.acquire(nc.tensor)
                    s_tp.inc(nc.tensor.transpose(pt[:, :], sbf[:, b + cc * 128:b + (cc + 1) * 128], ident[:, :]))
                    tmarks[cc] = s_tp.v
                if cc >= 2:
                    cp = cc - 2
                    s_tp.wait(nc.vector, tmarks[cp])
                    s_cp.inc(ctx.vec.tensor_copy(sstage[:, b + cp * 128:b + (cp + 1) * 128], ppt[cp % 4][:, :]))
                    pt_rot.release(s_cp)
            qst = nc.gpsimd
            s_cp.wait(qst)
            s_st[sl_i].inc(qst.dma_start(dst[rc, :, :], sstage[:, b:b + C]), 16)
            idx += 1

    for n in [t for t in NATURAL if t in sc]:
        src_name, K, N = NATURAL[n]
        src, dst = io[src_name], sc[n]
        grouped = (n == "woutw")
        for kc in range(K // 128):
            sl_i = idx % 2
            b = sl_i * HID
            qld = (nc.sync, nc.scalar)[sl_i]
            qst = nc.gpsimd
            load_rot.acquire(qld)
            s_ld[sl_i].inc(qld.dma_start(sload[:, b:b + N], src[kc * 128:(kc + 1) * 128, :]), 16)
            s_ld[sl_i].wait(nc.vector)
            s_st[sl_i].wait(nc.vector)
            s_cv.inc(ctx.vec.tensor_copy(sbf[:, b:b + N], sload[:, b:b + N]))
            load_rot.release(s_cv)
            s_cv.wait(qst)
            if grouped:
                s_st[sl_i].inc(qst.dma_start(
                    dst[:, :, kc * 128:(kc + 1) * 128].rearrange("o p j -> p o j"),
                    sbf[:, b:b + N].rearrange("p (o j) -> p o j", j=128)), 16)
            else:
                s_st[sl_i].inc(qst.dma_start(dst[kc, :, :], sbf[:, b:b + N]), 16)
            idx += 1
    if "xtok" in sc:
        zt = stack.enter_context(nc.sbuf_tensor("pp_zt", [128, D], BF16))
        s_cv.inc(ctx.vec.memset(zt[:, :], 0.0))
        s_cv.wait(nc.sync)
        xt = sc["xtok"]
        s_st[0].inc(nc.sync.dma_start(xt[0:128, :], zt[:, :]), 16)
        s_st[0].inc(nc.sync.dma_start(xt[128:256, :], zt[:, :]), 16)
        s_st[0].inc(nc.sync.dma_start(xt[256:PAD, :], zt[0:16, :]), 16)
        s_st[0].inc(nc.sync.dma_start(xt[T + PAD:T + PAD + 128, :], zt[:, :]), 16)
        s_st[0].inc(nc.sync.dma_start(xt[T + PAD + 128:T + PAD + 256, :], zt[:, :]), 16)
        s_st[0].inc(nc.sync.dma_start(xt[T + PAD + 256:TPAD, :], zt[0:16, :]), 16)
    s_st[0].wait(nc.sync)
    s_st[1].wait(nc.sync)
    stack.close()


def emit_ln(ctx, bufs, tag):
    nc = ctx.nc
    xres, xln = bufs["xres"], bufs["xln"]
    ones = bufs["consts"]["ones_bf"]
    eps = bufs["consts"]["epscol"]
    s_pp = ctx.sem("ln_pp")
    s_mm = ctx.sem("ln_mm")
    s_ev = ctx.sem("ln_ev")
    s_vv = ctx.sem("ln_vv")
    stack = ExitStack()
    xbf = stack.enter_context(nc.sbuf_tensor("ln_xbf" + tag, [128, NC * T], BF16))
    xsq = stack.enter_context(nc.sbuf_tensor("ln_xsq" + tag, [128, NC * T], BF16))
    mub = stack.enter_context(nc.sbuf_tensor("ln_mu" + tag, [128, T], BF16))
    rstdb = stack.enter_context(nc.sbuf_tensor("ln_rstd" + tag, [128, T], BF16))
    muf = stack.enter_context(nc.sbuf_tensor("ln_muf" + tag, [128, 2 * 512], F32))
    tmp = stack.enter_context(nc.sbuf_tensor("ln_tmp" + tag, [128, 2 * 512], F32))
    msq = stack.enter_context(nc.sbuf_tensor("ln_ms" + tag, [128, 2 * 512], F32))
    tmpb = stack.enter_context(nc.sbuf_tensor("ln_tb" + tag, [128, T], BF16))
    ps = stack.enter_context(nc.psum_tensor("ln_ps" + tag, [128, 1024], F32))
    pq = stack.enter_context(nc.psum_tensor("ln_pq" + tag, [128, 1024], F32))
    ppmark = {}
    sqmark = {}
    for st in range(4):
        n0 = st * 512
        for c in range(NC):
            i1 = ctx.act.activation(xbf[:, c * T + n0:c * T + n0 + 512],
                                    xres[:, c * T + n0:c * T + n0 + 512], AF.Copy)
        s_pp.inc(i1)
        ppmark[st] = s_pp.v
    s_pp.wait(nc.vector, ppmark[0])
    for st in range(4):
        n0 = st * 512
        if st > 0:
            s_pp.wait(nc.vector, ppmark[st])
        for c in range(NC):
            sl = slice(c * T + n0, c * T + n0 + 512)
            i1 = ctx.vec.tensor_mul(xsq[:, sl], xbf[:, sl], xbf[:, sl])
        s_vv.inc(i1)
        sqmark[st] = s_vv.v
    emark = {}
    amark = {}
    for st in range(4):
        n0 = st * 512
        c0 = (st % 2) * 512
        p0 = (st % 2) * 512
        if st >= 2:
            s_vv.wait(nc.tensor, amark[st - 2])
        s_pp.wait(nc.tensor, ppmark[st])
        for c in range(NC):
            nc.tensor.matmul(ps[:, c0:c0 + 512], ones[:, :], xbf[:, c * T + n0:c * T + n0 + 512],
                             start=(c == 0), stop=(c == NC - 1))
        s_vv.wait(nc.tensor, sqmark[st])
        for c in range(NC):
            mm = nc.tensor.matmul(pq[:, c0:c0 + 512], ones[:, :], xsq[:, c * T + n0:c * T + n0 + 512],
                                  start=(c == 0), stop=(c == NC - 1))
            if c == NC - 1:
                s_mm.inc(mm)
        s_mm.wait(nc.vector)
        if st >= 2:
            s_ev.wait(nc.vector, emark[st - 2])
        ctx.vec.tensor_scalar_mul(muf[:, p0:p0 + 512], ps[:, c0:c0 + 512], 1.0 / D)
        ctx.vec.tensor_scalar_mul(tmp[:, p0:p0 + 512], pq[:, c0:c0 + 512], 1.0 / D)
        ctx.vec.tensor_mul(msq[:, p0:p0 + 512], muf[:, p0:p0 + 512], muf[:, p0:p0 + 512])
        i1 = ctx.vec.tensor_sub(tmp[:, p0:p0 + 512], tmp[:, p0:p0 + 512], msq[:, p0:p0 + 512])
        s_vv.inc(i1)
        s_vv.wait(nc.scalar)
        ctx.act.activation(mub[:, n0:n0 + 512], muf[:, p0:p0 + 512], AF.Copy)
        ctx.act.activation(tmp[:, p0:p0 + 512], tmp[:, p0:p0 + 512], AF.Ln, bias=eps[:, 0:1])
        i1 = ctx.act.activation(rstdb[:, n0:n0 + 512], tmp[:, p0:p0 + 512], AF.Exp, scale=-0.5)
        s_ev.inc(i1)
        emark[st] = s_ev.v
        s_ev.wait(nc.vector, emark[st])
        for c in range(NC):
            sl = slice(c * T + n0, c * T + n0 + 512)
            ctx.vec.tensor_sub(tmpb[:, n0:n0 + 512], xbf[:, sl], mub[:, n0:n0 + 512])
            i1 = ctx.vec.tensor_mul(xln[:, sl], tmpb[:, n0:n0 + 512], rstdb[:, n0:n0 + 512])
            if c == NC - 1:
                s_vv.inc(i1)
        amark[st] = s_vv.v
    s_vv.wait(nc.tensor)
    s_vv.wait(nc.scalar)
    s_vv.wait(nc.sync)
    s_vv.wait(nc.gpsimd)
    s_ev.wait(nc.tensor)
    s_pp.wait(nc.tensor)
    stack.close()


def emit_l0ffn(ctx, bufs, sc, tag):
    nc = ctx.nc
    xres, xln = bufs["xres"], bufs["xln"]
    s_dl = [ctx.sem("f0_dl0"), ctx.sem("f0_dl1")]
    s_dd = [ctx.sem("f0_dd0"), ctx.sem("f0_dd1")]
    s_mm = ctx.sem("f0_mm")
    s_ev = ctx.sem("f0_ev")
    s_ab = ctx.sem("f0_ab")
    stack = ExitStack()
    wbuf = stack.enter_context(nc.sbuf_tensor("f0_w" + tag, [128, 2 * 2048], BF16))
    wdbuf = stack.enter_context(nc.sbuf_tensor("f0_wd" + tag, [128, 2 * HID], BF16))
    hbuf = stack.enter_context(nc.sbuf_tensor("f0_h" + tag, [128, HC * 1024], BF16))
    sg = stack.enter_context(nc.sbuf_tensor("f0_sg" + tag, [128, 2 * 1024], BF16))
    pg = [stack.enter_context(nc.psum_tensor(f"f0_pg{i}" + tag, [128, 1024], F32)) for i in range(2)]
    pu = [stack.enter_context(nc.psum_tensor(f"f0_pu{i}" + tag, [128, 1024], F32)) for i in range(2)]

    wgT, wuT, wdT = sc["wgT"], sc["wuT"], sc["wdT"]
    w_rot = Rot(2)
    p_rot = Rot(2)
    sg_rot = Rot(2)
    wd_rot = Rot(2)

    for strip in range(2):
        t0 = strip * 1024
        for hc in range(HC):
            slot = hc % 2
            wofs = slot * 2048
            w_rot.acquire(nc.sync)
            s_dl[slot].inc(nc.sync.dma_start(wbuf[:, wofs:wofs + 1024], wgT[hc, :, :]), 16)
            s_dl[slot].inc(nc.sync.dma_start(wbuf[:, wofs + 1024:wofs + 2048], wuT[hc, :, :]), 16)
            s_dl[slot].wait(nc.tensor)
            p_rot.acquire(nc.tensor)
            for n0 in (0, 512):
                for kc in range(NC):
                    nc.tensor.matmul(pg[slot][:, n0:n0 + 512],
                                     wbuf[:, wofs + kc * 128:wofs + (kc + 1) * 128],
                                     xln[:, kc * T + t0 + n0:kc * T + t0 + n0 + 512],
                                     start=(kc == 0), stop=(kc == NC - 1))
            for n0 in (0, 512):
                for kc in range(NC):
                    mm = nc.tensor.matmul(pu[slot][:, n0:n0 + 512],
                                          wbuf[:, wofs + 1024 + kc * 128:wofs + 1024 + (kc + 1) * 128],
                                          xln[:, kc * T + t0 + n0:kc * T + t0 + n0 + 512],
                                          start=(kc == 0), stop=(kc == NC - 1))
                    if n0 == 512 and kc == NC - 1:
                        s_mm.inc(mm)
            w_rot.release(s_mm)
            s_mm.wait(nc.scalar)
            sg_rot.acquire(nc.scalar)
            s_ab.inc(ctx.act.activation(sg[:, slot * 1024:(slot + 1) * 1024], pg[slot][:, :], AF.Silu))
            s_ab.wait(nc.vector)
            s_ev.inc(ctx.vec.tensor_mul(hbuf[:, hc * 1024:(hc + 1) * 1024],
                                          sg[:, slot * 1024:(slot + 1) * 1024], pu[slot][:, :]))
            sg_rot.release(s_ev)
            p_rot.release(s_ev)
        for oc in range(NC):
            slot = oc % 2
            wofs = slot * HID
            wd_rot.acquire(nc.sync)
            s_dd[slot].inc(nc.sync.dma_start(wdbuf[:, wofs:wofs + HID], wdT[oc, :, :]), 16)
            s_dd[slot].wait(nc.tensor)
            s_ev.wait(nc.tensor)
            p_rot.acquire(nc.tensor)
            for n0 in (0, 512):
                for kc in range(HC):
                    mm = nc.tensor.matmul(pg[slot][:, n0:n0 + 512],
                                          wdbuf[:, wofs + kc * 128:wofs + (kc + 1) * 128],
                                          hbuf[:, kc * 1024 + n0:kc * 1024 + n0 + 512],
                                          start=(kc == 0), stop=(kc == HC - 1))
                    if n0 == 512 and kc == HC - 1:
                        s_mm.inc(mm)
            wd_rot.release(s_mm)
            s_mm.wait(nc.vector)
            sl = slice(oc * T + t0, oc * T + t0 + 1024)
            s_ev.inc(ctx.vec.scalar_tensor_tensor(xres[:, sl], pg[slot][:, :], 0.5, xres[:, sl],
                                                    ALU.mult, ALU.add))
            p_rot.release(s_ev)
    s_ev.wait(nc.tensor)
    s_ev.wait(nc.scalar)
    s_ev.wait(nc.sync)
    s_ev.wait(nc.gpsimd)
    stack.close()


def emit_l0attn(ctx, bufs, sc, tag):
    nc = ctx.nc
    xres, xln = bufs["xres"], bufs["xln"]
    prm, consts = bufs["prm"], bufs["consts"]
    ones, tril = consts["ones_bf"], consts["tril"]
    s_dl = [ctx.sem("a0_dl0"), ctx.sem("a0_dl1")]
    s_mm = ctx.sem("a0_mm")
    s_ev = ctx.sem("a0_ev")
    s_ab = ctx.sem("a0_ab")
    stack = ExitStack()
    pA = stack.enter_context(nc.psum_tensor("a0_pA" + tag, [128, 1024], F32))
    pB = stack.enter_context(nc.psum_tensor("a0_pB" + tag, [128, 1024], F32))
    pS = stack.enter_context(nc.psum_tensor("a0_pS" + tag, [128, 1024], F32))
    pO = stack.enter_context(nc.psum_tensor("a0_pO" + tag, [128, 1024], F32))
    cqln = stack.enter_context(nc.sbuf_tensor("a0_cq" + tag, [128, 2 * T], BF16))
    mcol = stack.enter_context(nc.sbuf_tensor("a0_mc" + tag, [128, 8], F32))
    sqt = stack.enter_context(nc.sbuf_tensor("a0_sq" + tag, [128, 2 * 512], BF16))
    wuq = stack.enter_context(nc.sbuf_tensor("a0_wuq" + tag, [128, 2048], BF16))
    s_dl[1].inc(nc.sync.dma_start(wuq[:, :].rearrange("p (k n) -> p k n", k=NATURAL["wuq"][1] // 128), sc["wuq"].rearrange("k p n -> p k n")), 16)

    def comp_proj(wdown, g_col, b_col, dst_ln, sub, wofs):
        cbf = sub.enter_context(nc.sbuf_tensor(f"a0_cbf{wofs}" + tag, [128, 2 * T], BF16))
        csq = sub.enter_context(nc.sbuf_tensor(f"a0_csq{wofs}" + tag, [128, 2 * T], BF16))
        stat = sub.enter_context(nc.sbuf_tensor(f"a0_stt{wofs}" + tag, [128, 2 * T], F32))
        tmpf = sub.enter_context(nc.sbuf_tensor(f"a0_tf{wofs}" + tag, [128, T], F32))
        mu_t, m2_t = stat[:, 0:T], stat[:, T:2 * T]
        prj_rot = Rot(2)
        for j, (oc, st4) in enumerate([(o, st) for o in range(2) for st in range(4)]):
            n0 = st4 * 512
            ppp = (pA, pB)[j % 2]
            prj_rot.acquire(nc.tensor)
            for kc in range(NC):
                mm = nc.tensor.matmul(ppp[:, 0:512],
                                      wdown[:, kc * 256 + oc * 128:kc * 256 + oc * 128 + 128],
                                      xln[:, kc * T + n0:kc * T + n0 + 512],
                                      start=(kc == 0), stop=(kc == NC - 1))
                if kc == NC - 1:
                    s_mm.inc(mm)
            s_mm.wait(nc.scalar)
            ctx.act.activation(cbf[:, oc * T + n0:oc * T + n0 + 512], ppp[:, 0:512], AF.Copy)
            s_ab.inc(ctx.act.activation(csq[:, oc * T + n0:oc * T + n0 + 512], ppp[:, 0:512], AF.Square))
            prj_rot.release(s_ab)
        s_ab.wait(nc.tensor)
        st_rot = Rot(1)
        for st4 in range(4):
            n0 = st4 * 512
            st_rot.acquire(nc.tensor)
            for oc in range(2):
                nc.tensor.matmul(pS[:, 0:512], ones[:, :], cbf[:, oc * T + n0:oc * T + n0 + 512],
                                 start=(oc == 0), stop=(oc == 1))
            for oc in range(2):
                mm = nc.tensor.matmul(pS[:, 512:1024], ones[:, :], csq[:, oc * T + n0:oc * T + n0 + 512],
                                      start=(oc == 0), stop=(oc == 1))
                if oc == 1:
                    s_mm.inc(mm)
            s_mm.wait(nc.scalar)
            ctx.act.activation(mu_t[:, n0:n0 + 512], pS[:, 0:512], AF.Copy, scale=1.0 / 256)
            s_ab.inc(ctx.act.activation(m2_t[:, n0:n0 + 512], pS[:, 512:1024], AF.Copy, scale=1.0 / 256))
            st_rot.release(s_ab)
        s_ab.wait(nc.vector)
        ctx.vec.tensor_mul(tmpf[:, :], mu_t, mu_t)
        s_ev.inc(ctx.vec.tensor_sub(tmpf[:, :], m2_t, tmpf[:, :]))
        s_ev.wait(nc.scalar)
        ctx.act.activation(tmpf[:, :], tmpf[:, :], AF.Ln, bias=consts["epscol"][:, 0:1])
        s_ab.inc(ctx.act.activation(tmpf[:, :], tmpf[:, :], AF.Exp, scale=-0.5))
        s_ab.wait(nc.vector)
        for oc in range(2):
            ctx.vec.tensor_sub(m2_t, cbf[:, oc * T:(oc + 1) * T], mu_t)
            ctx.vec.tensor_mul(m2_t, m2_t, tmpf[:, :])
            i1 = ctx.vec.tensor_scalar(dst_ln[:, oc * T:(oc + 1) * T], m2_t,
                                         g_col[:, oc:oc + 1], b_col[:, oc:oc + 1], ALU.mult, ALU.add)
            if oc == 1:
                s_ev.inc(i1)
        s_ev.wait(nc.tensor)
        s_ev.wait(nc.scalar)
        s_ev.wait(nc.sync)

    kv_stack = ExitStack()
    ckvln = kv_stack.enter_context(nc.sbuf_tensor("a0_ckv" + tag, [128, 2 * T], BF16, side="right"))
    with ExitStack() as sub:
        wdq = sub.enter_context(nc.sbuf_tensor("a0_wdq" + tag, [128, 2048], BF16))
        s_dl[0].inc(nc.sync.dma_start(wdq[:, :].rearrange("p (k n) -> p k n", k=NATURAL["wdq"][1] // 128), sc["wdq"].rearrange("k p n -> p k n")), 16)
        s_dl[0].wait(nc.tensor)
        comp_proj(wdq, prm["q_ln_g"], prm["q_ln_b"], cqln, sub, 0)
    with ExitStack() as sub:
        wdkv = sub.enter_context(nc.sbuf_tensor("a0_wdkv" + tag, [128, 2048], BF16))
        s_dl[0].inc(nc.sync.dma_start(wdkv[:, :].rearrange("p (k n) -> p k n", k=NATURAL["wdkv"][1] // 128), sc["wdkv"].rearrange("k p n -> p k n")), 16)
        s_dl[0].wait(nc.tensor)
        comp_proj(wdkv, prm["kv_ln_g"], prm["kv_ln_b"], ckvln, sub, 1)

    kbf = stack.enter_context(nc.sbuf_tensor("a0_k" + tag, [128, NC * T], BF16))
    vaug = stack.enter_context(nc.sbuf_tensor("a0_v" + tag, [128, 16 * 1040], BF16))
    wukv = kv_stack.enter_context(nc.sbuf_tensor("a0_wukv" + tag, [128, 4096], BF16, side="right"))
    s_dl[0].inc(nc.sync.dma_start(wukv[:, :].rearrange("p (k n) -> p k n", k=NATURAL["wukv"][1] // 128), sc["wukv"].rearrange("k p n -> p k n")), 16)
    s_dl[0].wait(nc.tensor)

    k_rot = Rot(2)
    for j, (oc, st4) in enumerate([(o, st) for o in range(NC) for st in range(4)]):
        n0 = st4 * 512
        ppp = (pA, pB)[j % 2]
        k_rot.acquire(nc.tensor)
        for kc in range(2):
            mm = nc.tensor.matmul(ppp[:, 0:512],
                                  wukv[:, kc * 2048 + oc * 128:kc * 2048 + oc * 128 + 128],
                                  ckvln[:, kc * T + n0:kc * T + n0 + 512],
                                  start=(kc == 0), stop=(kc == 1))
            if kc == 1:
                s_mm.inc(mm)
        s_mm.wait(nc.scalar)
        s_ab.inc(ctx.act.activation(kbf[:, oc * T + n0:oc * T + n0 + 512], ppp[:, 0:512], AF.Copy))
        k_rot.release(s_ab)
    s_ab.wait(nc.tensor)

    def sumsq_max(src, dstcol, tmpcol):
        sq_rot = Rot(2)
        for st4 in range(4):
            n0 = st4 * 512
            for oc in range(NC):
                sq_rot.acquire(nc.scalar)
                s_ab.inc(ctx.act.activation(sqt[:, (oc % 2) * 512:(oc % 2) * 512 + 512],
                                              src[:, oc * T + n0:oc * T + n0 + 512], AF.Square))
                s_ab.wait(nc.tensor)
                mm = nc.tensor.matmul(pS[:, 0:512], ones[:, :], sqt[:, (oc % 2) * 512:(oc % 2) * 512 + 512],
                                      start=(oc == 0), stop=(oc == NC - 1))
                if oc == NC - 1:
                    s_mm.inc(mm)
                sq_rot.release(s_mm)
            s_mm.wait(nc.vector)
            if st4 == 0:
                s_ev.inc(ctx.vec.tensor_reduce(dstcol, pS[:, 0:512], AXX, ALU.max))
            else:
                ctx.vec.tensor_reduce(tmpcol, pS[:, 0:512], AXX, ALU.max)
                s_ev.inc(ctx.vec.tensor_max(dstcol, dstcol, tmpcol))
            s_ev.wait(nc.tensor)
    sumsq_max(kbf, mcol[:, 0:1], mcol[:, 4:5])

    vview = vaug[:, :].rearrange("p (k h e) -> p k h e", k=16, h=16)
    s_vo = ctx.sem("a0_vo")
    s_vo.inc(ctx.vec.memset(vview[:, :, :, 64:65], 1.0))
    v_rot = Rot(2)
    for j in range(32):
        kt, half = j // 2, j % 2
        n0 = half * 512
        ppp = (pA, pB)[j % 2]
        v_rot.acquire(nc.tensor)
        for kc in range(2):
            mm = nc.tensor.matmul(ppp[:, 0:512],
                                  ckvln[:, kc * T + kt * 128:kc * T + kt * 128 + 128],
                                  wukv[:, kc * 2048 + 1024 + n0:kc * 2048 + 1024 + n0 + 512],
                                  start=(kc == 0), stop=(kc == 1))
            if kc == 1:
                s_mm.inc(mm)
        s_mm.wait(nc.vector)
        s_vo.wait(nc.vector)
        dst = vview[:, kt, half * 8:half * 8 + 8, 0:64]
        src = ppp[:, 0:512].rearrange("p (h d) -> p h d", d=64)
        s_ev.inc(ctx.vec.tensor_copy(dst, src))
        v_rot.release(s_ev)
    s_ev.wait(nc.tensor)
    s_ev.wait(nc.scalar)
    s_ev.wait(nc.sync)
    kv_stack.close()

    qst = stack.enter_context(nc.sbuf_tensor("a0_qs" + tag, [128, 8 * 512], BF16))
    ptb = stack.enter_context(nc.sbuf_tensor("a0_pt" + tag, [128, 6 * 512], BF16))
    amid = stack.enter_context(nc.sbuf_tensor("a0_am" + tag, [128, 8 * 512], BF16))
    rden = stack.enter_context(nc.sbuf_tensor("a0_rd" + tag, [128, 2 * 512], F32))
    wostream = stack.enter_context(nc.sbuf_tensor("a0_wo" + tag, [128, 2 * 1024], BF16))

    woT = sc["woT"]
    wo_rot = Rot(2)
    qp_rot = Rot(2)
    pt_rot = Rot(3)
    sc_rot = Rot(2)
    po_rot = Rot(2)
    rd_rot = Rot(2)
    for st4 in range(4):
        q0 = st4 * 512
        ktmax = (q0 + 511) // 128
        for oc in range(NC):
            ppp = (pA, pB)[oc % 2]
            qp_rot.acquire(nc.tensor)
            for kc in range(2):
                mm = nc.tensor.matmul(ppp[:, 0:512],
                                      wuq[:, kc * 1024 + oc * 128:kc * 1024 + oc * 128 + 128],
                                      cqln[:, kc * T + q0:kc * T + q0 + 512],
                                      start=(kc == 0), stop=(kc == 1))
                if kc == 1:
                    s_mm.inc(mm)
            s_mm.wait(nc.scalar)
            s_ab.inc(ctx.act.activation(qst[:, oc * 512:(oc + 1) * 512], ppp[:, 0:512], AF.Copy))
            qp_rot.release(s_ab)
        s_ab.wait(nc.tensor)
        sq_rot = Rot(2)
        for oc in range(NC):
            sq_rot.acquire(nc.scalar)
            s_ab.inc(ctx.act.activation(sqt[:, (oc % 2) * 512:(oc % 2) * 512 + 512],
                                          qst[:, oc * 512:(oc + 1) * 512], AF.Square))
            s_ab.wait(nc.tensor)
            mm = nc.tensor.matmul(pS[:, 0:512], ones[:, :], sqt[:, (oc % 2) * 512:(oc % 2) * 512 + 512],
                                  start=(oc == 0), stop=(oc == NC - 1))
            if oc == NC - 1:
                s_mm.inc(mm)
            sq_rot.release(s_mm)
        s_mm.wait(nc.vector)
        s_ev.inc(ctx.vec.tensor_reduce(mcol[:, 1:2], pS[:, 0:512], AXX, ALU.max))
        s_ev.wait(nc.tensor)
        s_ev.wait(nc.vector)
        s_ab.wait(nc.vector)
        s_ev.inc(ctx.vec.tensor_mul(mcol[:, 2:3], mcol[:, 0:1], mcol[:, 1:2]))
        s_ev.wait(nc.scalar)
        s_ab.inc(ctx.act.activation(mcol[:, 2:3], mcol[:, 2:3], AF.Sqrt, scale=1.0 / 64))
        s_ab.wait(nc.vector)
        s_ev.inc(ctx.vec.tensor_scalar_mul(mcol[:, 3:4], mcol[:, 2:3], -1.0))
        s_ev.wait(nc.scalar)
        s_ev.wait(nc.tensor)
        denmark = {}
        po_of = {}

        def normalize(hh):
            pbb = (hh % 2) * 64
            hcc = hh // 2
            uu = st4 * 16 + hh
            poo, pdd = po_of[hh]
            nc.scalar.wait_ge(s_mm.h, denmark[hh])
            s_ev.wait(nc.scalar)
            rds = rden[pbb:pbb + 64, (uu % 2) * 512:(uu % 2) * 512 + 512]
            ctx.act.activation(rds, pdd, AF.Ln)
            s_ab.inc(ctx.act.activation(rds, rds, AF.Exp, scale=-1.0))
            s_ab.wait(nc.vector)
            i1 = ctx.vec.tensor_mul(amid[pbb:pbb + 64, hcc * 512:hcc * 512 + 512], poo, rds)
            s_ev.inc(i1)
            po_rot.release(s_ev)

        for h in range(16):
            pb = (h % 2) * 64
            hc = h // 2
            u = st4 * 16 + h
            po = pO[pb:pb + 64, (u % 2) * 512:(u % 2) * 512 + 512]
            pd = pA[pb:pb + 64, (u % 2) * 512:(u % 2) * 512 + 512]
            po_of[h] = (po, pd)
            po_rot.acquire(nc.tensor)

            def emit_qk(p):
                kt2 = p * 2
                npair = min(2, ktmax + 1 - kt2)
                psc = (pS, pB)[p % 2]
                sc_rot.acquire(nc.tensor)
                for j in range(npair):
                    kt = kt2 + j
                    mm = nc.tensor.matmul(psc[:, j * 512:j * 512 + 512],
                                          kbf[pb:pb + 64, hc * T + kt * 128:hc * T + kt * 128 + 128],
                                          qst[pb:pb + 64, hc * 512:hc * 512 + 512],
                                          start=True, stop=True)
                    if j == npair - 1:
                        s_mm.inc(mm)
                s_mm.wait(nc.scalar)
                pt_rot.acquire(nc.scalar)
                pslot = ptb[:, (p % 3) * 1024:(p % 3) * 1024 + npair * 512]
                s_ab.inc(ctx.act.activation(pslot, psc[:, 0:npair * 512], AF.Exp,
                                            scale=0.125, bias=mcol[:, 3:4]))
                sc_rot.release(s_ab)
                need_mask = any(q0 - (kt2 + j) * 128 < 128 for j in range(npair))
                if need_mask:
                    s_ab.wait(nc.vector)
                    last_i = None
                    for j in range(npair):
                        kt = kt2 + j
                        d = q0 - kt * 128
                        if d >= 128:
                            continue
                        sl = ptb[:, (p % 3) * 1024 + j * 512:(p % 3) * 1024 + (j + 1) * 512]
                        if d < 0:
                            ctx.vec.memset(sl[:, 0:-d], 0.0)
                        mw0 = max(0, -d)
                        last_i = ctx.vec.tensor_mul(sl[:, mw0:mw0 + 128], sl[:, mw0:mw0 + 128], tril[:, :])
                    s_ev.inc(last_i)
                return need_mask

            def emit_pv(p, masked):
                kt2 = p * 2
                npair = min(2, ktmax + 1 - kt2)
                if masked:
                    s_ev.wait(nc.tensor, evmark[p])
                else:
                    s_ab.wait(nc.tensor, abmark[p])
                for j in range(npair):
                    kt = kt2 + j
                    sl = ptb[:, (p % 3) * 1024 + j * 512:(p % 3) * 1024 + (j + 1) * 512]
                    nc.tensor.matmul(po, vaug[:, kt * 1040 + h * 65:kt * 1040 + h * 65 + 64],
                                     sl, start=(kt == 0), stop=(kt == ktmax),
                                     tile_position=(0, pb), skip_group_check=True)
                    mm2 = nc.tensor.matmul(pd, ones[:, 0:64],
                                           sl, start=(kt == 0), stop=(kt == ktmax),
                                           tile_position=(0, pb), skip_group_check=True)
                    if j == npair - 1:
                        s_mm.inc(mm2)
                pt_rot.release(s_mm)

            npairs = (ktmax + 2) // 2
            maskedp = {}
            abmark = {}
            evmark = {}
            for p in range(npairs):
                maskedp[p] = emit_qk(p)
                abmark[p] = s_ab.v
                evmark[p] = s_ev.v
                if p >= 2:
                    emit_pv(p - 2, maskedp[p - 2])
            if npairs >= 2:
                emit_pv(npairs - 2, maskedp[npairs - 2])
            emit_pv(npairs - 1, maskedp[npairs - 1])
            denmark[h] = s_mm.v
            if h >= 1:
                normalize(h - 1)
        normalize(15)
        s_ev.wait(nc.tensor)
        for oc in range(NC):
            slot = oc % 2
            wo_rot.acquire(nc.sync)
            s_dl[slot].inc(nc.sync.dma_start(wostream[:, slot * 1024:slot * 1024 + 1024], woT[oc, :, :]), 16)
            s_dl[slot].wait(nc.tensor)
            ppp = (pA, pB)[slot]
            qp_rot.acquire(nc.tensor)
            for kc in range(NC):
                mm = nc.tensor.matmul(ppp[:, 512:1024],
                                      wostream[:, slot * 1024 + kc * 128:slot * 1024 + kc * 128 + 128],
                                      amid[:, kc * 512:(kc + 1) * 512],
                                      start=(kc == 0), stop=(kc == NC - 1))
                if kc == NC - 1:
                    s_mm.inc(mm)
            wo_rot.release(s_mm)
            s_mm.wait(nc.vector)
            sl = slice(oc * T + q0, oc * T + q0 + 512)
            s_ev.inc(ctx.vec.tensor_add(xres[:, sl], xres[:, sl], ppp[:, 512:1024]))
            qp_rot.release(s_ev)
        s_ev.wait(nc.tensor)
        s_ev.wait(nc.scalar)
    s_ev.wait(nc.tensor)
    s_ev.wait(nc.scalar)
    s_ev.wait(nc.sync)
    s_ev.wait(nc.gpsimd)
    stack.close()


def emit_l1ffn(ctx, bufs, sc, tag):
    nc = ctx.nc
    xres, xln = bufs["xres"], bufs["xln"]
    mu_r, mu_k = bufs["prm"]["mu_r"], bufs["prm"]["mu_k"]
    s_dl = [ctx.sem("f1_dl0"), ctx.sem("f1_dl1")]
    s_mm = ctx.sem("f1_mm")
    s_ev = ctx.sem("f1_ev")
    s_ab = ctx.sem("f1_ab")
    s_mx = ctx.sem("f1_mx")
    stack = ExitStack()
    TS = 1024
    wbuf = stack.enter_context(nc.sbuf_tensor("f1_w" + tag, [128, 2 * 1024], BF16))
    xr = stack.enter_context(nc.sbuf_tensor("f1_xr" + tag, [128, NC * TS], BF16))
    xk = stack.enter_context(nc.sbuf_tensor("f1_xk" + tag, [128, NC * TS], BF16))
    rr = stack.enter_context(nc.sbuf_tensor("f1_r" + tag, [128, NC * TS], BF16))
    ka = stack.enter_context(nc.sbuf_tensor("f1_ka" + tag, [128, NC * TS], BF16))
    tv = stack.enter_context(nc.sbuf_tensor("f1_tv" + tag, [128, 1024], F32))
    pp = [stack.enter_context(nc.psum_tensor(f"f1_pp{i}" + tag, [128, 1024], F32)) for i in range(2)]
    wrT, wkT, wvT = sc["wrT"], sc["wkT"], sc["wvT"]
    w_rot = Rot(2)
    p_rot = Rot(2)

    for strip in range(2):
        t0 = strip * TS
        for c in range(NC):
            slc = slice(c * T + t0, c * T + t0 + TS)
            for dst, mu in ((xr, mu_r), (xk, mu_k)):
                dsl = dst[:, c * TS:(c + 1) * TS]
                if t0 == 0:
                    ctx.vec.tensor_scalar_mul(dsl[:, 0:1], xln[:, c * T:c * T + 1], mu[:, c:c + 1])
                    prev = xln[:, c * T:c * T + TS - 1]
                    cur = xln[:, c * T + 1:c * T + TS]
                    out = dsl[:, 1:TS]
                else:
                    prev = xln[:, c * T + t0 - 1:c * T + t0 + TS - 1]
                    cur = xln[:, slc]
                    out = dsl[:, :]
                ctx.vec.scalar_tensor_tensor(tv[:, :out.shape[-1]], prev, mu[:, c:c + 1], prev,
                                               ALU.mult, ALU.subtract_rev if False else ALU.subtract)
                i1 = ctx.vec.scalar_tensor_tensor(out, cur, mu[:, c:c + 1], tv[:, :out.shape[-1]],
                                                    ALU.mult, ALU.subtract)
                if c == NC - 1 and dst is xk:
                    s_mx.inc(i1)
        s_mx.wait(nc.tensor)
        for (wT, src, dst, kind) in ((wrT, xr, rr, "r"), (wkT, xk, ka, "k")):
            for oc in range(NC):
                slot = oc % 2
                wofs = slot * 1024
                w_rot.acquire(nc.sync)
                s_dl[slot].inc(nc.sync.dma_start(wbuf[:, wofs:wofs + 1024], wT[oc, :, :]), 16)
                s_dl[slot].wait(nc.tensor)
                p_rot.acquire(nc.tensor)
                for n0 in (0, 512):
                    for kc in range(NC):
                        mm = nc.tensor.matmul(pp[slot][:, n0:n0 + 512],
                                              wbuf[:, wofs + kc * 128:wofs + (kc + 1) * 128],
                                              src[:, kc * TS + n0:kc * TS + n0 + 512],
                                              start=(kc == 0), stop=(kc == NC - 1))
                        if n0 == 512 and kc == NC - 1:
                            s_mm.inc(mm)
                w_rot.release(s_mm)
                osl = dst[:, oc * TS:(oc + 1) * TS]
                if kind == "r":
                    s_mm.wait(nc.scalar)
                    s_ab.inc(ctx.act.activation(osl, pp[slot][:, :], AF.Sigmoid))
                    p_rot.release(s_ab)
                else:
                    s_mm.wait(nc.vector)
                    s_ab.wait(nc.vector)
                    s_ev.inc(ctx.vec.tensor_scalar_max(tv[:, :], pp[slot][:, :], 0.0))
                    p_rot.release(s_ev)
                    s_ev.wait(nc.scalar)
                    s_ab.inc(ctx.act.activation(osl, tv[:, :], AF.Square))
        s_ab.wait(nc.tensor)
        for oc in range(NC):
            slot = oc % 2
            wofs = slot * 1024
            w_rot.acquire(nc.sync)
            s_dl[slot].inc(nc.sync.dma_start(wbuf[:, wofs:wofs + 1024], wvT[oc, :, :]), 16)
            s_dl[slot].wait(nc.tensor)
            p_rot.acquire(nc.tensor)
            for n0 in (0, 512):
                for kc in range(NC):
                    mm = nc.tensor.matmul(pp[slot][:, n0:n0 + 512],
                                          wbuf[:, wofs + kc * 128:wofs + (kc + 1) * 128],
                                          ka[:, kc * TS + n0:kc * TS + n0 + 512],
                                          start=(kc == 0), stop=(kc == NC - 1))
                    if n0 == 512 and kc == NC - 1:
                        s_mm.inc(mm)
            w_rot.release(s_mm)
            s_mm.wait(nc.vector)
            if oc == 0:
                s_ab.wait(nc.vector)
            sl = slice(oc * T + t0, oc * T + t0 + TS)
            ctx.vec.tensor_mul(tv[:, :], rr[:, oc * TS:(oc + 1) * TS], pp[slot][:, :])
            s_ev.inc(ctx.vec.scalar_tensor_tensor(xres[:, sl], tv[:, :], 0.5, xres[:, sl],
                                                    ALU.mult, ALU.add))
            p_rot.release(s_ev)
    s_ev.wait(nc.tensor)
    s_ev.wait(nc.scalar)
    s_ev.wait(nc.sync)
    s_ev.wait(nc.gpsimd)
    stack.close()


def emit_l1attn(ctx, bufs, sc, tag):
    nc = ctx.nc
    xres, xln = bufs["xres"], bufs["xln"]
    prm, consts = bufs["prm"], bufs["consts"]
    ones = consts["ones_bf"]
    ident = consts["ident_bf"]
    obc, abc, center = consts["obc"], consts["abc"], consts["center"]
    xtok = sc["xtok"]
    s_dl = [ctx.sem("a1_dl0"), ctx.sem("a1_dl1")]
    s_mm = ctx.sem("a1_mm")
    s_ev = ctx.sem("a1_ev")
    s_ab = ctx.sem("a1_ab")
    NSLOT = L1_NSLOT
    s_gd = [ctx.sem(f"a1_gd{i}") for i in range(NSLOT)]
    s_xt = [ctx.sem(f"a1_xt{i}") for i in range(2)]
    s_gp = ctx.sem("a1_gp")
    stack = ExitStack()
    pA = stack.enter_context(nc.psum_tensor("a1_pA" + tag, [128, 1024], F32))
    pB = stack.enter_context(nc.psum_tensor("a1_pB" + tag, [128, 1024], F32))
    pT = [stack.enter_context(nc.psum_tensor(f"a1_pT{i}" + tag, [128, 128], BF16)) for i in range(2)]
    amid = stack.enter_context(nc.sbuf_tensor("a1_am" + tag, [128, NC * T], BF16))
    wqt = stack.enter_context(nc.sbuf_tensor("a1_wq" + tag, [128, 16 * 152], BF16))
    fdd = stack.enter_context(nc.sbuf_tensor("a1_fd" + tag, [128, 16 * 24], F32))
    fav = stack.enter_context(nc.sbuf_tensor("a1_fa" + tag, [128, 16 * 2], F32))
    karg = stack.enter_context(nc.sbuf_tensor("a1_ka" + tag, [128, 2176], BF16))
    scr = stack.enter_context(nc.sbuf_tensor("a1_sc" + tag, [128, 264 * 6], F32))
    sml = stack.enter_context(nc.sbuf_tensor("a1_sm" + tag, [128, 64], F32))
    la = stack.enter_context(nc.sbuf_tensor("a1_la" + tag, [128, 2 * 136], F32))
    idx32 = stack.enter_context(nc.sbuf_tensor("a1_ix" + tag, [128, 4 * NS], I32))
    vg = stack.enter_context(nc.sbuf_tensor("a1_vg" + tag, [128, L1_NSLOT * 1024], BF16))
    acc = stack.enter_context(nc.sbuf_tensor("a1_ac" + tag, [128, 1024], F32))
    xts = stack.enter_context(nc.sbuf_tensor("a1_xs" + tag, [128, 2 * 1024], BF16))
    wsml = stack.enter_context(nc.sbuf_tensor("a1_ws" + tag, [128, 192 + 1024 + 2048 + 2048], BF16))
    O_WAVE, O_QUERY, O_SE1, O_SE2 = 0, 192, 192 + 1024, 192 + 1024 + 2048
    wostream = stack.enter_context(nc.sbuf_tensor("a1_wo" + tag, [128, 2 * 1024], BF16))
    brow = stack.enter_context(nc.sbuf_tensor("a1_br" + tag, [1, 152], BF16))
    kcb = stack.enter_context(nc.sbuf_tensor("a1_kc" + tag, [128, 272], F32))
    kcr = stack.enter_context(nc.sbuf_tensor("a1_kr" + tag, [1, 272], F32))

    s_dl[0].inc(nc.sync.dma_start(wsml[:, O_WAVE:O_WAVE + 192].rearrange("p (k n) -> p k n", k=8),
                                  sc["wwave"].rearrange("k p n -> p k n")), 16)
    s_dl[0].inc(nc.sync.dma_start(wsml[:, O_QUERY:O_QUERY + 1024].rearrange("p (k n) -> p k n", k=8),
                                  sc["wquery"].rearrange("k p n -> p k n")), 16)
    s_dl[0].inc(nc.sync.dma_start(wsml[:, O_SE1:O_SE1 + 2048].rearrange("p (k n) -> p k n", k=8),
                                  sc["wse1"].rearrange("k p n -> p k n")), 16)
    s_dl[0].inc(nc.sync.dma_start(wsml[:, O_SE2:O_SE2 + 2048].rearrange("p (k n) -> p k n", k=2),
                                  sc["wse2"].rearrange("k p n -> p k n")), 16)
    s_dl[0].wait(nc.vector)
    s_dl[0].wait(nc.tensor)
    ctx.vec.tensor_copy(brow[0:1, 0:24], prm["wave_b"][0:1, :])
    s_ev.inc(ctx.vec.tensor_copy(brow[0:1, 24:152], prm["query_b"][0:1, :]))
    ctx.vec.tensor_tensor(kcr[0:1, :],
                            abc[0:1, :].rearrange("p (a o) -> p a o", o=1).broadcast_to([1, NA, POS]),
                            prm["key_w"][0:1, :].rearrange("p (o d) -> p o d", o=1).broadcast_to([1, NA, POS]),
                            ALU.mult)
    s_ev.inc(ctx.vec.tensor_copy(xts[0:1, 0:272], kcr[0:1, :]))
    s_ev.wait(nc.tensor)
    mm = nc.tensor.matmul(pA[:, 0:272], ones[0:1, :], xts[0:1, 0:272],
                          start=True, stop=True)
    s_mm.inc(mm)
    s_mm.wait(nc.vector)
    s_ev.inc(ctx.vec.tensor_copy(kcb[:, :], pA[:, 0:272]))
    s_ev.wait(nc.tensor)

    xt_rot = Rot(2)
    ptrot = Rot(2)
    for tc in range(16):
        sl = tc % 2
        xt_rot.acquire(nc.vector)
        s_xt[sl].wait(nc.vector)
        for c in range(NC):
            pt = pT[c % 2]
            ptrot.acquire(nc.tensor)
            s_mm.inc(nc.tensor.transpose(pt[:, :], xln[:, c * T + tc * 128:c * T + tc * 128 + 128], ident[:, :]))
            s_mm.wait(nc.vector)
            i1 = ctx.vec.tensor_copy(xts[:, sl * 1024 + c * 128:sl * 1024 + c * 128 + 128], pt[:, :])
            s_ev.inc(i1)
            ptrot.release(s_ev)
        s_ev.wait(nc.sync)
        s_xt[sl].inc(nc.sync.dma_start(xtok[PAD + tc * 128:PAD + tc * 128 + 128, :], xts[:, sl * 1024:sl * 1024 + 1024]), 16)
        xt_rot.release(s_xt[sl])
    if L1_LIMIT < 2:
        s_xt[0].wait(nc.sync)
        s_xt[1].wait(nc.sync)
        s_ev.wait(nc.tensor); s_ev.wait(nc.scalar); s_ev.wait(nc.sync); s_ev.wait(nc.gpsimd)
        stack.close()
        return
    s_ev.wait(nc.tensor)
    wq_rot = Rot(2)
    for tc in range(16):
        ppp = (pA, pB)[tc % 2]
        wq_rot.acquire(nc.tensor)
        for kc in range(NC):
            nc.tensor.matmul(ppp[:, 0:24], xln[:, kc * T + tc * 128:kc * T + tc * 128 + 128],
                             wsml[:, O_WAVE + kc * 24:O_WAVE + kc * 24 + 24],
                             start=(kc == 0), stop=False, skip_group_check=True)
            nc.tensor.matmul(ppp[:, 24:152], xln[:, kc * T + tc * 128:kc * T + tc * 128 + 128],
                             wsml[:, O_QUERY + kc * 128:O_QUERY + kc * 128 + 128],
                             start=False, stop=False, skip_group_check=True)
        mm = nc.tensor.matmul(ppp[:, 0:152], ones[0:1, :], brow[0:1, :],
                              start=False, stop=True, skip_group_check=True)
        s_mm.inc(mm)
        s_mm.wait(nc.scalar)
        s_ab.inc(ctx.act.activation(wqt[:, tc * 152:(tc + 1) * 152], ppp[:, 0:152], AF.Silu))
        wq_rot.release(s_ab)
    if L1_LIMIT < 3:
        s_ab.wait(nc.tensor); s_ab.wait(nc.vector); s_ab.wait(nc.sync); s_ab.wait(nc.gpsimd)
        stack.close()
        return
    for tc in range(16):
        w0 = tc * 152
        f0 = tc * 24
        ctx.act.activation(fdd[:, f0:f0 + 24].rearrange("p (g e) -> p g e", g=3)[:, 0:3:2, :],
                             wqt[:, w0:w0 + 24].rearrange("p (g e) -> p g e", g=3)[:, 0:3:2, :],
                             AF.Sigmoid)
        i1 = ctx.act.activation(fdd[:, f0 + 8:f0 + 16], wqt[:, w0 + 8:w0 + 16], AF.Tanh)
        if tc == 15:
            s_ab.inc(i1)
    s_ab.wait(nc.vector)
    fview = fdd[:, :].rearrange("p (t g e) -> p t g e", t=16, g=3)
    ctx.vec.tensor_scalar(fview[:, :, 0, :], fview[:, :, 0, :], 15.0, 1.0, ALU.mult, ALU.add)
    ctx.vec.tensor_scalar_mul(fview[:, :, 1, :], fview[:, :, 1, :], 16.0)
    ctx.vec.tensor_scalar(fview[:, :, 2, :], fview[:, :, 2, :], 9.5, 0.5, ALU.mult, ALU.add)
    ctx.vec.tensor_reduce(fav[:, 0:16], fview[:, :, 0, :], AXX, ALU.add)
    ctx.vec.tensor_reduce(fav[:, 16:32], fview[:, :, 1, :], AXX, ALU.add)
    s_ev.inc(ctx.vec.tensor_scalar_mul(fav[:, 0:32], fav[:, 0:32], 0.125))
    s_ev.wait(nc.gpsimd)
    s_xt[0].wait(nc.gpsimd)
    s_xt[1].wait(nc.gpsimd)

    if L1_LIMIT < 4:
        s_ev.wait(nc.tensor); s_ev.wait(nc.scalar); s_ev.wait(nc.sync); s_ev.wait(nc.gpsimd)
        stack.close()
        return
    LOOK = 4
    NMACG = 1
    wwall = stack.enter_context(nc.sbuf_tensor("a1_ww" + tag, [128, LOOK * 264], F32))
    s_cblk = ctx.sem("a1_cb")
    s_mg = ctx.sem("a1_mg")
    s_tp2 = ctx.sem("a1_t2")
    cmark = []
    gmark = {}
    LG, ENV, WE, T33 = 0, 264, 528, 792

    def stage_c_gen(tc):
        f0 = tc * 24
        ws = (tc % LOOK) * 264
        ww = wwall[:, ws:ws + 264]
        ctx.vec.tensor_scalar_add(sml[:, 48:49], center[:, 0:1], float(tc * 128))
        ctx.vec.tensor_add(sml[:, 48:49], sml[:, 48:49], fav[:, 16 + tc:17 + tc])
        ctx.vec.scalar_tensor_tensor(scr[:, T33:T33 + NS], obc[:, :], fav[:, tc:tc + 1],
                                     sml[:, 48:49].broadcast_to([128, NS]), ALU.mult, ALU.add)
        sp = scr[:, T33:T33 + NS]
        yield
        ctx.vec.tensor_single_scalar(scr[:, T33 + NS:T33 + 2 * NS], sp, 0.0, ALU.is_ge)
        ctx.vec.tensor_single_scalar(scr[:, T33 + 2 * NS:T33 + 3 * NS], sp, float(T), ALU.is_lt)
        vm = scr[:, T33 + NS:T33 + 2 * NS]
        ctx.vec.tensor_mul(vm, vm, scr[:, T33 + 2 * NS:T33 + 3 * NS])
        yield
        fl = scr[:, T33 + 3 * NS:T33 + 4 * NS]
        ctx.vec.tensor_scalar_add(fl, sp, MAGIC)
        ctx.vec.tensor_scalar_add(fl, fl, -MAGIC)
        gt = scr[:, T33 + 4 * NS:T33 + 5 * NS]
        ctx.vec.tensor_tensor(gt, fl, sp, ALU.is_gt)
        ctx.vec.tensor_sub(fl, fl, gt)
        ctx.vec.tensor_scalar_add(fl, fl, float(PAD))
        yield
        ctx.vec.tensor_tensor(
            karg[:, :].rearrange("p (h ad) -> p h ad", h=H1),
            kcb[:, :].rearrange("p (o ad) -> p o ad", o=1).broadcast_to([128, H1, 272]),
            fdd[:, f0:f0 + 8].rearrange("p (h o) -> p h o", o=1).broadcast_to([128, H1, 272]),
            ALU.mult)
        s_ev.inc(ctx.vec.engine_nop())
        yield
        s_ev.wait(nc.scalar)
        s_ab.inc(ctx.act.activation(karg[:, :], karg[:, :], AF.Silu))
        yield
        s_ab.wait(nc.vector)
        for h in range(H1):
            q = wqt[:, tc * 152 + 24 + h * 16:tc * 152 + 24 + (h + 1) * 16]
            ctx.vec.tensor_tensor(karg[:, h * 272:(h + 1) * 272],
                                  karg[:, h * 272:(h + 1) * 272].rearrange("p (a d) -> p a d", d=POS),
                                  q.rearrange("p (o d) -> p o d", o=1).broadcast_to([128, NA, POS]),
                                  ALU.mult)
            if h == 3:
                yield
        yield
        ctx.vec.tensor_reduce(la[:, 0:136], karg[:, :].rearrange("p (ha d) -> p ha d", d=POS),
                              AXX, ALU.add)
        yield
        ctx.vec.tensor_scalar_max(sml[:, 32:40], fdd[:, f0 + 16:f0 + 24], 0.1)
        ctx.vec.reciprocal(sml[:, 32:40], sml[:, 32:40])
        ctx.vec.tensor_tensor(
            la[:, 136:272].rearrange("p (h a) -> p h a", h=H1),
            abc[:, :].rearrange("p (o a) -> p o a", o=1).broadcast_to([128, H1, NA]),
            fdd[:, f0:f0 + 8].rearrange("p (h o) -> p h o", o=1).broadcast_to([128, H1, NA]),
            ALU.mult)
        ctx.vec.tensor_tensor(
            la[:, 136:272].rearrange("p (h a) -> p h a", h=H1),
            la[:, 136:272].rearrange("p (h a) -> p h a", h=H1),
            sml[:, 32:40].rearrange("p (h o) -> p h o", o=1).broadcast_to([128, H1, NA]),
            ALU.mult)
        yield
        lg = scr[:, LG:LG + 264]
        env = scr[:, ENV:ENV + 264]
        for (srcoff, dst) in ((0, lg), (136, env)):
            ctx.vec.tensor_copy(dst[:, :].rearrange("p (h s) -> p h s", h=H1)[:, :, 0:16],
                                la[:, srcoff:srcoff + 136].rearrange("p (h a) -> p h a", h=H1)[:, :, 16:0:-1])
            ctx.vec.tensor_copy(dst[:, :].rearrange("p (h s) -> p h s", h=H1)[:, :, 16:33],
                                la[:, srcoff:srcoff + 136].rearrange("p (h a) -> p h a", h=H1)[:, :, 0:17])
            yield
        BIG = 1e30
        ctx.vec.tensor_scalar(scr[:, T33 + 2 * NS:T33 + 3 * NS], vm, BIG, -BIG, ALU.mult, ALU.add)
        mneg = scr[:, T33 + 2 * NS:T33 + 3 * NS]
        ctx.vec.tensor_tensor(lg, lg, vm.rearrange("p (o s) -> p o s", o=1).broadcast_to([128, H1, NS]), ALU.mult)
        ctx.vec.tensor_tensor(lg, lg, mneg.rearrange("p (o s) -> p o s", o=1).broadcast_to([128, H1, NS]), ALU.add)
        yield
        ctx.vec.tensor_reduce(sml[:, 0:8], lg.rearrange("p (h s) -> p h s", h=H1), AXX, ALU.max)
        ctx.vec.tensor_scalar_mul(sml[:, 0:8], sml[:, 0:8], -0.25)
        i1 = ctx.vec.scalar_tensor_tensor(
            lg, lg, 0.25,
            sml[:, 0:8].rearrange("p (h o) -> p h o", o=1).broadcast_to([128, H1, NS]),
            ALU.mult, ALU.add)
        s_ev.inc(i1)
        yield
        s_ev.wait(nc.scalar)
        ctx.act.activation(lg, lg, AF.Exp)
        s_ab.inc(ctx.act.activation(env, env, AF.Exp, scale=-1.0))
        yield
        s_ab.wait(nc.vector)
        we = scr[:, WE:WE + 264]
        ctx.vec.tensor_reduce(sml[:, 8:16], lg.rearrange("p (h s) -> p h s", h=H1), AXX, ALU.add)
        ctx.vec.tensor_mul(we, lg, env)
        yield
        ctx.vec.tensor_reduce(sml[:, 16:24], we.rearrange("p (h s) -> p h s", h=H1), AXX, ALU.add)
        ctx.vec.scalar_tensor_tensor(sml[:, 24:32], sml[:, 8:16], 1e-8, sml[:, 16:24], ALU.mult, ALU.add)
        ctx.vec.reciprocal(sml[:, 24:32], sml[:, 24:32])
        yield "hold"
        s_ab.wait(nc.vector)
        i1 = ctx.vec.tensor_tensor(
            ww[:, :].rearrange("p (h s) -> p h s", h=H1),
            we[:, :].rearrange("p (h s) -> p h s", h=H1),
            sml[:, 24:32].rearrange("p (h o) -> p h o", o=1).broadcast_to([128, H1, NS]),
            ALU.mult)
        if tc >= LOOK:
            s_gp.wait(nc.vector, gmark[tc - LOOK])
        i1 = ctx.vec.tensor_copy(idx32[:, (tc % LOOK) * NS:(tc % LOOK) * NS + NS],
                                 scr[:, T33 + 3 * NS:T33 + 4 * NS])
        s_cblk.inc(i1)
        cmark.append(s_cblk.v)

    s_mac = [ctx.sem(f"a1_mc{i}") for i in range(NSLOT)]
    slot_free_at = [0] * NSLOT
    stream = [(t, si) for t in range(16) for si in range(NS)]
    gnext = [0]

    def issue_gather_upto(k):
        while gnext[0] <= min(k, len(stream) - 1):
            tg, sg = stream[gnext[0]]
            slot = gnext[0] % NSLOT
            if sg == 0:
                s_cblk.wait(nc.gpsimd, cmark[tg])
            if slot_free_at[slot] > 0:
                nc.gpsimd.wait_ge(s_mac[slot].h, slot_free_at[slot])
            ins = nc.gpsimd.indirect_dma_start(
                vg[:, slot * 1024:(slot + 1) * 1024], None,
                xtok, IndirectOffsetOnAxis(idx32[:, (tg % LOOK) * NS + sg:(tg % LOOK) * NS + sg + 1], 0))
            s_gd[slot].inc(ins, 16)
            if sg == NS - 1:
                s_gp.inc(nc.gpsimd.engine_nop())
                gmark[tg] = s_gp.v
            gnext[0] += 1

    scl_ready = [None] * NSLOT

    def macs(tc, gen):
        held = [gen is None]

        def pump():
            if held[0]:
                return
            try:
                if next(gen) == "hold":
                    held[0] = True
            except StopIteration:
                held[0] = True

        ws = (tc % LOOK) * 264
        ww = wwall[:, ws:ws + 264]
        pacc = (pA, pB)[tc % 2]
        if tc >= 2:
            s_ab.wait(nc.tensor, tpmark[tc - 2])
        for si in range(NS):
            k = tc * NS + si
            issue_gather_upto(k + NSLOT - 1)
            slot = k % NSLOT
            on_act = (L1_SPLIT > 0 and si % L1_SPLIT == L1_SPLIT - 1)
            weng = nc.scalar if on_act else nc.vector
            s_gd[slot].wait(weng)
            vgv = vg[:, slot * 1024:(slot + 1) * 1024].rearrange("p (h d) -> p h d", h=H1)
            wbc = ww[:, :].rearrange("p (h s) -> p h s", h=H1)[:, :, si:si + 1].broadcast_to([128, H1, 128])
            if on_act:
                for h in range(H1):
                    wcol = ww[:, h * NS + si:h * NS + si + 1]
                    vs = vg[:, slot * 1024 + h * 128:slot * 1024 + (h + 1) * 128]
                    i1 = ctx.act.activation(vs, vs, AF.Copy, scale=wcol)
                s_ab.inc(i1)
                scl_ready[slot] = (s_ab, s_ab.v)
            else:
                i1 = ctx.vec.tensor_tensor(vgv, vgv, wbc, ALU.mult)
                s_ev.inc(i1)
                scl_ready[slot] = (s_ev, s_ev.v)
            sem, v = scl_ready[slot]
            nc.tensor.wait_ge(sem.h, v)
            for half in range(2):
                mm = nc.tensor.matmul(pacc[:, half * 512:(half + 1) * 512],
                                      consts["ident_bf"][:, :],
                                      vg[:, slot * 1024 + half * 512:slot * 1024 + (half + 1) * 512],
                                      start=(si == 0), stop=(si == NS - 1),
                                      skip_group_check=True)
                if half == 1:
                    s_mac[slot].inc(mm)
                    slot_free_at[slot] = s_mac[slot].v
            pump()

    tpmark = {}

    def evac(tc):
        pacc = (pA, pB)[tc % 2]
        xsl = xts[:, (tc % 2) * 1024:(tc % 2) * 1024 + 1024]
        s_mm.wait(nc.scalar)
        s_mac[(tc * NS + NS - 1) % NSLOT].wait(nc.scalar)
        i1 = ctx.act.activation(xsl, pacc[:, 0:1024], AF.Copy)
        s_ab.inc(i1)
        tpmark[tc] = s_ab.v
        s_ab.wait(nc.tensor)
        for c in range(NC):
            pt = pT[c % 2]
            ptrot.acquire(nc.tensor)
            s_mm.inc(nc.tensor.transpose(pt[:, :], xsl[:, c * 128:c * 128 + 128], ident[:, :]))
            s_mm.wait(nc.scalar)
            i1 = ctx.act.activation(amid[:, c * T + tc * 128:c * T + tc * 128 + 128], pt[:, :], AF.Copy)
            s_ab.inc(i1)
            ptrot.release(s_ab)

    for t in range(LOOK):
        for _ in stage_c_gen(t):
            pass
    for t in range(16):
        gen = stage_c_gen(t + LOOK) if t + LOOK < 16 else None
        if not L1_ILV and gen is not None:
            while next(gen, "hold") != "hold":
                pass
            macs(t, None)
        else:
            macs(t, gen)
        if gen is not None:
            for _ in gen:
                pass
        evac(t)
    s_ev.wait(nc.tensor)
    s_ab.wait(nc.tensor)
    s_ev.wait(nc.gpsimd)
    s_ab.wait(nc.gpsimd)

    if L1_LIMIT < 7:
        s_ev.wait(nc.tensor); s_ev.wait(nc.scalar); s_ev.wait(nc.sync); s_ev.wait(nc.gpsimd)
        s_gd[0].wait(nc.vector); s_gd[1].wait(nc.vector); s_gd[2].wait(nc.vector); s_gd[3].wait(nc.vector)
        stack.close()
        return
    se_rot = Rot(2)
    seh = stack.enter_context(nc.sbuf_tensor("a1_sh" + tag, [128, 2 * T], BF16))
    sec = stack.enter_context(nc.sbuf_tensor("a1_se" + tag, [128, 512], BF16))
    for st4 in range(4):
        n0 = st4 * 512
        for oc in range(2):
            ppp = (pA, pB)[oc % 2]
            se_rot.acquire(nc.tensor)
            for kc in range(NC):
                mm = nc.tensor.matmul(ppp[:, 0:512],
                                      wsml[:, O_SE1 + kc * 256 + oc * 128:O_SE1 + kc * 256 + oc * 128 + 128],
                                      amid[:, kc * T + n0:kc * T + n0 + 512],
                                      start=(kc == 0), stop=(kc == NC - 1))
                if kc == NC - 1:
                    s_mm.inc(mm)
            s_mm.wait(nc.scalar)
            s_ab.inc(ctx.act.activation(seh[:, oc * T + n0:oc * T + n0 + 512], ppp[:, 0:512],
                                          AF.Silu, bias=prm["se1_b"][:, oc:oc + 1]))
            se_rot.release(s_ab)
    s_ab.wait(nc.tensor)
    for st4 in range(4):
        n0 = st4 * 512
        for oc in range(NC):
            ppp = (pA, pB)[oc % 2]
            se_rot.acquire(nc.tensor)
            for kc in range(2):
                mm = nc.tensor.matmul(ppp[:, 0:512],
                                      wsml[:, O_SE2 + kc * 1024 + oc * 128:O_SE2 + kc * 1024 + oc * 128 + 128],
                                      seh[:, kc * T + n0:kc * T + n0 + 512],
                                      start=(kc == 0), stop=(kc == 1))
                if kc == 1:
                    s_mm.inc(mm)
            s_mm.wait(nc.scalar)
            s_ab.inc(ctx.act.activation(sec[:, 0:512], ppp[:, 0:512],
                                          AF.Sigmoid, bias=prm["se2_b"][:, oc:oc + 1]))
            s_ab.wait(nc.vector)
            sl = slice(oc * T + n0, oc * T + n0 + 512)
            s_ev.inc(ctx.vec.tensor_mul(amid[:, sl], amid[:, sl], sec[:, 0:512]))
            se_rot.release(s_ev)
            s_ev.wait(nc.scalar)
    s_ev.wait(nc.tensor)
    woutw = sc["woutw"]
    wo_rot = Rot(2)
    for st4 in range(4):
        n0 = st4 * 512
        for oc in range(NC):
            slot = oc % 2
            wo_rot.acquire(nc.sync)
            s_dl[slot].inc(nc.sync.dma_start(wostream[:, slot * 1024:(slot + 1) * 1024], woutw[oc, :, :]), 16)
            s_dl[slot].wait(nc.tensor)
            ppp = (pA, pB)[slot]
            se_rot.acquire(nc.tensor)
            for kc in range(NC):
                mm = nc.tensor.matmul(ppp[:, 0:512],
                                      wostream[:, slot * 1024 + kc * 128:slot * 1024 + kc * 128 + 128],
                                      amid[:, kc * T + n0:kc * T + n0 + 512],
                                      start=(kc == 0), stop=(kc == NC - 1))
                if kc == NC - 1:
                    s_mm.inc(mm)
            wo_rot.release(s_mm)
            s_mm.wait(nc.scalar)
            s_ab.inc(ctx.act.activation(sec[:, 0:512], ppp[:, 0:512], AF.Silu))
            s_ab.wait(nc.vector)
            sl = slice(oc * T + n0, oc * T + n0 + 512)
            s_ev.inc(ctx.vec.tensor_add(xres[:, sl], xres[:, sl], sec[:, 0:512]))
            se_rot.release(s_ev)
            s_ev.wait(nc.scalar)
    s_ev.wait(nc.tensor)
    s_ev.wait(nc.scalar)
    s_ev.wait(nc.sync)
    s_ev.wait(nc.gpsimd)
    stack.close()


def build_program(phases, n_seq=NSEQ):
    nc = bass.Bass("TRN2", target_bir_lowering=False, detect_race_conditions=False)
    stack = ExitStack()
    ctx = Ctx(nc, stack)
    io = declare_io(nc, phases)
    sc = declare_scratch(nc, phases)

    ctx.vec
    ctx.act
    consts = emit_consts(ctx, io, phases)
    prm = emit_params(ctx, io, phases)
    barrier(ctx)
    emit_preprocess(ctx, io, sc, phases, consts)
    barrier(ctx)

    xres = ctx.sb("xres", [128, NC * T], F32)
    xln = ctx.sb("xln", [128, NC * T], BF16)
    bufs = {"xres": xres, "xln": xln, "consts": consts, "prm": prm, "io": io}

    sx = ctx.sem("x_dl")
    sy = ctx.sem("y_st")
    for s in range(n_seq):
        for c in range(NC):
            sx.inc(nc.sync.dma_start(xres[:, c * T:(c + 1) * T], io["x"][s, c * 128:(c + 1) * 128, :]), 16)
        for eng in (nc.vector, nc.scalar, nc.tensor, nc.gpsimd, nc.sync):
            sx.wait(eng)
        for pi, ph in enumerate(phases):
            tag = f"_s{s}p{pi}"
            emit_ln(ctx, bufs, tag)
            if ph == "l0ffn":
                emit_l0ffn(ctx, bufs, sc, tag)
            elif ph == "l0attn":
                emit_l0attn(ctx, bufs, sc, tag)
            elif ph == "l1ffn":
                emit_l1ffn(ctx, bufs, sc, tag)
            elif ph == "l1attn":
                emit_l1attn(ctx, bufs, sc, tag)
            else:
                raise ValueError(ph)
        for c in range(NC):
            sy.inc(nc.sync.dma_start(io["y"][s, c * 128:(c + 1) * 128, :], xres[:, c * T:(c + 1) * T]), 16)
        if s < n_seq - 1:
            sy.wait(nc.sync)
    sy.wait(nc.sync)
    stack.close()
    return nc


ALL_PHASES = ("l0ffn", "l0attn", "l0ffn", "l1ffn", "l1attn", "l1ffn")


def _run(phases, inputs, x_override=None, cores=8):
    x = inputs["L0_attn_x"] if x_override is None else x_override
    xt = np.ascontiguousarray(x.transpose(0, 2, 1))
    nc = build_program(phases)
    in_maps = [{"x": np.ascontiguousarray(xt[ci * NSEQ:(ci + 1) * NSEQ])} for ci in range(cores)]
    for m in in_maps:
        for n in _needed_inputs(phases):
            m[n] = np.ascontiguousarray(inputs[n])
    res = run_bass_kernel_spmd(nc, in_maps, core_ids=list(range(cores)))
    out = np.empty((cores * NSEQ, T, D), np.float32)
    for ci in range(cores):
        y = res.results[ci]["y"]
        out[ci * NSEQ:(ci + 1) * NSEQ] = y.transpose(0, 2, 1)
    return out


def kernel(**inputs):
    return _run(ALL_PHASES, inputs)
